# version 01
# baseline (speedup 1.0000x reference)
from contextlib import ExitStack

import numpy as np
import ml_dtypes

import concourse.bass as bass
import concourse.mybir as mybir
from concourse.bass_types import AP
from concourse.bass_utils import run_bass_kernel_spmd

F32 = mybir.dt.float32
BF16 = mybir.dt.bfloat16
AF = mybir.ActivationFunctionType
OP = mybir.AluOpType
AX = mybir.AxisListType.X

N_CORES = 8

FULL = dict(S=2048, D=1024, NH=16, E=32, H=512, Hs=1024, B_loc=2, MAXP=2048,
            JB=512, IQ=512, EXPB=4.0)


def build(p):
    S, D, E, H, Hs = p["S"], p["D"], p["E"], p["H"], p["Hs"]
    NH, B_loc, MAXP = p["NH"], p["B_loc"], p["MAXP"]
    JB, IQ, EXPB = p["JB"], p["IQ"], p["EXPB"]
    T = B_loc * S
    DC, HC, HsC = D // 128, H // 128, Hs // 128
    NT, ST = T // 128, S // 128
    R, C0 = 2 * MAXP - 1, MAXP - 1
    NJB, JT, NIQ = S // JB, S // 128, S // IQ
    IQC = IQ // 128
    TB = 512 if T >= 512 else T
    NTB = T // TB
    TBC = TB // 128
    OB = min(512, D)
    SBW = min(512, S)
    NB2 = D // OB
    SB2 = S // SBW
    WQ = JB + 127
    WK = IQ + 127

    nc = bass.Bass("TRN2", target_bir_lowering=False, detect_race_conditions=False)
    es = ExitStack()

    def sbuf(name, shape, dt):
        return es.enter_context(nc.sbuf_tensor(name, shape, dt))

    all_sems = []

    class Sem:
        __slots__ = ("h", "n")

        def __init__(self, name):
            self.h = es.enter_context(nc.semaphore(name))
            self.n = 0
            all_sems.append(self)

        def inc(self, k=1):
            self.n += k
            return (self.h, k)

    dt_ = nc.dram_tensor
    x_in = dt_("x_in", [T, D], F32, kind="ExternalInput").ap()
    x_out = dt_("x_out", [T, D], F32, kind="ExternalOutput").ap()
    WqT_in = dt_("WqT", [D, D], F32, kind="ExternalInput").ap()
    WkT_in = dt_("WkT", [D, D], F32, kind="ExternalInput").ap()
    WvT_in = dt_("WvT", [D, D], F32, kind="ExternalInput").ap()
    bqT_in = dt_("bqT", [128, DC], F32, kind="ExternalInput").ap()
    bkT_in = dt_("bkT", [128, DC], F32, kind="ExternalInput").ap()
    bv_in = dt_("bv_rep", [128, D], F32, kind="ExternalInput").ap()
    deT_in = dt_("deT_d", [128, R], F32, kind="ExternalInput").ap()
    derT_in = dt_("derT_d", [128, R], F32, kind="ExternalInput").ap()
    Wg_in = dt_("Wg_in", [D, E], F32, kind="ExternalInput").ap()
    W1_in = dt_("W1_in", [E, D, H], F32, kind="ExternalInput").ap()
    W2_in = dt_("W2_in", [E, H, D], F32, kind="ExternalInput").ap()
    W3_in = dt_("W3_in", [E, D, H], F32, kind="ExternalInput").ap()
    Ws1_in = dt_("Ws1_in", [D, Hs], F32, kind="ExternalInput").ap()
    Ws2_in = dt_("Ws2_in", [Hs, D], F32, kind="ExternalInput").ap()
    Ws3_in = dt_("Ws3_in", [D, Hs], F32, kind="ExternalInput").ap()
    ident_in = dt_("ident_in", [128, 128], BF16, kind="ExternalInput").ap()
    y_dram = dt_("y_dram", [T, D], BF16, kind="Internal").ap()
    ctx_dram = dt_("ctx_dram", [T, D], BF16, kind="Internal").ap()
    DBG = p.get("DBG", False)
    if DBG:
        dbg_y = dt_("dbg_y", [128, NT * D], BF16, kind="ExternalOutput").ap()
        dbg_mw = dt_("dbg_mw", [128, NT * E], F32, kind="ExternalOutput").ap()
        dbg_ctx = dt_("dbg_ctx", [128, NT * D], BF16, kind="ExternalOutput").ap()
        dbg_xln = dt_("dbg_xln", [128, (D // 128) * T], BF16, kind="ExternalOutput").ap()

    consts = sbuf("consts", [128, 8], F32)
    nc.const_aps.register(F32, 1.0, consts[:, 0:1])
    nc.const_aps.register(F32, 0.5, consts[:, 1:2])
    nc.const_aps.register(F32, 1e-5, consts[:, 2:3])
    nc.const_aps.register(F32, float(-EXPB), consts[:, 3:4])
    nc.const_aps.register(F32, 0.0, consts[:, 4:5])
    ident = sbuf("ident", [128, 128], BF16)
    deT = sbuf("deT", [128, R], BF16)
    derT = sbuf("derT", [128, R], BF16)
    wg_sb = sbuf("wg_sb", [128, DC * E], BF16)
    bq_sb = sbuf("bq_sb", [128, DC], F32)
    bk_sb = sbuf("bk_sb", [128, DC], F32)

    PS = es.enter_context(nc.psum_tensor("PS", [128, 4096], F32))

    def bank(i, w=512, off=0):
        return PS[:, i * 512 + off:i * 512 + off + w]

    s_boot, s_cboot = Sem("s_boot"), Sem("s_cboot")
    s_x, s_w = Sem("s_x"), Sem("s_w")
    s_fold, s_lnA, s_scA, s_lnN = Sem("s_fold"), Sem("s_lnA"), Sem("s_scA"), Sem("s_lnN")
    s_mmT, s_evT, s_xo = Sem("s_mmT"), Sem("s_evT"), Sem("s_xo")
    s_mmG, s_scG, s_evG = Sem("s_mmG"), Sem("s_scG"), Sem("s_evG")
    s_mmH, s_scH, s_evH = Sem("s_mmH"), Sem("s_scH"), Sem("s_evH")
    s_mmO, s_evO, s_wf = Sem("s_mmO"), Sem("s_evO"), Sem("s_wf")
    s_mmP, s_evP = Sem("s_mmP"), Sem("s_evP")
    s_mmK, s_evK, s_shK = Sem("s_mmK"), Sem("s_evK"), Sem("s_shK")
    s_mmS, s_evS, s_shS, s_scS = Sem("s_mmS"), Sem("s_evS"), Sem("s_shS"), Sem("s_scS")
    s_evZ, s_mmX, s_evX = Sem("s_evZ"), Sem("s_mmX"), Sem("s_evX")

    with nc.Block() as blk:

        def barrier():
            snap = [(s.h, s.n) for s in all_sems if s.n > 0]

            def w(eng):
                for h, n in snap:
                    eng.wait_ge(h, n)
            blk.sync(w)
            blk.vector(w)
            blk.scalar(w)
            blk.tensor(w)
            blk.gpsimd(w)

        @blk.vector
        def _(v):
            v.memset(consts[:, 0:1], 1.0)
            v.memset(consts[:, 1:2], 0.5)
            v.memset(consts[:, 2:3], 1e-5)
            v.memset(consts[:, 3:4], float(-EXPB))
            v.memset(consts[:, 4:5], 0.0).then_inc(*s_cboot.inc())

        @blk.sync
        def _(sy):
            sy.dma_start(out=ident[:], in_=ident_in).then_inc(*s_boot.inc(16))
            sy.dma_start(out=bq_sb[:], in_=bqT_in).then_inc(*s_boot.inc(16))
            sy.dma_start(out=bk_sb[:], in_=bkT_in).then_inc(*s_boot.inc(16))

        @blk.gpsimd
        def _(g):
            g.dma_start(out=deT[:], in_=deT_in).then_inc(*s_w.inc(16))
            g.dma_start(out=derT[:], in_=derT_in).then_inc(*s_w.inc(16))
            g.dma_start(out=wg_sb[:].rearrange("p (c e) -> p c e", c=DC),
                        in_=Wg_in.rearrange("(c p) e -> p c e", p=128)
                        ).then_inc(*s_w.inc(16))

        def emit_ln_pass(tag, src, row0, ntiles, tile0, fold_dram, writeback,
                         xlnT, span, ln=True, alloc=None):
            alloc = alloc or sbuf
            xt = alloc(tag + "_xt", [128, 2 * D], F32)
            xw = alloc(tag + "_xw", [128, 2 * D], F32)
            stat = alloc(tag + "_stat", [128, 8], F32)
            sq = alloc(tag + "_sq", [128, D], BF16)
            xln = alloc(tag + "_xln", [128, 2 * D], BF16) if ln else None
            have_fold = fold_dram is not None
            fbuf = alloc(tag + "_fb", [128, 2 * D], BF16) if have_fold else None
            b_x, b_fold, b_lnA, b_scA = s_x.n, s_fold.n, s_lnA.n, s_scA.n
            b_lnN, b_mmT, b_evT, b_xo = s_lnN.n, s_mmT.n, s_evT.n, s_xo.n
            tb = max(1, (DC * 128) // 512)
            per_t = 32 if have_fold else 16
            assert ln or (have_fold and writeback)

            def emit_evac(v, u):
                v.wait_ge(s_mmT.h, b_mmT + u + 1)
                g0 = (u % 2) * tb
                out_ap = AP(tensor=xlnT, offset=u * 128,
                            ap=[[xlnT.shape[1], 128], [span, DC], [1, 128]])
                in_ap = AP(tensor=PS, offset=g0 * 512,
                           ap=[[4096, 128], [128, DC], [1, 128]])
                v.tensor_copy(out=out_ap, in_=in_ap).then_inc(*s_evT.inc())

            @blk.sync
            def _(sy):
                for t in range(ntiles):
                    if t >= 2:
                        if ln:
                            sy.wait_ge(s_lnN.h, b_lnN + (t - 1))
                        else:
                            sy.wait_ge(s_fold.h, b_fold + (t - 1))
                    sy.dma_start(out=xt[:, (t % 2) * D:(t % 2 + 1) * D],
                                 in_=src[row0 + t * 128:row0 + (t + 1) * 128, :]
                                 ).then_inc(*s_x.inc(16))
                    if have_fold:
                        sy.dma_start(out=fbuf[:, (t % 2) * D:(t % 2 + 1) * D],
                                     in_=fold_dram[row0 + t * 128:row0 + (t + 1) * 128, :]
                                     ).then_inc(*s_x.inc(16))
                    if writeback:
                        sy.wait_ge(s_fold.h, b_fold + t + 1)
                        sy.dma_start(out=x_out[row0 + t * 128:row0 + (t + 1) * 128, :],
                                     in_=xw[:, (t % 2) * D:(t % 2 + 1) * D]
                                     ).then_inc(*s_xo.inc(16))

            @blk.vector
            def _(v):
                v.wait_ge(s_cboot.h, s_cboot.n)
                for t in range(ntiles):
                    sl = slice((t % 2) * D, (t % 2 + 1) * D)
                    v.wait_ge(s_x.h, b_x + per_t * (t + 1))
                    cur = xt[:, sl]
                    if have_fold:
                        if t >= 2:
                            if ln:
                                v.wait_ge(s_scA.h, b_scA + 2 * (t - 2) + 1)
                            if writeback:
                                v.wait_ge(s_xo.h, b_xo + 16 * (t - 1))
                        v.tensor_add(xw[:, sl], cur, fbuf[:, sl]
                                     ).then_inc(*s_fold.inc())
                        cur = xw[:, sl]
                    if not ln:
                        continue
                    v.reduce_sum(stat[:, 0:1], cur, axis=AX)
                    v.drain()
                    v.tensor_scalar(stat[:, 1:2], stat[:, 0:1], -1.0 / D, None, OP.mult)
                    v.wait_ge(s_scA.h, b_scA + 2 * t + 1)
                    v.tensor_scalar(stat[:, 2:3], stat[:, 4:5], 1.0 / D, None, OP.mult)
                    v.drain()
                    v.tensor_mul(stat[:, 3:4], stat[:, 1:2], stat[:, 1:2])
                    v.drain()
                    v.tensor_sub(stat[:, 2:3], stat[:, 2:3], stat[:, 3:4])
                    v.drain()
                    v.tensor_scalar(stat[:, 2:3], stat[:, 2:3], 1e-5, None, OP.add)
                    v.drain().then_inc(*s_lnA.inc())
                    v.wait_ge(s_scA.h, b_scA + 2 * t + 2)
                    v.reciprocal(stat[:, 6:7], stat[:, 5:6])
                    v.drain()
                    if t >= 2:
                        v.wait_ge(s_mmT.h, b_mmT + (t - 1))
                    v.tensor_scalar(xln[:, sl], cur, stat[:, 1:2], stat[:, 6:7],
                                    OP.add, OP.mult).then_inc(*s_lnN.inc())
                    if t >= 1:
                        emit_evac(v, t - 1)
                if ln:
                    emit_evac(v, ntiles - 1)

            if not ln:
                return

            @blk.scalar
            def _(sc):
                sc.wait_ge(s_cboot.h, s_cboot.n)
                for t in range(ntiles):
                    sl = slice((t % 2) * D, (t % 2 + 1) * D)
                    src_t = xw if have_fold else xt
                    if have_fold:
                        sc.wait_ge(s_fold.h, b_fold + t + 1)
                    else:
                        sc.wait_ge(s_x.h, b_x + per_t * (t + 1))
                    sc.activation(out=sq[:], in_=src_t[:, sl], func=AF.Square,
                                  accum_out=stat[:, 4:5])
                    sc.drain().then_inc(*s_scA.inc())
                    sc.wait_ge(s_lnA.h, b_lnA + t + 1)
                    sc.activation(out=stat[:, 5:6], in_=stat[:, 2:3], func=AF.Sqrt)
                    sc.drain().then_inc(*s_scA.inc())

            @blk.tensor
            def _(te):
                te.wait_ge(s_boot.h, 16)
                for t in range(ntiles):
                    te.wait_ge(s_lnN.h, b_lnN + t + 1)
                    if t >= 2:
                        te.wait_ge(s_evT.h, b_evT + (t - 1))
                    g0 = (t % 2) * tb
                    sl0 = (t % 2) * D
                    ins = None
                    for c in range(DC):
                        ins = te.matmul(
                            out=bank(g0 + (c * 128) // 512, 128, (c * 128) % 512),
                            lhsT=xln[:, sl0 + c * 128:sl0 + (c + 1) * 128],
                            rhs=ident[:],
                            start=((c * 128) % 512 == 0),
                            stop=((c * 128) % 512 == 512 - 128 or c == DC - 1))
                    ins.then_inc(*s_mmT.inc())


        def emit_gating(tag, xlnT, span, ntiles, tile0, mw_sb, alloc=None):
            alloc = alloc or sbuf
            pg = alloc(tag + "_pg", [128, 4 * E], F32)
            gz = alloc(tag + "_gz", [128, 16], F32)
            t8 = alloc(tag + "_t8", [128, 8], F32)
            b_mmG, b_scG, b_evG = s_mmG.n, s_scG.n, s_evG.n
            GE = 4
            ngrp = (ntiles + GE - 1) // GE

            @blk.tensor
            def _(te):
                te.wait_ge(s_w.h, 48)
                te.wait_ge(s_evT.h, s_evT.n)
                for g in range(ngrp):
                    tl = list(range(g * GE, min((g + 1) * GE, ntiles)))
                    if g >= 2:
                        te.wait_ge(s_scG.h, b_scG + (g - 1))
                    bk6 = 6 + (g % 2)
                    ins = None
                    for t in tl:
                        for c in range(DC):
                            ins = te.matmul(
                                out=bank(bk6, E, (t % GE) * E),
                                lhsT=xlnT[:, c * span + t * 128:c * span + (t + 1) * 128],
                                rhs=wg_sb[:, c * E:(c + 1) * E],
                                start=(t == tl[0] and c == 0),
                                stop=(t == tl[-1] and c == DC - 1))
                    ins.then_inc(*s_mmG.inc())

            @blk.vector
            def _(v):
                for g in range(ngrp):
                    tl = list(range(g * GE, min((g + 1) * GE, ntiles)))
                    bk6 = 6 + (g % 2)
                    v.wait_ge(s_mmG.h, b_mmG + g + 1)
                    for t in tl:
                        j = t % GE
                        v.reduce_max(gz[:, j:j + 1], bank(bk6, E, j * E),
                                     axis=AX, negate=True)
                    v.drain().then_inc(*s_evG.inc())
                    v.wait_ge(s_scG.h, b_scG + g + 1)
                    for t in tl:
                        j = t % GE
                        v.reciprocal(gz[:, 4 + j:5 + j], gz[:, 4 + j:5 + j])
                    v.drain()
                    for t in tl:
                        j = t % GE
                        gt = tile0 + t
                        msl = mw_sb[:, gt * E:(gt + 1) * E]
                        v.tensor_scalar(pg[:, j * E:(j + 1) * E], pg[:, j * E:(j + 1) * E],
                                        gz[:, 4 + j:5 + j], None, OP.mult)
                        v.drain()
                        v.max(out=t8[:], in_=pg[:, j * E:(j + 1) * E])
                        v.drain()
                        v.tensor_scalar(msl, pg[:, j * E:(j + 1) * E], t8[:, 3:4],
                                        None, OP.is_ge)
                        v.drain()
                        v.tensor_mul(msl, msl, pg[:, j * E:(j + 1) * E])
                        v.drain()
                        v.tensor_scalar(msl, msl, 0.5, None, OP.mult)
                    v.drain().then_inc(*s_evG.inc())

            @blk.scalar
            def _(sc):
                for g in range(ngrp):
                    tl = list(range(g * GE, min((g + 1) * GE, ntiles)))
                    bk6 = 6 + (g % 2)
                    sc.wait_ge(s_evG.h, b_evG + 2 * g + 1)
                    for t in tl:
                        j = t % GE
                        sc.activation(out=pg[:, j * E:(j + 1) * E],
                                      in_=bank(bk6, E, j * E), func=AF.Exp,
                                      bias=gz[:, j:j + 1], scale=1.0,
                                      accum_out=gz[:, 4 + j:5 + j])
                    sc.drain().then_inc(*s_scG.inc())

        def emit_swiglu(tag, xlnT, span, w1sb, w3sb, w2sb, hxc, y_sb, hg, hgl,
                        mw_sb=None, expert=None, wait_w=None, free_sem=None):
            b_mmH, b_scH, b_evH = s_mmH.n, s_scH.n, s_evH.n
            b_mmO, b_evO = s_mmO.n, s_evO.n
            HW = hxc * 128

            @blk.tensor
            def _(te):
                if wait_w is not None:
                    te.wait_ge(s_w.h, wait_w)
                te.wait_ge(s_evT.h, s_evT.n)
                for b in range(NTB):
                    col = b * TB
                    for hc in range(hxc):
                        idx = b * hxc + hc
                        if idx >= 2:
                            te.wait_ge(s_evH.h, b_evH + idx - 1)
                        bh1, bh3 = idx % 2, 2 + idx % 2
                        for c in range(DC):
                            te.matmul(out=bank(bh1, TB),
                                      lhsT=w1sb[:, c * HW + hc * 128:c * HW + (hc + 1) * 128],
                                      rhs=xlnT[:, c * span + col:c * span + col + TB],
                                      start=(c == 0), stop=(c == DC - 1))
                        ins = None
                        for c in range(DC):
                            ins = te.matmul(out=bank(bh3, TB),
                                            lhsT=w3sb[:, c * HW + hc * 128:c * HW + (hc + 1) * 128],
                                            rhs=xlnT[:, c * span + col:c * span + col + TB],
                                            start=(c == 0), stop=(c == DC - 1))
                        ins.then_inc(*s_mmH.inc())
                    te.wait_ge(s_evH.h, b_evH + (b + 1) * hxc)
                    for tc in range(TBC):
                        for n in range(NB2):
                            oidx = b * TBC * NB2 + tc * NB2 + n
                            bo = 4 + oidx % 2
                            if oidx >= 2:
                                te.wait_ge(s_evO.h, b_evO + oidx - 1)
                            ins = None
                            for k in range(hxc):
                                ins = te.matmul(
                                    out=bank(bo, OB),
                                    lhsT=hgl[:, k * TB + tc * 128:k * TB + (tc + 1) * 128],
                                    rhs=w2sb[:, k * D + n * OB:k * D + (n + 1) * OB],
                                    start=(k == 0), stop=(k == hxc - 1))
                            ins.then_inc(*s_mmO.inc())

            @blk.scalar
            def _(sc):
                for b in range(NTB):
                    for hc in range(hxc):
                        idx = b * hxc + hc
                        sc.wait_ge(s_mmH.h, b_mmH + idx + 1)
                        sc.activation(out=hg[:, (idx % 2) * TB:(idx % 2) * TB + TB],
                                      in_=bank(idx % 2, TB), func=AF.Silu
                                      ).then_inc(*s_scH.inc())

            @blk.vector
            def _(v):
                for b in range(NTB):
                    for hc in range(hxc):
                        idx = b * hxc + hc
                        v.wait_ge(s_scH.h, b_scH + idx + 1)
                        v.tensor_mul(hgl[:, hc * TB:hc * TB + TB],
                                     bank(2 + idx % 2, TB),
                                     hg[:, (idx % 2) * TB:(idx % 2) * TB + TB]
                                     ).then_inc(*s_evH.inc())
                    for tc in range(TBC):
                        for n in range(NB2):
                            oidx = b * TBC * NB2 + tc * NB2 + n
                            bo = 4 + oidx % 2
                            v.wait_ge(s_mmO.h, b_mmO + oidx + 1)
                            gt = b * TBC + tc
                            ysl = y_sb[:, gt * D + n * OB:gt * D + (n + 1) * OB]
                            if expert is None:
                                ins = v.tensor_scalar(ysl, bank(bo, OB), 0.5, None, OP.mult)
                            else:
                                ins = v.scalar_tensor_tensor(
                                    out=ysl, in0=bank(bo, OB),
                                    scalar=mw_sb[:, gt * E + expert:gt * E + expert + 1],
                                    in1=ysl, op0=OP.mult, op1=OP.add)
                            ins.then_inc(*s_evO.inc())
                if free_sem is not None:
                    v.engine_nop().then_inc(*free_sem.inc())

        def emit_ffn(tag, src, fold_dram, writeback):
            with ExitStack() as ph:
                psb = lambda n, s2, d: ph.enter_context(nc.sbuf_tensor(tag + n, s2, d))
                y_sb = psb("_y", [128, NT * D], BF16)
                mw_sb = psb("_mw", [128, NT * E], F32)
                xlnT = psb("_xlnT", [128, DC * T], BF16)
                hg = psb("_hg", [128, 2 * TB], BF16)
                hgl = psb("_hgl", [128, max(HsC, HC) * TB], BF16)
                with ExitStack() as lnst:
                    lsb = lambda n, s2, d: lnst.enter_context(
                        nc.sbuf_tensor(tag + n, s2, d))
                    emit_ln_pass(tag + "_ln", src, 0, NT, 0, fold_dram, writeback,
                                 xlnT, T, alloc=lsb)
                    emit_gating(tag + "_g", xlnT, T, NT, 0, mw_sb, alloc=lsb)
                    barrier()
                with ExitStack() as wsst:
                    wsb = lambda n, s2, d: wsst.enter_context(
                        nc.sbuf_tensor(tag + n, s2, d))
                    ws1 = wsb("_ws1", [128, DC * Hs], BF16)
                    ws3 = wsb("_ws3", [128, DC * Hs], BF16)
                    ws2 = wsb("_ws2", [128, HsC * D], BF16)
                    b_w = s_w.n

                    @blk.gpsimd
                    def _(g):
                        g.dma_start(out=ws1[:].rearrange("p (c h) -> p c h", c=DC),
                                    in_=Ws1_in.rearrange("(c p) h -> p c h", p=128)
                                    ).then_inc(*s_w.inc(16))
                        g.dma_start(out=ws3[:].rearrange("p (c h) -> p c h", c=DC),
                                    in_=Ws3_in.rearrange("(c p) h -> p c h", p=128)
                                    ).then_inc(*s_w.inc(16))
                        g.dma_start(out=ws2[:].rearrange("p (c h) -> p c h", c=HsC),
                                    in_=Ws2_in.rearrange("(c p) h -> p c h", p=128)
                                    ).then_inc(*s_w.inc(16))

                    emit_swiglu(tag + "_sh", xlnT, T, ws1, ws3, ws2, HsC, y_sb,
                                hg, hgl, mw_sb, expert=None, wait_w=b_w + 48)
                shared_mmO = s_mmO.n
                w1e = psb("_w1e", [128, 2 * DC * H], BF16)
                w3e = psb("_w3e", [128, 2 * DC * H], BF16)
                w2e = psb("_w2e", [128, 2 * HC * D], BF16)
                EH = DC * H
                b_w2 = s_w.n
                b_wf = s_wf.n

                @blk.gpsimd
                def _(g):
                    g.wait_ge(s_mmO.h, shared_mmO)
                    for e in range(E):
                        if e >= 2:
                            g.wait_ge(s_wf.h, b_wf + e - 1)
                        sl1 = slice((e % 2) * EH, (e % 2 + 1) * EH)
                        sl2 = slice((e % 2) * HC * D, (e % 2 + 1) * HC * D)
                        g.dma_start(out=w1e[:, sl1].rearrange("p (c h) -> p c h", c=DC),
                                    in_=W1_in[e].rearrange("(c p) h -> p c h", p=128)
                                    ).then_inc(*s_w.inc(16))
                        g.dma_start(out=w3e[:, sl1].rearrange("p (c h) -> p c h", c=DC),
                                    in_=W3_in[e].rearrange("(c p) h -> p c h", p=128)
                                    ).then_inc(*s_w.inc(16))
                        g.dma_start(out=w2e[:, sl2].rearrange("p (c h) -> p c h", c=HC),
                                    in_=W2_in[e].rearrange("(c p) h -> p c h", p=128)
                                    ).then_inc(*s_w.inc(16))

                for e in range(E):
                    sl1 = slice((e % 2) * EH, (e % 2 + 1) * EH)
                    sl2 = slice((e % 2) * HC * D, (e % 2 + 1) * HC * D)
                    emit_swiglu(tag + "_e", xlnT, T, w1e[:, sl1], w3e[:, sl1],
                                w2e[:, sl2], HC, y_sb, hg, hgl, mw_sb, expert=e,
                                wait_w=b_w2 + 48 * (e + 1), free_sem=s_wf)
                if DBG and tag == "f1":
                    @blk.sync
                    def _(sy):
                        sy.wait_ge(s_evT.h, s_evT.n)
                        sy.dma_start(out=dbg_xln, in_=xlnT[:]).then_inc(*s_xo.inc(16))
                @blk.sync
                def _(sy):
                    sy.wait_ge(s_evO.h, s_evO.n)
                    sy.dma_start(out=y_dram.rearrange("(t p) d -> p t d", p=128),
                                 in_=y_sb[:].rearrange("p (t d) -> p t d", d=D)
                                 ).then_inc(*s_xo.inc(16))
                barrier()

        def emit_attn():
            for sq_ in range(B_loc):
                with ExitStack() as ph:
                    psb = lambda n, sh, d: ph.enter_context(
                        nc.sbuf_tensor(f"at{sq_}{n}", sh, d))
                    ctx_seq = psb("_ctx", [128, ST * D], BF16)
                    bv_sb = psb("_bv", [128, D], F32)

                    @blk.sync
                    def _(sy):
                        sy.dma_start(out=bv_sb[:], in_=bv_in).then_inc(*s_boot.inc(16))
                    qT = psb("_qT", [128, DC * S], BF16)
                    kT = psb("_kT", [128, DC * S], BF16)
                    v_sb = psb("_v", [128, ST * D], BF16)
                    st1 = ph.enter_context(ExitStack())
                    s1b = lambda n, sh, d: st1.enter_context(
                        nc.sbuf_tensor(f"at{sq_}{n}", sh, d))
                    xlnT = s1b("_xlnT", [128, DC * S], BF16)
                    with ExitStack() as st2:
                        s2b = lambda n, sh, d: st2.enter_context(
                            nc.sbuf_tensor(f"at{sq_}{n}", sh, d))
                        emit_ln_pass(f"at{sq_}_ln", x_in, sq_ * S, ST, sq_ * ST,
                                     y_dram, True, xlnT, S, alloc=s2b)
                        barrier()
                    wpj = s1b("_wpj", [128, DC * D], BF16)
                    b_w = s_w.n
                    b_mmP, b_evP = s_mmP.n, s_evP.n
                    per = DC * SB2

                    @blk.gpsimd
                    def _(g):
                        for j, wt in enumerate((WqT_in, WkT_in, WvT_in)):
                            if j >= 1:
                                g.wait_ge(s_evP.h, b_evP + j * per)
                            g.dma_start(
                                out=wpj[:].rearrange("p (c h) -> p c h", c=DC),
                                in_=wt.rearrange("(c p) h -> p c h", p=128)
                                ).then_inc(*s_w.inc(16))

                    @blk.tensor
                    def _(te):
                        for j in range(3):
                            te.wait_ge(s_w.h, b_w + 16 * (j + 1))
                            wsl = 0
                            if j < 2:
                                for mc in range(DC):
                                    for n in range(SB2):
                                        idx = j * per + mc * SB2 + n
                                        bo = idx % 4
                                        if idx >= 4:
                                            te.wait_ge(s_evP.h, b_evP + idx - 3)
                                        ins = None
                                        for c in range(DC):
                                            ins = te.matmul(
                                                out=bank(bo, SBW),
                                                lhsT=wpj[:, wsl + c * D + mc * 128:
                                                         wsl + c * D + (mc + 1) * 128],
                                                rhs=xlnT[:, c * S + n * SBW:
                                                         c * S + (n + 1) * SBW],
                                                start=(c == 0), stop=(c == DC - 1))
                                        ins.then_inc(*s_mmP.inc())
                            else:
                                for tc in range(ST):
                                    for n in range(NB2):
                                        idx = 2 * per + tc * NB2 + n
                                        bo = idx % 4
                                        if idx >= 4:
                                            te.wait_ge(s_evP.h, b_evP + idx - 3)
                                        ins = None
                                        for c in range(DC):
                                            ins = te.matmul(
                                                out=bank(bo, OB),
                                                lhsT=xlnT[:, c * S + tc * 128:
                                                          c * S + (tc + 1) * 128],
                                                rhs=wpj[:, wsl + c * D + n * OB:
                                                         wsl + c * D + (n + 1) * OB],
                                                start=(c == 0), stop=(c == DC - 1))
                                        ins.then_inc(*s_mmP.inc())

                    @blk.vector
                    def _(v):
                        for j in range(3):
                            if j < 2:
                                bias = bq_sb if j == 0 else bk_sb
                                for mc in range(DC):
                                    for n in range(SB2):
                                        idx = j * per + mc * SB2 + n
                                        bo = idx % 4
                                        v.wait_ge(s_mmP.h, b_mmP + idx + 1)
                                        dst = (qT if j == 0 else kT)[
                                            :, mc * S + n * SBW:mc * S + (n + 1) * SBW]
                                        if j == 0:
                                            ins = v.tensor_scalar(
                                                dst, bank(bo, SBW), bias[:, mc:mc + 1],
                                                0.125, OP.add, OP.mult)
                                        else:
                                            ins = v.tensor_scalar(
                                                dst, bank(bo, SBW), bias[:, mc:mc + 1],
                                                None, OP.add)
                                        ins.then_inc(*s_evP.inc())
                            else:
                                for tc in range(ST):
                                    for n in range(NB2):
                                        idx = 2 * per + tc * NB2 + n
                                        bo = idx % 4
                                        v.wait_ge(s_mmP.h, b_mmP + idx + 1)
                                        v.tensor_add(
                                            v_sb[:, tc * D + n * OB:tc * D + (n + 1) * OB],
                                            bank(bo, OB), bv_sb[:, n * OB:(n + 1) * OB]
                                            ).then_inc(*s_evP.inc())
                    barrier()
                    st1.close()

                    relkT = psb("_relkT", [128, JT * IQ], BF16)
                    kd_sb = psb("_kd", [128, 2 * WK], BF16)
                    qdb = psb("_qdb", [128, 2 * WQ], BF16)
                    rq_sb = psb("_rq", [128, 2 * JB], BF16)
                    Pb = psb("_Pb", [128, S], BF16)
                    PT = psb("_PT", [128, 512], BF16)
                    zp = psb("_zp", [128, 2 * NJB + 4], F32)

                    for h in range(NH):
                        hb = 64 * (h % 2)
                        hcol = (h // 2) * S
                        for iq in range(NIQ):
                            b_mmK, b_evK, b_shK = s_mmK.n, s_evK.n, s_shK.n

                            @blk.tensor
                            def _(te):
                                te.wait_ge(s_w.h, 32)
                                te.wait_ge(s_shS.h, s_shS.n)
                                te.wait_ge(s_scS.h, s_scS.n)
                                for jt in range(JT):
                                    if jt >= 2:
                                        te.wait_ge(s_evK.h, b_evK + jt - 1)
                                    s0 = C0 + iq * IQ - jt * 128 - 127
                                    g0 = (jt % 2) * 2
                                    w1k = min(512, WK)
                                    ins = te.matmul(out=bank(g0, w1k),
                                              lhsT=kT[hb:hb + 64,
                                                      hcol + jt * 128:hcol + (jt + 1) * 128],
                                              rhs=deT[hb:hb + 64, s0:s0 + w1k],
                                              start=True, stop=True)
                                    if WK > 512:
                                        ins = te.matmul(out=bank(g0 + 1, WK - 512),
                                              lhsT=kT[hb:hb + 64,
                                                      hcol + jt * 128:hcol + (jt + 1) * 128],
                                              rhs=deT[hb:hb + 64, s0 + 512:s0 + WK],
                                              start=True, stop=True)
                                    ins.then_inc(*s_mmK.inc())

                            @blk.vector
                            def _(v):
                                for jt in range(JT):
                                    v.wait_ge(s_mmK.h, b_mmK + jt + 1)
                                    if jt >= 2:
                                        v.wait_ge(s_shK.h, b_shK + 16 * (jt - 1))
                                    g0 = (jt % 2) * 2
                                    v.tensor_scalar(
                                        kd_sb[:, (jt % 2) * WK:(jt % 2) * WK + WK],
                                        PS[:, g0 * 512:g0 * 512 + WK],
                                        0.125, None, OP.mult).then_inc(*s_evK.inc())

                            @blk.scalar
                            def _(sc):
                                for jt in range(JT):
                                    sc.wait_ge(s_evK.h, b_evK + jt + 1)
                                    diag = AP(tensor=kd_sb, offset=(jt % 2) * WK + 127,
                                              ap=[[2 * WK - 1, 128], [1, IQ]])
                                    sc.dma_start(out=relkT[:, jt * IQ:(jt + 1) * IQ],
                                                 in_=diag).then_inc(*s_shK.inc(16))

                            G4 = 4 if JT % 4 == 0 else JT
                            for itl in range(IQC):
                                it = iq * IQC + itl
                                b_mmS, b_evS, b_shS = s_mmS.n, s_evS.n, s_shS.n
                                b_scS, b_evZ = s_scS.n, s_evZ.n
                                b_mmX, b_evX = s_mmX.n, s_evX.n
                                git = b_evZ
                                zb = (git % 2) * NJB

                                @blk.tensor
                                def _(te):
                                    te.wait_ge(s_shK.h, b_shK + 16 * JT)
                                    for jb in range(NJB):
                                        g0 = (jb % 2) * 2
                                        if jb >= 2:
                                            te.wait_ge(s_evS.h, b_evS + jb - 1)
                                        s0q = C0 + jb * JB - it * 128 - 127
                                        w1q = min(512, WQ)
                                        ins = te.matmul(out=bank(g0, w1q),
                                                  lhsT=qT[hb:hb + 64,
                                                          hcol + it * 128:hcol + (it + 1) * 128],
                                                  rhs=derT[hb:hb + 64, s0q:s0q + w1q],
                                                  start=True, stop=True)
                                        if WQ > 512:
                                            ins = te.matmul(out=bank(g0 + 1, WQ - 512),
                                                  lhsT=qT[hb:hb + 64,
                                                          hcol + it * 128:hcol + (it + 1) * 128],
                                                  rhs=derT[hb:hb + 64, s0q + 512:s0q + WQ],
                                                  start=True, stop=True)
                                        ins.then_inc(*s_mmS.inc())
                                        bs = 4 + jb % 2
                                        if jb >= 2:
                                            te.wait_ge(s_scS.h, b_scS + jb - 1)
                                        te.wait_ge(s_shS.h, b_shS + 16 * (jb + 1))
                                        te.matmul(out=bank(bs, JB),
                                                  lhsT=qT[hb:hb + 64,
                                                          hcol + it * 128:hcol + (it + 1) * 128],
                                                  rhs=kT[hb:hb + 64,
                                                         hcol + jb * JB:hcol + (jb + 1) * JB],
                                                  start=True, stop=False)
                                        for j2 in range(JB // 128):
                                            jt = jb * (JB // 128) + j2
                                            te.matmul(out=bank(bs, 128, j2 * 128),
                                                      lhsT=relkT[:, jt * IQ + itl * 128:
                                                                 jt * IQ + (itl + 1) * 128],
                                                      rhs=ident[:],
                                                      start=False, stop=False)
                                        te.matmul(out=bank(bs, JB), lhsT=ident[:],
                                                  rhs=rq_sb[:, (jb % 2) * JB:(jb % 2 + 1) * JB],
                                                  start=False, stop=True
                                                  ).then_inc(*s_mmS.inc())

                                @blk.vector
                                def _(v):
                                    for jb in range(NJB):
                                        v.wait_ge(s_mmS.h, b_mmS + 2 * jb + 1)
                                        if jb >= 2:
                                            v.wait_ge(s_shS.h, b_shS + 16 * (jb - 1))
                                        g0 = (jb % 2) * 2
                                        v.tensor_copy(
                                            out=qdb[:, (jb % 2) * WQ:(jb % 2) * WQ + WQ],
                                            in_=PS[:, g0 * 512:g0 * 512 + WQ]
                                            ).then_inc(*s_evS.inc())
                                    v.wait_ge(s_scS.h, b_scS + NJB + 1)
                                    v.reduce_sum(zp[:, 2 * NJB + (git % 2):2 * NJB + (git % 2) + 1],
                                                 zp[:, zb:zb + NJB], axis=AX)
                                    v.drain()
                                    v.reciprocal(zp[:, 2 * NJB + 2 + (git % 2):2 * NJB + 3 + (git % 2)],
                                                 zp[:, 2 * NJB + (git % 2):2 * NJB + (git % 2) + 1])
                                    v.drain().then_inc(*s_evZ.inc())

                                @blk.scalar
                                def _(sc):
                                    if git >= 2:
                                        sc.wait_ge(s_evZ.h, b_evZ - 1)
                                    sc.wait_ge(s_mmX.h, b_mmX)
                                    for jb in range(NJB):
                                        sc.wait_ge(s_evS.h, b_evS + jb + 1)
                                        diag = AP(tensor=qdb, offset=(jb % 2) * WQ + 127,
                                                  ap=[[2 * WQ - 1, 128], [1, JB]])
                                        sc.dma_start(
                                            out=rq_sb[:, (jb % 2) * JB:(jb % 2 + 1) * JB],
                                            in_=diag).then_inc(*s_shS.inc(16))
                                        sc.wait_ge(s_mmS.h, b_mmS + 2 * (jb + 1))
                                        bs = 4 + jb % 2
                                        sc.activation(out=Pb[:, jb * JB:(jb + 1) * JB],
                                                      in_=bank(bs, JB), func=AF.Exp,
                                                      bias=float(-EXPB), scale=1.0,
                                                      accum_out=zp[:, zb + jb:zb + jb + 1]
                                                      ).then_inc(*s_scS.inc())
                                    sc.drain().then_inc(*s_scS.inc())

                                @blk.tensor
                                def _(te):
                                    te.wait_ge(s_scS.h, b_scS + NJB + 1)
                                    for grp in range(JT // G4):
                                        if grp >= 1:
                                            te.wait_ge(s_evX.h, b_evX + grp)
                                        ins = None
                                        for j2 in range(G4):
                                            jt = grp * G4 + j2
                                            ins = te.matmul(out=bank(6, 128, j2 * 128),
                                                            lhsT=Pb[:, jt * 128:(jt + 1) * 128],
                                                            rhs=ident[:],
                                                            start=(j2 == 0), stop=(j2 == G4 - 1))
                                        ins.then_inc(*s_mmX.inc())
                                        te.wait_ge(s_evX.h, b_evX + grp + 1)
                                        ins = None
                                        for j2 in range(G4):
                                            jt = grp * G4 + j2
                                            ins = te.matmul(
                                                out=bank(7, 64),
                                                lhsT=PT[:, j2 * 128:(j2 + 1) * 128],
                                                rhs=v_sb[:, jt * D + h * 64:jt * D + (h + 1) * 64],
                                                start=(jt == 0), stop=(jt == JT - 1))
                                        ins.then_inc(*s_mmX.inc())

                                @blk.vector
                                def _(v):
                                    for grp in range(JT // G4):
                                        v.wait_ge(s_mmX.h, b_mmX + 2 * grp + 1)
                                        v.tensor_copy(out=PT[:, 0:G4 * 128], in_=bank(6, G4 * 128)
                                                      ).then_inc(*s_evX.inc())
                                    v.wait_ge(s_mmX.h, b_mmX + 2 * (JT // G4))
                                    v.wait_ge(s_evZ.h, b_evZ + 1)
                                    v.tensor_scalar(
                                        ctx_seq[:, it * D + h * 64:it * D + (h + 1) * 64],
                                        bank(7, 64),
                                        zp[:, 2 * NJB + 2 + (git % 2):2 * NJB + 3 + (git % 2)],
                                        None, OP.mult).then_inc(*s_evX.inc())

                    @blk.sync
                    def _(sy):
                        sy.wait_ge(s_evX.h, s_evX.n)
                        sy.dma_start(
                            out=ctx_dram[sq_ * S:(sq_ + 1) * S]
                            .rearrange("(t p) d -> p t d", p=128),
                            in_=ctx_seq[:].rearrange("p (t d) -> p t d", d=D)
                            ).then_inc(*s_xo.inc(16))
                    barrier()

        LVL = p.get("LVL", 3)
        emit_ffn("f1", x_in, None, False)
        if DBG:
            @blk.sync
            def _(sy):
                sy.dma_start(out=dbg_y.rearrange("p (t d) -> p t d", d=D),
                             in_=y_dram.rearrange("(t p) d -> p t d", p=128)
                             ).then_inc(*s_xo.inc(16))

            barrier()
        if LVL >= 2:
            emit_attn()
        if DBG and LVL >= 2:
            @blk.sync
            def _(sy):
                sy.dma_start(out=dbg_ctx.rearrange("p (t d) -> p t d", d=D),
                             in_=ctx_dram.rearrange("(t p) d -> p t d", p=128)
                             ).then_inc(*s_xo.inc(16))
            barrier()
        if LVL >= 3:
            emit_ffn("f2", x_out, ctx_dram, True)
            barrier()
            emit_ln_pass("fin", x_out, 0, NT, 0, y_dram, True, None, 0, ln=False)

        @blk.sync
        def _(sy):
            sy.wait_ge(s_xo.h, s_xo.n)

    es.close()
    return nc


def _prep_inputs(inp, p, core):
    S, D = p["S"], p["D"]
    DC = D // 128
    B_loc = p["B_loc"]
    T = B_loc * S
    x = np.ascontiguousarray(
        inp["L0_attn_hidden_states"][core * B_loc:(core + 1) * B_loc]
    ).reshape(T, D)
    de = np.asarray(inp["L0_attn_dist_emb"])
    deT = np.ascontiguousarray(de.T)
    derT = np.ascontiguousarray(de[::-1].T)
    f32 = np.float32
    return {
        "x_in": np.asarray(x, f32),
        "WqT": np.ascontiguousarray(np.asarray(inp["L0_attn_Wq"], f32).T),
        "WkT": np.ascontiguousarray(np.asarray(inp["L0_attn_Wk"], f32).T),
        "WvT": np.ascontiguousarray(np.asarray(inp["L0_attn_Wv"], f32).T),
        "bqT": np.ascontiguousarray(np.asarray(inp["L0_attn_bq"], f32).reshape(DC, 128).T),
        "bkT": np.ascontiguousarray(np.asarray(inp["L0_attn_bk"], f32).reshape(DC, 128).T),
        "bv_rep": np.ascontiguousarray(
            np.broadcast_to(np.asarray(inp["L0_attn_bv"], f32), (128, D))),
        "deT_d": np.ascontiguousarray(np.concatenate([deT, deT], axis=0), f32),
        "derT_d": np.ascontiguousarray(np.concatenate([derT, derT], axis=0), f32),
        "Wg_in": np.asarray(inp["L0_ffn_Wg"], f32),
        "W1_in": np.asarray(inp["L0_ffn_W1"], f32),
        "W2_in": np.asarray(inp["L0_ffn_W2"], f32),
        "W3_in": np.asarray(inp["L0_ffn_W3"], f32),
        "Ws1_in": np.asarray(inp["L0_ffn_Ws1"], f32),
        "Ws2_in": np.asarray(inp["L0_ffn_Ws2"], f32),
        "Ws3_in": np.asarray(inp["L0_ffn_Ws3"], f32),
        "ident_in": np.eye(128, dtype=ml_dtypes.bfloat16),
    }


def kernel(**inputs):
    p = FULL
    nc = build(p)
    in_maps = [_prep_inputs(inputs, p, c) for c in range(N_CORES)]
    res = run_bass_kernel_spmd(nc, in_maps, core_ids=list(range(N_CORES)))
    out = np.concatenate(
        [r["x_out"].reshape(p["B_loc"], p["S"], p["D"]) for r in res.results], axis=0)
    return np.ascontiguousarray(out, dtype=np.float32)


# version 02
# speedup vs baseline: 1.2539x; 1.2539x over previous
from contextlib import ExitStack

import numpy as np
import ml_dtypes

import concourse.bass as bass
import concourse.mybir as mybir
from concourse.bass_types import AP
from concourse.bass_core_rust import IndirectOffsetOnAxis
from concourse.bass_utils import run_bass_kernel_spmd

F32 = mybir.dt.float32
BF16 = mybir.dt.bfloat16
U32 = mybir.dt.uint32
I32 = mybir.dt.int32
AF = mybir.ActivationFunctionType
OP = mybir.AluOpType
AX = mybir.AxisListType.X

N_CORES = 8

FULL = dict(S=2048, D=1024, NH=16, E=32, H=512, Hs=1024, B_loc=2, MAXP=2048,
            JB=512, IQ=512, EXPB=4.0, CAP=768)


def build(p):
    S, D, E, H, Hs = p["S"], p["D"], p["E"], p["H"], p["Hs"]
    NH, B_loc, MAXP = p["NH"], p["B_loc"], p["MAXP"]
    JB, IQ, EXPB = p["JB"], p["IQ"], p["EXPB"]
    T = B_loc * S
    CAP = p.get("CAP", T)
    GCH = CAP // 128
    PB = min(512, CAP)
    while CAP % PB:
        PB -= 128
    NPB = CAP // PB
    DC, HC, HsC = D // 128, H // 128, Hs // 128
    NT, ST = T // 128, S // 128
    R, C0 = 2 * MAXP - 1, MAXP - 1
    NJB, JT, NIQ = S // JB, S // 128, S // IQ
    IQC = IQ // 128
    TB = 512 if T >= 512 else T
    NTB = T // TB
    TBC = TB // 128
    OB = min(512, D)
    SBW = min(512, S)
    NB2 = D // OB
    SB2 = S // SBW
    WQ = JB + 127
    WK = IQ + 127

    nc = bass.Bass("TRN2", target_bir_lowering=False, detect_race_conditions=False)
    es = ExitStack()

    def sbuf(name, shape, dt):
        return es.enter_context(nc.sbuf_tensor(name, shape, dt))

    all_sems = []

    class Sem:
        __slots__ = ("h", "n")

        def __init__(self, name):
            self.h = es.enter_context(nc.semaphore(name))
            self.n = 0
            all_sems.append(self)

        def inc(self, k=1):
            self.n += k
            return (self.h, k)

    dt_ = nc.dram_tensor
    x_in = dt_("x_in", [T, D], F32, kind="ExternalInput").ap()
    x_out = dt_("x_out", [T, D], F32, kind="ExternalOutput").ap()
    WqT_in = dt_("WqT", [D, D], F32, kind="ExternalInput").ap()
    WkT_in = dt_("WkT", [D, D], F32, kind="ExternalInput").ap()
    WvT_in = dt_("WvT", [D, D], F32, kind="ExternalInput").ap()
    bqT_in = dt_("bqT", [128, DC], F32, kind="ExternalInput").ap()
    bkT_in = dt_("bkT", [128, DC], F32, kind="ExternalInput").ap()
    bv_in = dt_("bv_rep", [128, D], F32, kind="ExternalInput").ap()
    deT_in = dt_("deT_d", [128, R], F32, kind="ExternalInput").ap()
    derT_in = dt_("derT_d", [128, R], F32, kind="ExternalInput").ap()
    Wg_in = dt_("Wg_in", [D, E], F32, kind="ExternalInput").ap()
    W1_in = dt_("W1_in", [E, D, H], F32, kind="ExternalInput").ap()
    W2_in = dt_("W2_in", [E, H, D], F32, kind="ExternalInput").ap()
    W3_in = dt_("W3_in", [E, D, H], F32, kind="ExternalInput").ap()
    Ws1_in = dt_("Ws1_in", [D, Hs], F32, kind="ExternalInput").ap()
    Ws2_in = dt_("Ws2_in", [Hs, D], F32, kind="ExternalInput").ap()
    Ws3_in = dt_("Ws3_in", [D, Hs], F32, kind="ExternalInput").ap()
    ident_in = dt_("ident_in", [128, 128], BF16, kind="ExternalInput").ap()
    y_dram = dt_("y_dram", [T, D], BF16, kind="Internal").ap()
    ctx_dram = dt_("ctx_dram", [T, D], BF16, kind="Internal").ap()
    xln_dram = dt_("xln_dram", [T + 128, D], BF16, kind="Internal").ap()
    po_dram = dt_("po_dram", [E * CAP + 128, D], BF16, kind="Internal").ap()
    WT = E * GCH + 1
    tok_dram = dt_("tok_dram", [128 * WT, 1], U32, kind="Internal").ap()
    iota_tok_in = dt_("iota_tok", [128, NT], U32, kind="ExternalInput").ap()
    tok_pad_in = dt_("tok_pad", [128, (E * CAP + 128) // 128], U32,
                     kind="ExternalInput").ap()
    eoff_in = dt_("eoff", [32, 1], F32, kind="ExternalInput").ap()
    identf_in = dt_("identf", [128, 128], F32, kind="ExternalInput").ap()
    DBG = p.get("DBG", False)
    if DBG:
        dbg_y = dt_("dbg_y", [128, NT * D], BF16, kind="ExternalOutput").ap()
        dbg_mw = dt_("dbg_mw", [128, NT * E], F32, kind="ExternalOutput").ap()
        dbg_ctx = dt_("dbg_ctx", [128, NT * D], BF16, kind="ExternalOutput").ap()
        dbg_xln = dt_("dbg_xln", [128, (D // 128) * T], BF16, kind="ExternalOutput").ap()
        dbg_wtk = dt_("dbg_wtk", [128, NT * 4], F32, kind="ExternalOutput").ap()
        dbg_dtk = dt_("dbg_dtk", [128, NT * 4], U32, kind="ExternalOutput").ap()
        dbg_dtk2 = dt_("dbg_dtk2", [128, NT * 4], U32, kind="ExternalOutput").ap()
        dbg_tok = dt_("dbg_tok", [128, WT], U32, kind="ExternalOutput").ap()
        dbg_po = dt_("dbg_po", [E * CAP + 128, D], BF16, kind="ExternalOutput").ap()
        dbg_toksb = dt_("dbg_toksb", [128, E * GCH], U32, kind="ExternalOutput").ap()
        dbg_xg = dt_("dbg_xg", [128, 2 * D], BF16, kind="ExternalOutput").ap()
        dbg_xe = dt_("dbg_xe", [128, (D // 128) * CAP], BF16, kind="ExternalOutput").ap()

    consts = sbuf("consts", [128, 8], F32)
    nc.const_aps.register(F32, 1.0, consts[:, 0:1])
    nc.const_aps.register(F32, 0.5, consts[:, 1:2])
    nc.const_aps.register(F32, 1e-5, consts[:, 2:3])
    nc.const_aps.register(F32, float(-EXPB), consts[:, 3:4])
    nc.const_aps.register(F32, 0.0, consts[:, 4:5])
    ident = sbuf("ident", [128, 128], BF16)
    deT = sbuf("deT", [128, R], BF16)
    derT = sbuf("derT", [128, R], BF16)
    wg_sb = sbuf("wg_sb", [128, DC * E], BF16)
    bq_sb = sbuf("bq_sb", [128, DC], F32)
    bk_sb = sbuf("bk_sb", [128, DC], F32)
    wtk_sb = sbuf("wtk_sb", [128, NT * 4], F32)
    dtk_sb = sbuf("dtk_sb", [128, NT * 4], U32)
    dtk2_sb = sbuf("dtk2_sb", [128, NT * 4], U32)
    scri = sbuf("scri", [128, 8], I32)
    iota_tok = sbuf("iota_tok_sb", [128, NT], U32)
    eoff_sb = sbuf("eoff_sb", [32, 1], F32)
    identf = sbuf("identf_sb", [128, 128], F32)
    zrow = sbuf("zrow", [128, D], BF16)

    PS = es.enter_context(nc.psum_tensor("PS", [128, 4096], F32))

    def bank(i, w=512, off=0):
        return PS[:, i * 512 + off:i * 512 + off + w]

    s_boot, s_cboot = Sem("s_boot"), Sem("s_cboot")
    s_x, s_w = Sem("s_x"), Sem("s_w")
    s_fold, s_lnA, s_scA, s_lnN = Sem("s_fold"), Sem("s_lnA"), Sem("s_scA"), Sem("s_lnN")
    s_mmT, s_evT, s_xo = Sem("s_mmT"), Sem("s_evT"), Sem("s_xo")
    s_mmG, s_scG, s_evG = Sem("s_mmG"), Sem("s_scG"), Sem("s_evG")
    s_mmH, s_scH, s_evH = Sem("s_mmH"), Sem("s_scH"), Sem("s_evH")
    s_mmO, s_evO, s_wf = Sem("s_mmO"), Sem("s_evO"), Sem("s_wf")
    s_mmP, s_evP = Sem("s_mmP"), Sem("s_evP")
    s_mmK, s_evK, s_shK = Sem("s_mmK"), Sem("s_evK"), Sem("s_shK")
    s_mmS, s_evS, s_shS, s_scS = Sem("s_mmS"), Sem("s_evS"), Sem("s_shS"), Sem("s_scS")
    s_evZ, s_mmX, s_evX = Sem("s_evZ"), Sem("s_mmX"), Sem("s_evX")
    s_xst, s_rt, s_tok = Sem("s_xst"), Sem("s_rt"), Sem("s_tok")
    s_pog, s_po, s_xg = Sem("s_pog"), Sem("s_po"), Sem("s_xg")

    with nc.Block() as blk:

        def barrier():
            snap = [(s.h, s.n) for s in all_sems if s.n > 0]

            def w(eng):
                for h, n in snap:
                    eng.wait_ge(h, n)
            blk.sync(w)
            blk.vector(w)
            blk.scalar(w)
            blk.tensor(w)
            blk.gpsimd(w)

        @blk.vector
        def _(v):
            v.memset(consts[:, 0:1], 1.0)
            v.memset(consts[:, 1:2], 0.5)
            v.memset(consts[:, 2:3], 1e-5)
            v.memset(consts[:, 3:4], float(-EXPB))
            v.memset(consts[:, 4:5], 0.0).then_inc(*s_cboot.inc())

        @blk.sync
        def _(sy):
            sy.dma_start(out=ident[:], in_=ident_in).then_inc(*s_boot.inc(16))
            sy.dma_start(out=bq_sb[:], in_=bqT_in).then_inc(*s_boot.inc(16))
            sy.dma_start(out=bk_sb[:], in_=bkT_in).then_inc(*s_boot.inc(16))
            sy.dma_start(out=iota_tok[:], in_=iota_tok_in).then_inc(*s_boot.inc(16))
            sy.dma_start(out=eoff_sb[:], in_=eoff_in).then_inc(*s_boot.inc(16))
            sy.dma_start(out=identf[:], in_=identf_in).then_inc(*s_boot.inc(16))

        @blk.vector
        def _(v):
            v.memset(zrow[:], 0.0).then_inc(*s_cboot.inc())

        @blk.sync
        def _(sy):
            sy.wait_ge(s_cboot.h, s_cboot.n)
            sy.dma_start(out=xln_dram[T:T + 128, :], in_=zrow[:]
                         ).then_inc(*s_boot.inc(16))
            sy.dma_start(out=po_dram[E * CAP:E * CAP + 128, :], in_=zrow[:]
                         ).then_inc(*s_boot.inc(16))

        @blk.gpsimd
        def _(g):
            g.dma_start(out=deT[:], in_=deT_in).then_inc(*s_w.inc(16))
            g.dma_start(out=derT[:], in_=derT_in).then_inc(*s_w.inc(16))
            g.dma_start(out=wg_sb[:].rearrange("p (c e) -> p c e", c=DC),
                        in_=Wg_in.rearrange("(c p) e -> p c e", p=128)
                        ).then_inc(*s_w.inc(16))

        def emit_ln_pass(tag, src, row0, ntiles, tile0, fold_dram, writeback,
                         xlnT, span, ln=True, alloc=None, stage_xln=False,
                         combine=False):
            alloc = alloc or sbuf
            xt = alloc(tag + "_xt", [128, 2 * D], F32)
            xw = alloc(tag + "_xw", [128, 2 * D], F32)
            stat = alloc(tag + "_stat", [128, 8], F32)
            sq = alloc(tag + "_sq", [128, D], BF16)
            xln = alloc(tag + "_xln", [128, 2 * D], BF16) if ln else None
            have_fold = fold_dram is not None
            fbuf = alloc(tag + "_fb", [128, 2 * D], BF16) if have_fold else None
            pob = alloc(tag + "_po", [128, 4 * D], BF16) if combine else None
            b_x, b_fold, b_lnA, b_scA = s_x.n, s_fold.n, s_lnA.n, s_scA.n
            b_lnN, b_mmT, b_evT, b_xo = s_lnN.n, s_mmT.n, s_evT.n, s_xo.n
            b_xst, b_pog = s_xst.n, s_pog.n
            tb = max(1, (DC * 128) // 512)
            per_t = 32 if have_fold else 16
            assert ln or (have_fold and writeback)

            def emit_evac(v, u):
                v.wait_ge(s_mmT.h, b_mmT + u + 1)
                g0 = (u % 2) * tb
                out_ap = AP(tensor=xlnT, offset=u * 128,
                            ap=[[xlnT.shape[1], 128], [span, DC], [1, 128]])
                in_ap = AP(tensor=PS, offset=g0 * 512,
                           ap=[[4096, 128], [128, DC], [1, 128]])
                v.tensor_copy(out=out_ap, in_=in_ap).then_inc(*s_evT.inc())

            @blk.sync
            def _(sy):
                for t in range(ntiles):
                    if t >= 2:
                        if ln:
                            sy.wait_ge(s_lnN.h, b_lnN + (t - 1))
                        else:
                            sy.wait_ge(s_fold.h, b_fold + (t - 1))
                    sy.dma_start(out=xt[:, (t % 2) * D:(t % 2 + 1) * D],
                                 in_=src[row0 + t * 128:row0 + (t + 1) * 128, :]
                                 ).then_inc(*s_x.inc(16))
                    if have_fold:
                        sy.dma_start(out=fbuf[:, (t % 2) * D:(t % 2 + 1) * D],
                                     in_=fold_dram[row0 + t * 128:row0 + (t + 1) * 128, :]
                                     ).then_inc(*s_x.inc(16))
                    if writeback:
                        sy.wait_ge(s_fold.h, b_fold + t + 1)
                        sy.dma_start(out=x_out[row0 + t * 128:row0 + (t + 1) * 128, :],
                                     in_=xw[:, (t % 2) * D:(t % 2 + 1) * D]
                                     ).then_inc(*s_xo.inc(16))
                    if stage_xln:
                        sy.wait_ge(s_lnN.h, b_lnN + t + 1)
                        sy.dma_start(out=xln_dram[row0 + t * 128:row0 + (t + 1) * 128, :],
                                     in_=xln[:, (t % 2) * D:(t % 2 + 1) * D]
                                     ).then_inc(*s_xst.inc(16))

            if combine:
                @blk.gpsimd
                def _(g):
                    for t in range(ntiles):
                        if t >= 1:
                            g.wait_ge(s_fold.h, b_fold + t)
                        for k in range(4):
                            col = (tile0 + t) * 4 + k
                            dst = pob[:, k * D:(k + 1) * D]
                            g.indirect_dma_start(
                                out=dst.bitcast(F32), out_offset=None,
                                in_=po_dram.bitcast(F32),
                                in_offset=IndirectOffsetOnAxis(
                                    dtk_sb[:, col:col + 1], 0)
                                ).then_inc(*s_pog.inc(16))

            @blk.vector
            def _(v):
                v.wait_ge(s_cboot.h, s_cboot.n)
                for t in range(ntiles):
                    sl = slice((t % 2) * D, (t % 2 + 1) * D)
                    v.wait_ge(s_x.h, b_x + per_t * (t + 1))
                    cur = xt[:, sl]
                    if have_fold:
                        if t >= 2:
                            if ln:
                                v.wait_ge(s_scA.h, b_scA + 2 * (t - 2) + 1)
                            if writeback:
                                v.wait_ge(s_xo.h, b_xo + 16 * (t - 1))
                        if combine:
                            v.tensor_add(xw[:, sl], cur, fbuf[:, sl])
                            v.wait_ge(s_pog.h, b_pog + 64 * (t + 1))
                            ins = None
                            for k in range(4):
                                col = (tile0 + t) * 4 + k
                                pk = pob[:, k * D:(k + 1) * D]
                                ins = v.scalar_tensor_tensor(
                                    out=xw[:, sl], in0=pk,
                                    scalar=wtk_sb[:, col:col + 1],
                                    in1=xw[:, sl], op0=OP.mult, op1=OP.add)
                            ins.then_inc(*s_fold.inc())
                        else:
                            v.tensor_add(xw[:, sl], cur, fbuf[:, sl]
                                         ).then_inc(*s_fold.inc())
                        cur = xw[:, sl]
                    if not ln:
                        continue
                    v.reduce_sum(stat[:, 0:1], cur, axis=AX)
                    v.drain()
                    v.tensor_scalar(stat[:, 1:2], stat[:, 0:1], -1.0 / D, None, OP.mult)
                    v.wait_ge(s_scA.h, b_scA + 2 * t + 1)
                    v.tensor_scalar(stat[:, 2:3], stat[:, 4:5], 1.0 / D, None, OP.mult)
                    v.drain()
                    v.tensor_mul(stat[:, 3:4], stat[:, 1:2], stat[:, 1:2])
                    v.drain()
                    v.tensor_sub(stat[:, 2:3], stat[:, 2:3], stat[:, 3:4])
                    v.drain()
                    v.tensor_scalar(stat[:, 2:3], stat[:, 2:3], 1e-5, None, OP.add)
                    v.drain().then_inc(*s_lnA.inc())
                    v.wait_ge(s_scA.h, b_scA + 2 * t + 2)
                    v.reciprocal(stat[:, 6:7], stat[:, 5:6])
                    v.drain()
                    if t >= 2:
                        v.wait_ge(s_mmT.h, b_mmT + (t - 1))
                        if stage_xln:
                            v.wait_ge(s_xst.h, b_xst + 16 * (t - 1))
                    v.tensor_scalar(xln[:, sl], cur, stat[:, 1:2], stat[:, 6:7],
                                    OP.add, OP.mult).then_inc(*s_lnN.inc())
                    if t >= 1:
                        emit_evac(v, t - 1)
                if ln:
                    emit_evac(v, ntiles - 1)

            if not ln:
                return

            @blk.scalar
            def _(sc):
                sc.wait_ge(s_cboot.h, s_cboot.n)
                for t in range(ntiles):
                    sl = slice((t % 2) * D, (t % 2 + 1) * D)
                    src_t = xw if have_fold else xt
                    if have_fold:
                        sc.wait_ge(s_fold.h, b_fold + t + 1)
                    else:
                        sc.wait_ge(s_x.h, b_x + per_t * (t + 1))
                    sc.activation(out=sq[:], in_=src_t[:, sl], func=AF.Square,
                                  accum_out=stat[:, 4:5])
                    sc.drain().then_inc(*s_scA.inc())
                    sc.wait_ge(s_lnA.h, b_lnA + t + 1)
                    sc.activation(out=stat[:, 5:6], in_=stat[:, 2:3], func=AF.Sqrt)
                    sc.drain().then_inc(*s_scA.inc())

            @blk.tensor
            def _(te):
                te.wait_ge(s_boot.h, 16)
                for t in range(ntiles):
                    te.wait_ge(s_lnN.h, b_lnN + t + 1)
                    if t >= 2:
                        te.wait_ge(s_evT.h, b_evT + (t - 1))
                    g0 = (t % 2) * tb
                    sl0 = (t % 2) * D
                    ins = None
                    for c in range(DC):
                        ins = te.matmul(
                            out=bank(g0 + (c * 128) // 512, 128, (c * 128) % 512),
                            lhsT=xln[:, sl0 + c * 128:sl0 + (c + 1) * 128],
                            rhs=ident[:],
                            start=((c * 128) % 512 == 0),
                            stop=((c * 128) % 512 == 512 - 128 or c == DC - 1))
                    ins.then_inc(*s_mmT.inc())


        def emit_gating(tag, xlnT, span, ntiles, tile0, alloc=None, balloc=None,
                        waits=()):
            alloc = alloc or sbuf
            balloc = balloc or alloc
            pg = balloc(tag + "_pg", [128, 4 * E], F32)
            p_all = balloc(tag + "_pa", [128, ntiles * E], F32)
            gz = balloc(tag + "_gz", [128, 16], F32)
            t8 = balloc(tag + "_t8", [128, 8], F32)
            mk = balloc(tag + "_mk", [128, 4 * E], F32)
            mT = balloc(tag + "_mT", [32, ntiles * 128], BF16)
            cum = balloc(tag + "_cum", [32, ntiles * 128], F32)
            zer = balloc(tag + "_zer", [32, ntiles * 128], BF16)
            dT = balloc(tag + "_dT", [128, 32], F32)
            scr = balloc(tag + "_scr", [128, E + 8], F32)
            b_mmG, b_scG, b_evG, b_rt = s_mmG.n, s_scG.n, s_evG.n, s_rt.n
            GE = 4
            ngrp = (ntiles + GE - 1) // GE
            DUMMY = float(E * CAP)
            W3_ = ntiles * 128

            @blk.sync
            def _(sy):
                sy.wait_ge(s_xg.h, s_xg.n)
                sy.dma_start(out=tok_dram.rearrange("(q s) o -> q (s o)", q=128),
                             in_=tok_pad_in).then_inc(*s_tok.inc(16))

            @blk.tensor
            def _(te):
                te.wait_ge(s_w.h, 48)
                te.wait_ge(s_evT.h, s_evT.n)
                for g in range(ngrp):
                    tl = list(range(g * GE, min((g + 1) * GE, ntiles)))
                    if g >= 2:
                        te.wait_ge(s_scG.h, b_scG + (g - 1))
                    bk6 = 6 + (g % 2)
                    ins = None
                    for t in tl:
                        for c in range(DC):
                            ins = te.matmul(
                                out=bank(bk6, E, (t % GE) * E),
                                lhsT=xlnT[:, c * span + t * 128:c * span + (t + 1) * 128],
                                rhs=wg_sb[:, c * E:(c + 1) * E],
                                start=(t == tl[0] and c == 0),
                                stop=(t == tl[-1] and c == DC - 1))
                    ins.then_inc(*s_mmG.inc())

            @blk.vector
            def _(v):
                for sm, val in waits:
                    v.wait_ge(sm.h, val)
                v.memset(zer[:], 0.0)
                for g in range(ngrp):
                    tl = list(range(g * GE, min((g + 1) * GE, ntiles)))
                    bk6 = 6 + (g % 2)
                    v.wait_ge(s_mmG.h, b_mmG + g + 1)
                    for t in tl:
                        j = t % GE
                        v.reduce_max(gz[:, j:j + 1], bank(bk6, E, j * E),
                                     axis=AX, negate=True)
                    v.drain().then_inc(*s_evG.inc())
                    v.wait_ge(s_scG.h, b_scG + g + 1)
                    for t in tl:
                        j = t % GE
                        v.reciprocal(gz[:, 4 + j:5 + j], gz[:, 4 + j:5 + j])
                    v.drain()
                    for t in tl:
                        j = t % GE
                        gt = tile0 + t
                        v.tensor_scalar(p_all[:, t * E:(t + 1) * E],
                                        pg[:, j * E:(j + 1) * E],
                                        gz[:, 4 + j:5 + j], None, OP.mult)
                        v.drain()
                        v.max(out=t8[:], in_=p_all[:, t * E:(t + 1) * E])
                        v.drain()
                        v.tensor_copy(wtk_sb[:, gt * 4:gt * 4 + 4], t8[:, 0:4])
                    v.drain().then_inc(*s_evG.inc())

            @blk.scalar
            def _(sc):
                for g in range(ngrp):
                    tl = list(range(g * GE, min((g + 1) * GE, ntiles)))
                    bk6 = 6 + (g % 2)
                    sc.wait_ge(s_evG.h, b_evG + 2 * g + 1)
                    for t in tl:
                        j = t % GE
                        sc.activation(out=pg[:, j * E:(j + 1) * E],
                                      in_=bank(bk6, E, j * E), func=AF.Exp,
                                      bias=gz[:, j:j + 1], scale=1.0,
                                      accum_out=gz[:, 4 + j:5 + j])
                    sc.drain().then_inc(*s_scG.inc())

            vA_end = s_evG.n
            b1_ev = s_evG.n
            b1_mm = s_mmG.n

            @blk.vector
            def _(v):
                nonlocal_marker = None
                for g in range(ngrp):
                    tl = list(range(g * GE, min((g + 1) * GE, ntiles)))
                    for t in tl:
                        j = t % GE
                        gt = tile0 + t
                        if g >= 1:
                            v.wait_ge(s_mmG.h, b1_mm + g)
                        v.tensor_scalar(mk[:, j * E:(j + 1) * E],
                                        p_all[:, t * E:(t + 1) * E],
                                        wtk_sb[:, gt * 4 + 3:gt * 4 + 4],
                                        None, OP.is_ge)
                    v.drain().then_inc(*s_evG.inc())
                    v.wait_ge(s_mmG.h, b1_mm + g + 1)
                    v.tensor_copy(out=mT[0:E, g * GE * 128:(g * GE + len(tl)) * 128],
                                  in_=PS[0:E, 0:len(tl) * 128]).then_inc(*s_evG.inc())

            @blk.tensor
            def _(te):
                for g in range(ngrp):
                    tl = list(range(g * GE, min((g + 1) * GE, ntiles)))
                    te.wait_ge(s_evG.h, b1_ev + 2 * g + 1)
                    if g >= 1:
                        te.wait_ge(s_evG.h, b1_ev + 2 * g)
                    ins = None
                    for t in tl:
                        j = t % GE
                        ins = te.matmul(out=PS[0:E, j * 128:(j + 1) * 128],
                                        lhsT=mk[:, j * E:(j + 1) * E],
                                        rhs=identf[:],
                                        start=(j == 0), stop=(t == tl[-1]))
                    ins.then_inc(*s_mmG.inc())

            b2_ev = s_evG.n

            @blk.vector
            def _(v):
                v.wait_ge(s_evG.h, b2_ev)
                v.tensor_tensor_scan(out=cum[0:E, :], data0=mT[0:E, :],
                                     data1=zer[0:E, :], initial=0.0,
                                     op0=OP.add, op1=OP.add)
                v.drain()
                v.tensor_scalar(mT[0:E, :], cum[0:E, :], float(CAP), None, OP.is_le)
                v.tensor_scalar(cum[0:E, :], cum[0:E, :], eoff_sb[0:E, :],
                                None, OP.add)
                v.tensor_scalar(cum[0:E, :], cum[0:E, :], -DUMMY, None, OP.add)
                v.tensor_mul(cum[0:E, :], cum[0:E, :], mT[0:E, :])
                v.tensor_scalar(cum[0:E, :], cum[0:E, :], DUMMY, None, OP.add)
                v.drain().then_inc(*s_evG.inc())

            b3_ev = s_evG.n
            b3_mm = s_mmG.n

            @blk.tensor
            def _(te):
                te.wait_ge(s_evG.h, b3_ev)
                for t in range(ntiles):
                    if t >= 1:
                        te.wait_ge(s_evG.h, b3_ev + t)
                    te.matmul(out=PS[:, 512:512 + 32],
                              lhsT=cum[0:E, t * 128:(t + 1) * 128],
                              rhs=identf[0:E, 0:32],
                              start=True, stop=True).then_inc(*s_mmG.inc())

            @blk.vector
            def _(v):
                for t in range(ntiles):
                    gt = tile0 + t
                    v.wait_ge(s_mmG.h, b3_mm + t + 1)
                    v.tensor_copy(out=dT[:], in_=PS[:, 512:512 + 32])
                    v.drain()
                    ins = None
                    for k in range(4):
                        v.tensor_scalar(scr[:, 0:E], p_all[:, t * E:(t + 1) * E],
                                        wtk_sb[:, gt * 4 + k:gt * 4 + k + 1],
                                        None, OP.is_equal)
                        v.drain()
                        v.scalar_tensor_tensor(
                            out=scr[:, 0:E], in0=scr[:, 0:E], scalar=1.0,
                            in1=dT[:, 0:E], op0=OP.mult, op1=OP.mult,
                            accum_out=scr[:, E + k:E + k + 1])
                        v.drain()
                        v.tensor_copy(dtk_sb[:, gt * 4 + k:gt * 4 + k + 1],
                                      scr[:, E + k:E + k + 1])
                        v.tensor_copy(scri[:, 0:1], scr[:, E + k:E + k + 1])
                        v.drain()
                        v.tensor_scalar(scri[:, 1:2], scri[:, 0:1], 7, None,
                                        OP.logical_shift_right)
                        v.tensor_scalar(scri[:, 2:3], scri[:, 0:1], 127, None,
                                        OP.bitwise_and)
                        v.drain()
                        v.tensor_scalar(scri[:, 2:3], scri[:, 2:3], WT, None,
                                        OP.mult)
                        v.drain()
                        v.tensor_add(scri[:, 3:4], scri[:, 2:3], scri[:, 1:2])
                        v.drain()
                        v.tensor_copy(dtk2_sb[:, gt * 4 + k:gt * 4 + k + 1],
                                      scri[:, 3:4])
                    v.drain().then_inc(*s_evG.inc())

            @blk.gpsimd
            def _(g):
                g.wait_ge(s_tok.h, s_tok.n)
                for t in range(ntiles):
                    gt = tile0 + t
                    g.wait_ge(s_evG.h, b3_ev + t + 1)
                    for k in range(4):
                        g.indirect_dma_start(
                            out=tok_dram, out_offset=IndirectOffsetOnAxis(
                                dtk2_sb[:, gt * 4 + k:gt * 4 + k + 1], 0),
                            in_=iota_tok[:, gt:gt + 1], in_offset=None
                            ).then_inc(*s_rt.inc(16))

        def emit_swiglu(tag, xlnT, span, w1sb, w3sb, w2sb, hxc, y_sb, hg, hgl,
                        wait_w=None, extra_waits=()):
            b_mmH, b_scH, b_evH = s_mmH.n, s_scH.n, s_evH.n
            b_mmO, b_evO = s_mmO.n, s_evO.n
            HW = hxc * 128

            @blk.tensor
            def _(te):
                if wait_w is not None:
                    te.wait_ge(s_w.h, wait_w)
                te.wait_ge(s_evT.h, s_evT.n)
                for sm, val in extra_waits:
                    te.wait_ge(sm.h, val)
                for b in range(NTB):
                    col = b * TB
                    for hc in range(hxc):
                        idx = b * hxc + hc
                        if idx >= 2:
                            te.wait_ge(s_evH.h, b_evH + idx - 1)
                        bh1, bh3 = idx % 2, 2 + idx % 2
                        for c in range(DC):
                            te.matmul(out=bank(bh1, TB),
                                      lhsT=w1sb[:, c * HW + hc * 128:c * HW + (hc + 1) * 128],
                                      rhs=xlnT[:, c * span + col:c * span + col + TB],
                                      start=(c == 0), stop=(c == DC - 1))
                        ins = None
                        for c in range(DC):
                            ins = te.matmul(out=bank(bh3, TB),
                                            lhsT=w3sb[:, c * HW + hc * 128:c * HW + (hc + 1) * 128],
                                            rhs=xlnT[:, c * span + col:c * span + col + TB],
                                            start=(c == 0), stop=(c == DC - 1))
                        ins.then_inc(*s_mmH.inc())
                    te.wait_ge(s_evH.h, b_evH + (b + 1) * hxc)
                    for tc in range(TBC):
                        for n in range(NB2):
                            oidx = b * TBC * NB2 + tc * NB2 + n
                            bo = 4 + oidx % 2
                            if oidx >= 2:
                                te.wait_ge(s_evO.h, b_evO + oidx - 1)
                            ins = None
                            for k in range(hxc):
                                ins = te.matmul(
                                    out=bank(bo, OB),
                                    lhsT=hgl[:, k * TB + tc * 128:k * TB + (tc + 1) * 128],
                                    rhs=w2sb[:, k * D + n * OB:k * D + (n + 1) * OB],
                                    start=(k == 0), stop=(k == hxc - 1))
                            ins.then_inc(*s_mmO.inc())

            @blk.scalar
            def _(sc):
                for b in range(NTB):
                    for hc in range(hxc):
                        idx = b * hxc + hc
                        sc.wait_ge(s_mmH.h, b_mmH + idx + 1)
                        sc.activation(out=hg[:, (idx % 2) * TB:(idx % 2) * TB + TB],
                                      in_=bank(idx % 2, TB), func=AF.Silu
                                      ).then_inc(*s_scH.inc())

            @blk.vector
            def _(v):
                for b in range(NTB):
                    for hc in range(hxc):
                        idx = b * hxc + hc
                        v.wait_ge(s_scH.h, b_scH + idx + 1)
                        v.tensor_mul(hgl[:, hc * TB:hc * TB + TB],
                                     bank(2 + idx % 2, TB),
                                     hg[:, (idx % 2) * TB:(idx % 2) * TB + TB]
                                     ).then_inc(*s_evH.inc())
                    for tc in range(TBC):
                        for n in range(NB2):
                            oidx = b * TBC * NB2 + tc * NB2 + n
                            bo = 4 + oidx % 2
                            v.wait_ge(s_mmO.h, b_mmO + oidx + 1)
                            gt = b * TBC + tc
                            ysl = y_sb[:, gt * D + n * OB:gt * D + (n + 1) * OB]
                            v.tensor_scalar(ysl, bank(bo, OB), 0.5, None, OP.mult
                                            ).then_inc(*s_evO.inc())

        def emit_expert_routed(e, w1sb, w3sb, w2sb, tok_sb, xg, xeT, po_sb,
                               hg, hgl, wait_w, free_sem, b_tok, sweep,
                               prefetch=None):
            b_xg, b_mmT, b_evT = s_xg.n, s_mmT.n, s_evT.n
            b_mmH, b_scH, b_evH = s_mmH.n, s_scH.n, s_evH.n
            b_mmO, b_evO, b_po = s_mmO.n, s_evO.n, s_po.n
            PC = CAP // 128
            NHX = NPB * HC

            @blk.gpsimd
            def _(g):
                g.wait_ge(s_tok.h, b_tok)
                for gc in range(GCH):
                    gci = e * GCH + gc
                    if gci >= 2:
                        g.wait_ge(s_mmT.h, sweep["mmT"] + gci - 1)
                    g.indirect_dma_start(
                        out=xg[:, (gc % 2) * D:(gc % 2 + 1) * D].bitcast(F32),
                        out_offset=None,
                        in_=xln_dram.bitcast(F32),
                        in_offset=IndirectOffsetOnAxis(
                            tok_sb[:, e * GCH + gc:e * GCH + gc + 1], 0)
                        ).then_inc(*s_xg.inc(16))
                if prefetch is not None:
                    prefetch(g)

            @blk.tensor
            def _(te):
                te.wait_ge(s_w.h, wait_w)
                for gc in range(GCH):
                    gci = e * GCH + gc
                    te.wait_ge(s_xg.h, b_xg + 16 * (gc + 1))
                    if gci >= 1:
                        te.wait_ge(s_evT.h, sweep["evT"] + gci)
                    ins = None
                    for c in range(DC):
                        ins = te.matmul(
                            out=bank(6 + c // 4, 128, (c % 4) * 128),
                            lhsT=xg[:, (gc % 2) * D + c * 128:
                                    (gc % 2) * D + (c + 1) * 128],
                            rhs=ident[:],
                            start=(c % 4 == 0), stop=(c % 4 == 3 or c == DC - 1))
                    ins.then_inc(*s_mmT.inc())
                for pb in range(NPB):
                    for hc in range(HC):
                        idx = pb * HC + hc
                        ghx = e * NHX + idx
                        te.wait_ge(s_evH.h, sweep["evH"] + max(ghx - 1, 0))
                        te.wait_ge(s_evT.h, b_evT + GCH)
                        bh1, bh3 = idx % 2, 2 + idx % 2
                        for c in range(DC):
                            te.matmul(out=bank(bh1, PB),
                                      lhsT=w1sb[:, c * H + hc * 128:c * H + (hc + 1) * 128],
                                      rhs=xeT[:, c * CAP + pb * PB:c * CAP + (pb + 1) * PB],
                                      start=(c == 0), stop=(c == DC - 1))
                        ins = None
                        for c in range(DC):
                            ins = te.matmul(out=bank(bh3, PB),
                                            lhsT=w3sb[:, c * H + hc * 128:c * H + (hc + 1) * 128],
                                            rhs=xeT[:, c * CAP + pb * PB:c * CAP + (pb + 1) * PB],
                                            start=(c == 0), stop=(c == DC - 1))
                        ins.then_inc(*s_mmH.inc())
                te.wait_ge(s_evH.h, sweep["evH"] + (e + 1) * NHX)
                for pc in range(PC):
                    for n in range(NB2):
                        oidx = pc * NB2 + n
                        gox = e * PC * NB2 + oidx
                        gpc = e * PC + pc
                        bo = 4 + oidx % 2
                        te.wait_ge(s_evO.h, sweep["evO"] + max(gox - 1, 0))
                        if gpc >= 2:
                            te.wait_ge(s_po.h, sweep["po"] + 16 * (gpc - 1))
                        ins = None
                        for k in range(HC):
                            ins = te.matmul(
                                out=bank(bo, OB),
                                lhsT=hgl[:, k * CAP + pc * 128:k * CAP + (pc + 1) * 128],
                                rhs=w2sb[:, k * D + n * OB:k * D + (n + 1) * OB],
                                start=(k == 0), stop=(k == HC - 1))
                        ins.then_inc(*s_mmO.inc())

            @blk.vector
            def _(v):
                for gc in range(GCH):
                    v.wait_ge(s_mmT.h, b_mmT + gc + 1)
                    out_ap = AP(tensor=xeT, offset=gc * 128,
                                ap=[[xeT.shape[1], 128], [CAP, DC], [1, 128]])
                    in_ap = AP(tensor=PS, offset=6 * 512,
                               ap=[[4096, 128], [128, DC], [1, 128]])
                    v.tensor_copy(out=out_ap, in_=in_ap).then_inc(*s_evT.inc())
                for pb in range(NPB):
                    for hc in range(HC):
                        idx = pb * HC + hc
                        v.wait_ge(s_scH.h, b_scH + idx + 1)
                        v.tensor_mul(hgl[:, hc * CAP + pb * PB:hc * CAP + pb * PB + PB],
                                     bank(2 + idx % 2, PB),
                                     hg[:, (idx % 2) * PB:(idx % 2) * PB + PB]
                                     ).then_inc(*s_evH.inc())
                ins = None
                for pc in range(PC):
                    for n in range(NB2):
                        oidx = pc * NB2 + n
                        gpc = e * PC + pc
                        bo = 4 + oidx % 2
                        v.wait_ge(s_mmO.h, b_mmO + oidx + 1)
                        if gpc >= 2 and n == 0:
                            v.wait_ge(s_po.h, sweep["po"] + 16 * (gpc - 1))
                        ins = v.tensor_scalar(
                            po_sb[:, (pc % 2) * D + n * OB:(pc % 2) * D + (n + 1) * OB],
                            bank(bo, OB), 0.5, None, OP.mult)
                        ins.then_inc(*s_evO.inc())
                if free_sem is not None:
                    v.drain().then_inc(*free_sem.inc())

            @blk.scalar
            def _(sc):
                for pb in range(NPB):
                    for hc in range(HC):
                        idx = pb * HC + hc
                        sc.wait_ge(s_mmH.h, b_mmH + idx + 1)
                        sc.activation(out=hg[:, (idx % 2) * PB:(idx % 2) * PB + PB],
                                      in_=bank(idx % 2, PB), func=AF.Silu
                                      ).then_inc(*s_scH.inc())

            @blk.sync
            def _(sy):
                for pc in range(PC):
                    sy.wait_ge(s_evO.h, b_evO + pc * NB2 + NB2)
                    sy.dma_start(
                        out=po_dram[e * CAP + pc * 128:e * CAP + (pc + 1) * 128, :],
                        in_=po_sb[:, (pc % 2) * D:(pc % 2 + 1) * D]
                        ).then_inc(*s_po.inc(16))

        def emit_ffn(tag, src, fold_dram, writeback, combine):
            with ExitStack() as ph:
                psb = lambda n, s2, d: ph.enter_context(nc.sbuf_tensor(tag + n, s2, d))
                y_sb = psb("_y", [128, NT * D], BF16)
                hg = psb("_hg", [128, 2 * max(TB, PB)], BF16)
                hgl = psb("_hgl", [128, max(HsC * TB, HC * CAP)], BF16)
                with ExitStack() as st_a:
                    asb = lambda n, s2, d: st_a.enter_context(
                        nc.sbuf_tensor(tag + n, s2, d))
                    xlnT = asb("_xlnT", [128, DC * T], BF16)
                    st_l = ExitStack()
                    lsb = lambda n, s2, d: st_l.enter_context(
                        nc.sbuf_tensor(tag + n, s2, d))
                    emit_ln_pass(tag + "_ln", src, 0, NT, 0, fold_dram, writeback,
                                 xlnT, T, alloc=lsb, stage_xln=True,
                                 combine=combine)
                    ln_marks = [(s_xo, s_xo.n), (s_xst, s_xst.n),
                                (s_scA, s_scA.n), (s_fold, s_fold.n),
                                (s_lnN, s_lnN.n)]
                    st_l.close()
                    st_g = ExitStack()
                    gsb = lambda n, s2, d: st_g.enter_context(
                        nc.sbuf_tensor(tag + n, s2, d))
                    emit_gating(tag + "_g", xlnT, T, NT, 0, alloc=asb,
                                balloc=gsb, waits=ln_marks)
                    rt_marks = [(s_evG, s_evG.n), (s_scG, s_scG.n),
                                (s_mmG, s_mmG.n)]
                    st_g.close()
                    if DBG and tag == "f1":
                        @blk.sync
                        def _(sy):
                            sy.wait_ge(s_evT.h, s_evT.n)
                            sy.dma_start(out=dbg_xln, in_=xlnT[:]
                                         ).then_inc(*s_xo.inc(16))
                    ws1 = asb("_ws1", [128, DC * Hs], BF16)
                    ws3 = asb("_ws3", [128, DC * Hs], BF16)
                    ws2 = asb("_ws2", [128, HsC * D], BF16)
                    b_w = s_w.n

                    @blk.gpsimd
                    def _(g):
                        for sm, val in rt_marks:
                            g.wait_ge(sm.h, val)
                        g.dma_start(out=ws1[:].rearrange("p (c h) -> p c h", c=DC),
                                    in_=Ws1_in.rearrange("(c p) h -> p c h", p=128)
                                    ).then_inc(*s_w.inc(16))
                        g.dma_start(out=ws3[:].rearrange("p (c h) -> p c h", c=DC),
                                    in_=Ws3_in.rearrange("(c p) h -> p c h", p=128)
                                    ).then_inc(*s_w.inc(16))
                        g.dma_start(out=ws2[:].rearrange("p (c h) -> p c h", c=HsC),
                                    in_=Ws2_in.rearrange("(c p) h -> p c h", p=128)
                                    ).then_inc(*s_w.inc(16))

                    emit_swiglu(tag + "_sh", xlnT, T, ws1, ws3, ws2, HsC, y_sb,
                                hg, hgl, wait_w=b_w + 48, extra_waits=rt_marks)
                shared_mmO = s_mmO.n
                sweep_marks = [(s_mmO, s_mmO.n), (s_scG, s_scG.n)]
                with ExitStack() as st_b:
                    bsb = lambda n, s2, d: st_b.enter_context(
                        nc.sbuf_tensor(tag + n, s2, d))
                    tok_sb = bsb("_tok", [128, E * GCH], U32)
                    xg = bsb("_xg", [128, 2 * D], BF16)
                    xeT = bsb("_xeT", [128, DC * CAP], BF16)
                    po_sb = bsb("_po2", [128, 2 * D], BF16)
                    w1e = bsb("_w1e", [128, 2 * DC * H], BF16)
                    w3e = bsb("_w3e", [128, 2 * DC * H], BF16)
                    w2e = bsb("_w2e", [128, 2 * HC * D], BF16)
                    EH = DC * H
                    b_w2 = s_w.n
                    b_wf = s_wf.n
                    b_rt_all = s_rt.n

                    @blk.sync
                    def _(sy):
                        sy.wait_ge(s_rt.h, b_rt_all)
                        sy.wait_ge(s_mmO.h, shared_mmO)
                        sy.wait_ge(s_evO.h, s_evO.n)
                        sy.dma_start(out=tok_sb[:],
                                     in_=tok_dram.rearrange("(q s) o -> q (s o)", q=128)
                                     [:, 0:E * GCH]).then_inc(*s_tok.inc(16))
                    b_tok = s_tok.n

                    load_tgt = {}

                    def load_expert(g, e):
                        sl1 = slice((e % 2) * EH, (e % 2 + 1) * EH)
                        sl2 = slice((e % 2) * HC * D, (e % 2 + 1) * HC * D)
                        g.dma_start(out=w1e[:, sl1].rearrange("p (c h) -> p c h", c=DC),
                                    in_=W1_in[e].rearrange("(c p) h -> p c h", p=128)
                                    ).then_inc(*s_w.inc(16))
                        g.dma_start(out=w3e[:, sl1].rearrange("p (c h) -> p c h", c=DC),
                                    in_=W3_in[e].rearrange("(c p) h -> p c h", p=128)
                                    ).then_inc(*s_w.inc(16))
                        g.dma_start(out=w2e[:, sl2].rearrange("p (c h) -> p c h", c=HC),
                                    in_=W2_in[e].rearrange("(c p) h -> p c h", p=128)
                                    ).then_inc(*s_w.inc(16))
                        load_tgt[e] = s_w.n

                    @blk.gpsimd
                    def _(g):
                        g.wait_ge(s_mmO.h, shared_mmO)
                        load_expert(g, 0)
                        if E > 1:
                            load_expert(g, 1)

                    first = True
                    sweep = dict(mmT=s_mmT.n, evT=s_evT.n, evH=s_evH.n,
                                 evO=s_evO.n, po=s_po.n)
                    for e in range(E):
                        if first:
                            @blk.tensor
                            def _(te):
                                for sm, val in sweep_marks:
                                    te.wait_ge(sm.h, val)
                            first = False
                        if 1 <= e < E - 1:
                            def prefetch(g, e=e):
                                g.wait_ge(s_wf.h, b_wf + e)
                                load_expert(g, e + 1)
                        else:
                            prefetch = None
                        sl1 = slice((e % 2) * EH, (e % 2 + 1) * EH)
                        sl2 = slice((e % 2) * HC * D, (e % 2 + 1) * HC * D)
                        emit_expert_routed(e, w1e[:, sl1], w3e[:, sl1],
                                           w2e[:, sl2], tok_sb, xg, xeT, po_sb,
                                           hg, hgl, load_tgt[e], s_wf, b_tok,
                                           sweep, prefetch=prefetch)
                    if DBG and tag == "f1":
                        @blk.sync
                        def _(sy):
                            sy.wait_ge(s_wf.h, b_wf + E)
                            sy.dma_start(out=dbg_toksb, in_=tok_sb[:]
                                         ).then_inc(*s_xo.inc(16))
                            sy.dma_start(out=dbg_xg, in_=xg[:]
                                         ).then_inc(*s_xo.inc(16))
                            sy.dma_start(out=dbg_xe, in_=xeT[:]
                                         ).then_inc(*s_xo.inc(16))
                @blk.sync
                def _(sy):
                    sy.wait_ge(s_evO.h, s_evO.n)
                    sy.dma_start(out=y_dram.rearrange("(t p) d -> p t d", p=128),
                                 in_=y_sb[:].rearrange("p (t d) -> p t d", d=D)
                                 ).then_inc(*s_xo.inc(16))
                barrier()

        def emit_attn():
            for sq_ in range(B_loc):
                with ExitStack() as ph:
                    psb = lambda n, sh, d: ph.enter_context(
                        nc.sbuf_tensor(f"at{sq_}{n}", sh, d))
                    bv_sb = psb("_bv", [128, D], F32)

                    @blk.sync
                    def _(sy):
                        sy.dma_start(out=bv_sb[:], in_=bv_in).then_inc(*s_boot.inc(16))
                    qT = psb("_qT", [128, DC * S], BF16)
                    kT = psb("_kT", [128, DC * S], BF16)
                    v_sb = psb("_v", [128, ST * D], BF16)
                    st1 = ph.enter_context(ExitStack())
                    s1b = lambda n, sh, d: st1.enter_context(
                        nc.sbuf_tensor(f"at{sq_}{n}", sh, d))
                    xlnT = s1b("_xlnT", [128, DC * S], BF16)
                    with ExitStack() as st2:
                        s2b = lambda n, sh, d: st2.enter_context(
                            nc.sbuf_tensor(f"at{sq_}{n}", sh, d))
                        emit_ln_pass(f"at{sq_}_ln", x_in, sq_ * S, ST, sq_ * ST,
                                     y_dram, True, xlnT, S, alloc=s2b,
                                     combine=True)
                        barrier()
                    wpj = s1b("_wpj", [128, DC * D], BF16)
                    b_w = s_w.n
                    b_mmP, b_evP = s_mmP.n, s_evP.n
                    per = DC * SB2

                    @blk.gpsimd
                    def _(g):
                        for j, wt in enumerate((WqT_in, WkT_in, WvT_in)):
                            if j >= 1:
                                g.wait_ge(s_evP.h, b_evP + j * per)
                            g.dma_start(
                                out=wpj[:].rearrange("p (c h) -> p c h", c=DC),
                                in_=wt.rearrange("(c p) h -> p c h", p=128)
                                ).then_inc(*s_w.inc(16))

                    @blk.tensor
                    def _(te):
                        for j in range(3):
                            te.wait_ge(s_w.h, b_w + 16 * (j + 1))
                            wsl = 0
                            if j < 2:
                                for mc in range(DC):
                                    for n in range(SB2):
                                        idx = j * per + mc * SB2 + n
                                        bo = idx % 4
                                        if idx >= 4:
                                            te.wait_ge(s_evP.h, b_evP + idx - 3)
                                        ins = None
                                        for c in range(DC):
                                            ins = te.matmul(
                                                out=bank(bo, SBW),
                                                lhsT=wpj[:, wsl + c * D + mc * 128:
                                                         wsl + c * D + (mc + 1) * 128],
                                                rhs=xlnT[:, c * S + n * SBW:
                                                         c * S + (n + 1) * SBW],
                                                start=(c == 0), stop=(c == DC - 1))
                                        ins.then_inc(*s_mmP.inc())
                            else:
                                for tc in range(ST):
                                    for n in range(NB2):
                                        idx = 2 * per + tc * NB2 + n
                                        bo = idx % 4
                                        if idx >= 4:
                                            te.wait_ge(s_evP.h, b_evP + idx - 3)
                                        ins = None
                                        for c in range(DC):
                                            ins = te.matmul(
                                                out=bank(bo, OB),
                                                lhsT=xlnT[:, c * S + tc * 128:
                                                          c * S + (tc + 1) * 128],
                                                rhs=wpj[:, wsl + c * D + n * OB:
                                                         wsl + c * D + (n + 1) * OB],
                                                start=(c == 0), stop=(c == DC - 1))
                                        ins.then_inc(*s_mmP.inc())

                    @blk.vector
                    def _(v):
                        for j in range(3):
                            if j < 2:
                                bias = bq_sb if j == 0 else bk_sb
                                for mc in range(DC):
                                    for n in range(SB2):
                                        idx = j * per + mc * SB2 + n
                                        bo = idx % 4
                                        v.wait_ge(s_mmP.h, b_mmP + idx + 1)
                                        dst = (qT if j == 0 else kT)[
                                            :, mc * S + n * SBW:mc * S + (n + 1) * SBW]
                                        if j == 0:
                                            ins = v.tensor_scalar(
                                                dst, bank(bo, SBW), bias[:, mc:mc + 1],
                                                0.125, OP.add, OP.mult)
                                        else:
                                            ins = v.tensor_scalar(
                                                dst, bank(bo, SBW), bias[:, mc:mc + 1],
                                                None, OP.add)
                                        ins.then_inc(*s_evP.inc())
                            else:
                                for tc in range(ST):
                                    for n in range(NB2):
                                        idx = 2 * per + tc * NB2 + n
                                        bo = idx % 4
                                        v.wait_ge(s_mmP.h, b_mmP + idx + 1)
                                        v.tensor_add(
                                            v_sb[:, tc * D + n * OB:tc * D + (n + 1) * OB],
                                            bank(bo, OB), bv_sb[:, n * OB:(n + 1) * OB]
                                            ).then_inc(*s_evP.inc())
                    barrier()
                    st1.close()

                    ctx_seq = psb("_ctx", [128, ST * D], BF16)
                    relkT = psb("_relkT", [128, JT * IQ], BF16)
                    kd_sb = psb("_kd", [128, 2 * WK], BF16)
                    qdb = psb("_qdb", [128, 2 * WQ], BF16)
                    rq_sb = psb("_rq", [128, 2 * JB], BF16)
                    Pb = psb("_Pb", [128, S], BF16)
                    PT = psb("_PT", [128, 512], BF16)
                    zp = psb("_zp", [128, 2 * NJB + 4], F32)

                    for h in range(NH):
                        hb = 64 * (h % 2)
                        hcol = (h // 2) * S
                        for iq in range(NIQ):
                            b_mmK, b_evK, b_shK = s_mmK.n, s_evK.n, s_shK.n

                            @blk.tensor
                            def _(te):
                                te.wait_ge(s_w.h, 32)
                                te.wait_ge(s_shS.h, s_shS.n)
                                te.wait_ge(s_scS.h, s_scS.n)
                                for jt in range(JT):
                                    if jt >= 2:
                                        te.wait_ge(s_evK.h, b_evK + jt - 1)
                                    s0 = C0 + iq * IQ - jt * 128 - 127
                                    g0 = (jt % 2) * 2
                                    w1k = min(512, WK)
                                    ins = te.matmul(out=bank(g0, w1k),
                                              lhsT=kT[hb:hb + 64,
                                                      hcol + jt * 128:hcol + (jt + 1) * 128],
                                              rhs=deT[hb:hb + 64, s0:s0 + w1k],
                                              start=True, stop=True)
                                    if WK > 512:
                                        ins = te.matmul(out=bank(g0 + 1, WK - 512),
                                              lhsT=kT[hb:hb + 64,
                                                      hcol + jt * 128:hcol + (jt + 1) * 128],
                                              rhs=deT[hb:hb + 64, s0 + 512:s0 + WK],
                                              start=True, stop=True)
                                    ins.then_inc(*s_mmK.inc())

                            @blk.vector
                            def _(v):
                                for jt in range(JT):
                                    v.wait_ge(s_mmK.h, b_mmK + jt + 1)
                                    if jt >= 2:
                                        v.wait_ge(s_shK.h, b_shK + 16 * (jt - 1))
                                    g0 = (jt % 2) * 2
                                    v.tensor_scalar(
                                        kd_sb[:, (jt % 2) * WK:(jt % 2) * WK + WK],
                                        PS[:, g0 * 512:g0 * 512 + WK],
                                        0.125, None, OP.mult).then_inc(*s_evK.inc())

                            @blk.scalar
                            def _(sc):
                                for jt in range(JT):
                                    sc.wait_ge(s_evK.h, b_evK + jt + 1)
                                    diag = AP(tensor=kd_sb, offset=(jt % 2) * WK + 127,
                                              ap=[[2 * WK - 1, 128], [1, IQ]])
                                    sc.dma_start(out=relkT[:, jt * IQ:(jt + 1) * IQ],
                                                 in_=diag).then_inc(*s_shK.inc(16))

                            G4 = 4 if JT % 4 == 0 else JT
                            for itl in range(IQC):
                                it = iq * IQC + itl
                                b_mmS, b_evS, b_shS = s_mmS.n, s_evS.n, s_shS.n
                                b_scS, b_evZ = s_scS.n, s_evZ.n
                                b_mmX, b_evX = s_mmX.n, s_evX.n
                                git = b_evZ
                                zb = (git % 2) * NJB

                                @blk.tensor
                                def _(te):
                                    te.wait_ge(s_shK.h, b_shK + 16 * JT)
                                    for jb in range(NJB):
                                        g0 = (jb % 2) * 2
                                        if jb >= 2:
                                            te.wait_ge(s_evS.h, b_evS + jb - 1)
                                        s0q = C0 + jb * JB - it * 128 - 127
                                        w1q = min(512, WQ)
                                        ins = te.matmul(out=bank(g0, w1q),
                                                  lhsT=qT[hb:hb + 64,
                                                          hcol + it * 128:hcol + (it + 1) * 128],
                                                  rhs=derT[hb:hb + 64, s0q:s0q + w1q],
                                                  start=True, stop=True)
                                        if WQ > 512:
                                            ins = te.matmul(out=bank(g0 + 1, WQ - 512),
                                                  lhsT=qT[hb:hb + 64,
                                                          hcol + it * 128:hcol + (it + 1) * 128],
                                                  rhs=derT[hb:hb + 64, s0q + 512:s0q + WQ],
                                                  start=True, stop=True)
                                        ins.then_inc(*s_mmS.inc())
                                        bs = 4 + jb % 2
                                        if jb >= 2:
                                            te.wait_ge(s_scS.h, b_scS + jb - 1)
                                        te.wait_ge(s_shS.h, b_shS + 16 * (jb + 1))
                                        te.matmul(out=bank(bs, JB),
                                                  lhsT=qT[hb:hb + 64,
                                                          hcol + it * 128:hcol + (it + 1) * 128],
                                                  rhs=kT[hb:hb + 64,
                                                         hcol + jb * JB:hcol + (jb + 1) * JB],
                                                  start=True, stop=False)
                                        for j2 in range(JB // 128):
                                            jt = jb * (JB // 128) + j2
                                            te.matmul(out=bank(bs, 128, j2 * 128),
                                                      lhsT=relkT[:, jt * IQ + itl * 128:
                                                                 jt * IQ + (itl + 1) * 128],
                                                      rhs=ident[:],
                                                      start=False, stop=False)
                                        te.matmul(out=bank(bs, JB), lhsT=ident[:],
                                                  rhs=rq_sb[:, (jb % 2) * JB:(jb % 2 + 1) * JB],
                                                  start=False, stop=True
                                                  ).then_inc(*s_mmS.inc())

                                @blk.vector
                                def _(v):
                                    for jb in range(NJB):
                                        v.wait_ge(s_mmS.h, b_mmS + 2 * jb + 1)
                                        if jb >= 2:
                                            v.wait_ge(s_shS.h, b_shS + 16 * (jb - 1))
                                        g0 = (jb % 2) * 2
                                        v.tensor_copy(
                                            out=qdb[:, (jb % 2) * WQ:(jb % 2) * WQ + WQ],
                                            in_=PS[:, g0 * 512:g0 * 512 + WQ]
                                            ).then_inc(*s_evS.inc())
                                    v.wait_ge(s_scS.h, b_scS + NJB + 1)
                                    v.reduce_sum(zp[:, 2 * NJB + (git % 2):2 * NJB + (git % 2) + 1],
                                                 zp[:, zb:zb + NJB], axis=AX)
                                    v.drain()
                                    v.reciprocal(zp[:, 2 * NJB + 2 + (git % 2):2 * NJB + 3 + (git % 2)],
                                                 zp[:, 2 * NJB + (git % 2):2 * NJB + (git % 2) + 1])
                                    v.drain().then_inc(*s_evZ.inc())

                                @blk.scalar
                                def _(sc):
                                    if git >= 2:
                                        sc.wait_ge(s_evZ.h, b_evZ - 1)
                                    sc.wait_ge(s_mmX.h, b_mmX)
                                    for jb in range(NJB):
                                        sc.wait_ge(s_evS.h, b_evS + jb + 1)
                                        diag = AP(tensor=qdb, offset=(jb % 2) * WQ + 127,
                                                  ap=[[2 * WQ - 1, 128], [1, JB]])
                                        sc.dma_start(
                                            out=rq_sb[:, (jb % 2) * JB:(jb % 2 + 1) * JB],
                                            in_=diag).then_inc(*s_shS.inc(16))
                                        sc.wait_ge(s_mmS.h, b_mmS + 2 * (jb + 1))
                                        bs = 4 + jb % 2
                                        sc.activation(out=Pb[:, jb * JB:(jb + 1) * JB],
                                                      in_=bank(bs, JB), func=AF.Exp,
                                                      bias=float(-EXPB), scale=1.0,
                                                      accum_out=zp[:, zb + jb:zb + jb + 1]
                                                      ).then_inc(*s_scS.inc())
                                    sc.drain().then_inc(*s_scS.inc())

                                @blk.tensor
                                def _(te):
                                    te.wait_ge(s_scS.h, b_scS + NJB + 1)
                                    for grp in range(JT // G4):
                                        if grp >= 1:
                                            te.wait_ge(s_evX.h, b_evX + grp)
                                        ins = None
                                        for j2 in range(G4):
                                            jt = grp * G4 + j2
                                            ins = te.matmul(out=bank(6, 128, j2 * 128),
                                                            lhsT=Pb[:, jt * 128:(jt + 1) * 128],
                                                            rhs=ident[:],
                                                            start=(j2 == 0), stop=(j2 == G4 - 1))
                                        ins.then_inc(*s_mmX.inc())
                                        te.wait_ge(s_evX.h, b_evX + grp + 1)
                                        ins = None
                                        for j2 in range(G4):
                                            jt = grp * G4 + j2
                                            ins = te.matmul(
                                                out=bank(7, 64),
                                                lhsT=PT[:, j2 * 128:(j2 + 1) * 128],
                                                rhs=v_sb[:, jt * D + h * 64:jt * D + (h + 1) * 64],
                                                start=(jt == 0), stop=(jt == JT - 1))
                                        ins.then_inc(*s_mmX.inc())

                                @blk.vector
                                def _(v):
                                    for grp in range(JT // G4):
                                        v.wait_ge(s_mmX.h, b_mmX + 2 * grp + 1)
                                        v.tensor_copy(out=PT[:, 0:G4 * 128], in_=bank(6, G4 * 128)
                                                      ).then_inc(*s_evX.inc())
                                    v.wait_ge(s_mmX.h, b_mmX + 2 * (JT // G4))
                                    v.wait_ge(s_evZ.h, b_evZ + 1)
                                    v.tensor_scalar(
                                        ctx_seq[:, it * D + h * 64:it * D + (h + 1) * 64],
                                        bank(7, 64),
                                        zp[:, 2 * NJB + 2 + (git % 2):2 * NJB + 3 + (git % 2)],
                                        None, OP.mult).then_inc(*s_evX.inc())

                    @blk.sync
                    def _(sy):
                        sy.wait_ge(s_evX.h, s_evX.n)
                        sy.dma_start(
                            out=ctx_dram[sq_ * S:(sq_ + 1) * S]
                            .rearrange("(t p) d -> p t d", p=128),
                            in_=ctx_seq[:].rearrange("p (t d) -> p t d", d=D)
                            ).then_inc(*s_xo.inc(16))
                    barrier()

        LVL = p.get("LVL", 3)
        emit_ffn("f1", x_in, None, False, False)
        if DBG:
            @blk.sync
            def _(sy):
                sy.dma_start(out=dbg_y.rearrange("p (t d) -> p t d", d=D),
                             in_=y_dram.rearrange("(t p) d -> p t d", p=128)
                             ).then_inc(*s_xo.inc(16))
                sy.dma_start(out=dbg_wtk, in_=wtk_sb[:]).then_inc(*s_xo.inc(16))
                sy.dma_start(out=dbg_dtk, in_=dtk_sb[:]).then_inc(*s_xo.inc(16))
                sy.dma_start(out=dbg_dtk2, in_=dtk2_sb[:]).then_inc(*s_xo.inc(16))
                sy.dma_start(out=dbg_tok,
                             in_=tok_dram.rearrange("(q s) o -> q (s o)", q=128)
                             ).then_inc(*s_xo.inc(16))
                sy.dma_start(out=dbg_po, in_=po_dram).then_inc(*s_xo.inc(16))

            barrier()
        if LVL >= 2:
            emit_attn()
        if DBG and LVL >= 2:
            @blk.sync
            def _(sy):
                sy.dma_start(out=dbg_ctx.rearrange("p (t d) -> p t d", d=D),
                             in_=ctx_dram.rearrange("(t p) d -> p t d", p=128)
                             ).then_inc(*s_xo.inc(16))
            barrier()
        if LVL >= 3:
            emit_ffn("f2", x_out, ctx_dram, True, False)
            barrier()
            emit_ln_pass("fin", x_out, 0, NT, 0, y_dram, True, None, 0,
                         ln=False, combine=True)

        @blk.sync
        def _(sy):
            sy.wait_ge(s_xo.h, s_xo.n)

    es.close()
    return nc


def _prep_inputs(inp, p, core):
    p = dict(p)
    p.setdefault("CAP", p["B_loc"] * p["S"])
    S, D = p["S"], p["D"]
    DC = D // 128
    B_loc = p["B_loc"]
    T = B_loc * S
    x = np.ascontiguousarray(
        inp["L0_attn_hidden_states"][core * B_loc:(core + 1) * B_loc]
    ).reshape(T, D)
    de = np.asarray(inp["L0_attn_dist_emb"])
    deT = np.ascontiguousarray(de.T)
    derT = np.ascontiguousarray(de[::-1].T)
    f32 = np.float32
    return {
        "x_in": np.asarray(x, f32),
        "WqT": np.ascontiguousarray(np.asarray(inp["L0_attn_Wq"], f32).T),
        "WkT": np.ascontiguousarray(np.asarray(inp["L0_attn_Wk"], f32).T),
        "WvT": np.ascontiguousarray(np.asarray(inp["L0_attn_Wv"], f32).T),
        "bqT": np.ascontiguousarray(np.asarray(inp["L0_attn_bq"], f32).reshape(DC, 128).T),
        "bkT": np.ascontiguousarray(np.asarray(inp["L0_attn_bk"], f32).reshape(DC, 128).T),
        "bv_rep": np.ascontiguousarray(
            np.broadcast_to(np.asarray(inp["L0_attn_bv"], f32), (128, D))),
        "deT_d": np.ascontiguousarray(np.concatenate([deT, deT], axis=0), f32),
        "derT_d": np.ascontiguousarray(np.concatenate([derT, derT], axis=0), f32),
        "Wg_in": np.asarray(inp["L0_ffn_Wg"], f32),
        "W1_in": np.asarray(inp["L0_ffn_W1"], f32),
        "W2_in": np.asarray(inp["L0_ffn_W2"], f32),
        "W3_in": np.asarray(inp["L0_ffn_W3"], f32),
        "Ws1_in": np.asarray(inp["L0_ffn_Ws1"], f32),
        "Ws2_in": np.asarray(inp["L0_ffn_Ws2"], f32),
        "Ws3_in": np.asarray(inp["L0_ffn_Ws3"], f32),
        "ident_in": np.eye(128, dtype=ml_dtypes.bfloat16),
        "identf": np.eye(128, dtype=np.float32),
        "iota_tok": np.ascontiguousarray(
            (np.arange(T // 128)[None, :] * 128 + np.arange(128)[:, None])
            .astype(np.uint32)),
        "tok_pad": np.full((128, (p["E"] * p["CAP"]) // 128 + 1), T,
                           dtype=np.uint32),
        "eoff": np.ascontiguousarray(
            (np.arange(32, dtype=np.float32) * p["CAP"] - 1.0).reshape(32, 1)),
    }


def kernel(**inputs):
    p = FULL
    nc = build(p)
    in_maps = [_prep_inputs(inputs, p, c) for c in range(N_CORES)]
    res = run_bass_kernel_spmd(nc, in_maps, core_ids=list(range(N_CORES)))
    out = np.concatenate(
        [r["x_out"].reshape(p["B_loc"], p["S"], p["D"]) for r in res.results], axis=0)
    return np.ascontiguousarray(out, dtype=np.float32)


# version 03
# speedup vs baseline: 1.5543x; 1.2396x over previous
from contextlib import ExitStack

import numpy as np
import ml_dtypes

import concourse.bass as bass
import concourse.mybir as mybir
from concourse.bass_types import AP
from concourse.bass_core_rust import IndirectOffsetOnAxis
from concourse.bass_utils import run_bass_kernel_spmd

F32 = mybir.dt.float32
BF16 = mybir.dt.bfloat16
U32 = mybir.dt.uint32
I32 = mybir.dt.int32
AF = mybir.ActivationFunctionType
OP = mybir.AluOpType
AX = mybir.AxisListType.X

N_CORES = 8

FULL = dict(S=2048, D=1024, NH=16, E=32, H=512, Hs=1024, B_loc=2, MAXP=2048,
            JB=512, IQ=512, EXPB=4.0, CAP=768)


def build(p):
    S, D, E, H, Hs = p["S"], p["D"], p["E"], p["H"], p["Hs"]
    NH, B_loc, MAXP = p["NH"], p["B_loc"], p["MAXP"]
    JB, IQ, EXPB = p["JB"], p["IQ"], p["EXPB"]
    T = B_loc * S
    CAP = p.get("CAP", T)
    GCH = CAP // 128
    PB = min(512, CAP)
    while CAP % PB:
        PB -= 128
    NPB = CAP // PB
    DC, HC, HsC = D // 128, H // 128, Hs // 128
    NT, ST = T // 128, S // 128
    R, C0 = 2 * MAXP - 1, MAXP - 1
    NJB, JT, NIQ = S // JB, S // 128, S // IQ
    IQC = IQ // 128
    TB = 512 if T >= 512 else T
    NTB = T // TB
    TBC = TB // 128
    OB = min(512, D)
    SBW = min(512, S)
    NB2 = D // OB
    SB2 = S // SBW
    WQ = JB + 127
    WK = IQ + 127

    nc = bass.Bass("TRN2", target_bir_lowering=False, detect_race_conditions=False)
    es = ExitStack()

    def sbuf(name, shape, dt):
        return es.enter_context(nc.sbuf_tensor(name, shape, dt))

    all_sems = []

    class Sem:
        __slots__ = ("h", "n")

        def __init__(self, name):
            self.h = es.enter_context(nc.semaphore(name))
            self.n = 0
            all_sems.append(self)

        def inc(self, k=1):
            self.n += k
            return (self.h, k)

    dt_ = nc.dram_tensor
    x_in = dt_("x_in", [T, D], F32, kind="ExternalInput").ap()
    x_out = dt_("x_out", [T, D], F32, kind="ExternalOutput").ap()
    WqT_in = dt_("WqT", [D, D], F32, kind="ExternalInput").ap()
    WkT_in = dt_("WkT", [D, D], F32, kind="ExternalInput").ap()
    WvT_in = dt_("WvT", [D, D], F32, kind="ExternalInput").ap()
    bqT_in = dt_("bqT", [128, DC], F32, kind="ExternalInput").ap()
    bkT_in = dt_("bkT", [128, DC], F32, kind="ExternalInput").ap()
    bv_in = dt_("bv_rep", [128, D], F32, kind="ExternalInput").ap()
    deT_in = dt_("deT_d", [128, R], F32, kind="ExternalInput").ap()
    derT_in = dt_("derT_d", [128, R], F32, kind="ExternalInput").ap()
    Wg_in = dt_("Wg_in", [D, E], F32, kind="ExternalInput").ap()
    W1_in = dt_("W1_in", [E, D, H], F32, kind="ExternalInput").ap()
    W2_in = dt_("W2_in", [E, H, D], F32, kind="ExternalInput").ap()
    W3_in = dt_("W3_in", [E, D, H], F32, kind="ExternalInput").ap()
    Ws1_in = dt_("Ws1_in", [D, Hs], F32, kind="ExternalInput").ap()
    Ws2_in = dt_("Ws2_in", [Hs, D], F32, kind="ExternalInput").ap()
    Ws3_in = dt_("Ws3_in", [D, Hs], F32, kind="ExternalInput").ap()
    ident_in = dt_("ident_in", [128, 128], BF16, kind="ExternalInput").ap()
    y_dram = dt_("y_dram", [T, D], BF16, kind="Internal").ap()
    ctx_dram = dt_("ctx_dram", [T, D], BF16, kind="Internal").ap()
    xln_dram = dt_("xln_dram", [T + 128, D], BF16, kind="Internal").ap()
    po_dram = dt_("po_dram", [E * CAP + 128, D], BF16, kind="Internal").ap()
    WT = E * GCH + 1
    tok_dram = dt_("tok_dram", [128 * WT, 1], U32, kind="Internal").ap()
    iota_tok_in = dt_("iota_tok", [128, NT], U32, kind="ExternalInput").ap()
    tok_pad_in = dt_("tok_pad", [128, (E * CAP + 128) // 128], U32,
                     kind="ExternalInput").ap()
    eoff_in = dt_("eoff", [32, 1], F32, kind="ExternalInput").ap()
    identf_in = dt_("identf", [128, 128], F32, kind="ExternalInput").ap()
    DBG = p.get("DBG", False)
    if DBG:
        dbg_y = dt_("dbg_y", [128, NT * D], BF16, kind="ExternalOutput").ap()
        dbg_mw = dt_("dbg_mw", [128, NT * E], F32, kind="ExternalOutput").ap()
        dbg_ctx = dt_("dbg_ctx", [128, NT * D], BF16, kind="ExternalOutput").ap()
        dbg_xln = dt_("dbg_xln", [128, (D // 128) * T], BF16, kind="ExternalOutput").ap()
        dbg_wtk = dt_("dbg_wtk", [128, NT * 4], F32, kind="ExternalOutput").ap()
        dbg_dtk = dt_("dbg_dtk", [128, NT * 4], U32, kind="ExternalOutput").ap()
        dbg_dtk2 = dt_("dbg_dtk2", [128, NT * 4], U32, kind="ExternalOutput").ap()
        dbg_tok = dt_("dbg_tok", [128, WT], U32, kind="ExternalOutput").ap()
        dbg_po = dt_("dbg_po", [E * CAP + 128, D], BF16, kind="ExternalOutput").ap()
        dbg_toksb = dt_("dbg_toksb", [128, E * GCH], U32, kind="ExternalOutput").ap()
        dbg_xg = dt_("dbg_xg", [128, 2 * D], BF16, kind="ExternalOutput").ap()
        dbg_xe = dt_("dbg_xe", [128, (D // 128) * CAP], BF16, kind="ExternalOutput").ap()

    consts = sbuf("consts", [128, 8], F32)
    nc.const_aps.register(F32, 1.0, consts[:, 0:1])
    nc.const_aps.register(F32, 0.5, consts[:, 1:2])
    nc.const_aps.register(F32, 1e-5, consts[:, 2:3])
    nc.const_aps.register(F32, float(-EXPB), consts[:, 3:4])
    nc.const_aps.register(F32, 0.0, consts[:, 4:5])
    ident = sbuf("ident", [128, 128], BF16)
    deT = sbuf("deT", [128, R], BF16)
    derT = sbuf("derT", [128, R], BF16)
    wg_sb = sbuf("wg_sb", [128, DC * E], BF16)
    bq_sb = sbuf("bq_sb", [128, DC], F32)
    bk_sb = sbuf("bk_sb", [128, DC], F32)
    wtk_sb = sbuf("wtk_sb", [128, NT * 4], F32)
    dtk_sb = sbuf("dtk_sb", [128, NT * 4], U32)
    dtk2_sb = sbuf("dtk2_sb", [128, NT * 4], U32)
    scri = sbuf("scri", [128, 8], I32)
    iota_tok = sbuf("iota_tok_sb", [128, NT], U32)
    eoff_sb = sbuf("eoff_sb", [32, 1], F32)
    identf = sbuf("identf_sb", [128, 128], F32)
    zrow = sbuf("zrow", [128, D], BF16)

    PS = es.enter_context(nc.psum_tensor("PS", [128, 4096], F32))

    def bank(i, w=512, off=0):
        return PS[:, i * 512 + off:i * 512 + off + w]

    s_boot, s_cboot = Sem("s_boot"), Sem("s_cboot")
    s_x, s_w = Sem("s_x"), Sem("s_w")
    s_fold, s_lnA, s_scA, s_lnN = Sem("s_fold"), Sem("s_lnA"), Sem("s_scA"), Sem("s_lnN")
    s_mmT, s_evT, s_xo = Sem("s_mmT"), Sem("s_evT"), Sem("s_xo")
    s_mmG, s_scG, s_evG = Sem("s_mmG"), Sem("s_scG"), Sem("s_evG")
    s_mmH, s_scH, s_evH = Sem("s_mmH"), Sem("s_scH"), Sem("s_evH")
    s_mmO, s_evO, s_wf = Sem("s_mmO"), Sem("s_evO"), Sem("s_wf")
    s_mmP, s_evP = Sem("s_mmP"), Sem("s_evP")
    s_mmK, s_evK, s_shK = Sem("s_mmK"), Sem("s_evK"), Sem("s_shK")
    s_mmS, s_evS, s_shS, s_scS = Sem("s_mmS"), Sem("s_evS"), Sem("s_shS"), Sem("s_scS")
    s_evZ, s_mmX, s_evX = Sem("s_evZ"), Sem("s_mmX"), Sem("s_evX")
    s_mmU, s_zpD, s_ptc = Sem("s_mmU"), Sem("s_zpD"), Sem("s_ptc")
    s_xst, s_rt, s_tok = Sem("s_xst"), Sem("s_rt"), Sem("s_tok")
    s_pog, s_po, s_xg = Sem("s_pog"), Sem("s_po"), Sem("s_xg")

    with nc.Block() as blk:

        def barrier():
            snap = [(s.h, s.n) for s in all_sems if s.n > 0]

            def w(eng):
                for h, n in snap:
                    eng.wait_ge(h, n)
            blk.sync(w)
            blk.vector(w)
            blk.scalar(w)
            blk.tensor(w)
            blk.gpsimd(w)

        @blk.vector
        def _(v):
            v.memset(consts[:, 0:1], 1.0)
            v.memset(consts[:, 1:2], 0.5)
            v.memset(consts[:, 2:3], 1e-5)
            v.memset(consts[:, 3:4], float(-EXPB))
            v.memset(consts[:, 4:5], 0.0).then_inc(*s_cboot.inc())

        @blk.sync
        def _(sy):
            sy.dma_start(out=ident[:], in_=ident_in).then_inc(*s_boot.inc(16))
            sy.dma_start(out=bq_sb[:], in_=bqT_in).then_inc(*s_boot.inc(16))
            sy.dma_start(out=bk_sb[:], in_=bkT_in).then_inc(*s_boot.inc(16))
            sy.dma_start(out=iota_tok[:], in_=iota_tok_in).then_inc(*s_boot.inc(16))
            sy.dma_start(out=eoff_sb[:], in_=eoff_in).then_inc(*s_boot.inc(16))
            sy.dma_start(out=identf[:], in_=identf_in).then_inc(*s_boot.inc(16))

        @blk.vector
        def _(v):
            v.memset(zrow[:], 0.0).then_inc(*s_cboot.inc())

        @blk.sync
        def _(sy):
            sy.wait_ge(s_cboot.h, s_cboot.n)
            sy.dma_start(out=xln_dram[T:T + 128, :], in_=zrow[:]
                         ).then_inc(*s_boot.inc(16))
            sy.dma_start(out=po_dram[E * CAP:E * CAP + 128, :], in_=zrow[:]
                         ).then_inc(*s_boot.inc(16))

        @blk.gpsimd
        def _(g):
            g.dma_start(out=deT[:], in_=deT_in).then_inc(*s_w.inc(16))
            g.dma_start(out=derT[:], in_=derT_in).then_inc(*s_w.inc(16))
            g.dma_start(out=wg_sb[:].rearrange("p (c e) -> p c e", c=DC),
                        in_=Wg_in.rearrange("(c p) e -> p c e", p=128)
                        ).then_inc(*s_w.inc(16))

        def emit_ln_pass(tag, src, row0, ntiles, tile0, fold_dram, writeback,
                         xlnT, span, ln=True, alloc=None, stage_xln=False,
                         combine=False):
            alloc = alloc or sbuf
            xt = alloc(tag + "_xt", [128, 2 * D], F32)
            xw = alloc(tag + "_xw", [128, 2 * D], F32)
            stat = alloc(tag + "_stat", [128, 8], F32)
            sq = alloc(tag + "_sq", [128, D], BF16)
            xln = alloc(tag + "_xln", [128, 2 * D], BF16) if ln else None
            have_fold = fold_dram is not None
            fbuf = alloc(tag + "_fb", [128, 2 * D], BF16) if have_fold else None
            pob = alloc(tag + "_po", [128, 4 * D], BF16) if combine else None
            b_x, b_fold, b_lnA, b_scA = s_x.n, s_fold.n, s_lnA.n, s_scA.n
            b_lnN, b_mmT, b_evT, b_xo = s_lnN.n, s_mmT.n, s_evT.n, s_xo.n
            b_xst, b_pog = s_xst.n, s_pog.n
            tb = max(1, (DC * 128) // 512)
            per_t = 32 if have_fold else 16
            assert ln or (have_fold and writeback)

            def emit_evac(v, u):
                v.wait_ge(s_mmT.h, b_mmT + u + 1)
                g0 = (u % 2) * tb
                out_ap = AP(tensor=xlnT, offset=u * 128,
                            ap=[[xlnT.shape[1], 128], [span, DC], [1, 128]])
                in_ap = AP(tensor=PS, offset=g0 * 512,
                           ap=[[4096, 128], [128, DC], [1, 128]])
                v.tensor_copy(out=out_ap, in_=in_ap).then_inc(*s_evT.inc())

            @blk.sync
            def _(sy):
                for t in range(ntiles):
                    if t >= 2:
                        if ln:
                            sy.wait_ge(s_lnN.h, b_lnN + (t - 1))
                        else:
                            sy.wait_ge(s_fold.h, b_fold + (t - 1))
                    sy.dma_start(out=xt[:, (t % 2) * D:(t % 2 + 1) * D],
                                 in_=src[row0 + t * 128:row0 + (t + 1) * 128, :]
                                 ).then_inc(*s_x.inc(16))
                    if have_fold:
                        sy.dma_start(out=fbuf[:, (t % 2) * D:(t % 2 + 1) * D],
                                     in_=fold_dram[row0 + t * 128:row0 + (t + 1) * 128, :]
                                     ).then_inc(*s_x.inc(16))
                    if writeback:
                        sy.wait_ge(s_fold.h, b_fold + t + 1)
                        sy.dma_start(out=x_out[row0 + t * 128:row0 + (t + 1) * 128, :],
                                     in_=xw[:, (t % 2) * D:(t % 2 + 1) * D]
                                     ).then_inc(*s_xo.inc(16))
                    if stage_xln:
                        sy.wait_ge(s_lnN.h, b_lnN + t + 1)
                        sy.dma_start(out=xln_dram[row0 + t * 128:row0 + (t + 1) * 128, :],
                                     in_=xln[:, (t % 2) * D:(t % 2 + 1) * D]
                                     ).then_inc(*s_xst.inc(16))

            if combine:
                @blk.gpsimd
                def _(g):
                    for t in range(ntiles):
                        if t >= 1:
                            g.wait_ge(s_fold.h, b_fold + t)
                        for k in range(4):
                            col = (tile0 + t) * 4 + k
                            dst = pob[:, k * D:(k + 1) * D]
                            g.indirect_dma_start(
                                out=dst.bitcast(F32), out_offset=None,
                                in_=po_dram.bitcast(F32),
                                in_offset=IndirectOffsetOnAxis(
                                    dtk_sb[:, col:col + 1], 0)
                                ).then_inc(*s_pog.inc(16))

            @blk.vector
            def _(v):
                v.wait_ge(s_cboot.h, s_cboot.n)
                for t in range(ntiles):
                    sl = slice((t % 2) * D, (t % 2 + 1) * D)
                    v.wait_ge(s_x.h, b_x + per_t * (t + 1))
                    cur = xt[:, sl]
                    if have_fold:
                        if t >= 2:
                            if ln:
                                v.wait_ge(s_scA.h, b_scA + 2 * (t - 2) + 1)
                            if writeback:
                                v.wait_ge(s_xo.h, b_xo + 16 * (t - 1))
                        if combine:
                            v.tensor_add(xw[:, sl], cur, fbuf[:, sl])
                            v.wait_ge(s_pog.h, b_pog + 64 * (t + 1))
                            ins = None
                            for k in range(4):
                                col = (tile0 + t) * 4 + k
                                pk = pob[:, k * D:(k + 1) * D]
                                ins = v.scalar_tensor_tensor(
                                    out=xw[:, sl], in0=pk,
                                    scalar=wtk_sb[:, col:col + 1],
                                    in1=xw[:, sl], op0=OP.mult, op1=OP.add)
                            ins.then_inc(*s_fold.inc())
                        else:
                            v.tensor_add(xw[:, sl], cur, fbuf[:, sl]
                                         ).then_inc(*s_fold.inc())
                        cur = xw[:, sl]
                    if not ln:
                        continue
                    v.reduce_sum(stat[:, 0:1], cur, axis=AX)
                    v.drain()
                    v.tensor_scalar(stat[:, 1:2], stat[:, 0:1], -1.0 / D, None, OP.mult)
                    v.wait_ge(s_scA.h, b_scA + 2 * t + 1)
                    v.tensor_scalar(stat[:, 2:3], stat[:, 4:5], 1.0 / D, None, OP.mult)
                    v.drain()
                    v.tensor_mul(stat[:, 3:4], stat[:, 1:2], stat[:, 1:2])
                    v.drain()
                    v.tensor_sub(stat[:, 2:3], stat[:, 2:3], stat[:, 3:4])
                    v.drain()
                    v.tensor_scalar(stat[:, 2:3], stat[:, 2:3], 1e-5, None, OP.add)
                    v.drain().then_inc(*s_lnA.inc())
                    v.wait_ge(s_scA.h, b_scA + 2 * t + 2)
                    v.reciprocal(stat[:, 6:7], stat[:, 5:6])
                    v.drain()
                    if t >= 2:
                        v.wait_ge(s_mmT.h, b_mmT + (t - 1))
                        if stage_xln:
                            v.wait_ge(s_xst.h, b_xst + 16 * (t - 1))
                    v.tensor_scalar(xln[:, sl], cur, stat[:, 1:2], stat[:, 6:7],
                                    OP.add, OP.mult).then_inc(*s_lnN.inc())
                    if t >= 1:
                        emit_evac(v, t - 1)
                if ln:
                    emit_evac(v, ntiles - 1)

            if not ln:
                return

            @blk.scalar
            def _(sc):
                sc.wait_ge(s_cboot.h, s_cboot.n)
                for t in range(ntiles):
                    sl = slice((t % 2) * D, (t % 2 + 1) * D)
                    src_t = xw if have_fold else xt
                    if have_fold:
                        sc.wait_ge(s_fold.h, b_fold + t + 1)
                    else:
                        sc.wait_ge(s_x.h, b_x + per_t * (t + 1))
                    sc.activation(out=sq[:], in_=src_t[:, sl], func=AF.Square,
                                  accum_out=stat[:, 4:5])
                    sc.drain().then_inc(*s_scA.inc())
                    sc.wait_ge(s_lnA.h, b_lnA + t + 1)
                    sc.activation(out=stat[:, 5:6], in_=stat[:, 2:3], func=AF.Sqrt)
                    sc.drain().then_inc(*s_scA.inc())

            @blk.tensor
            def _(te):
                te.wait_ge(s_boot.h, 16)
                for t in range(ntiles):
                    te.wait_ge(s_lnN.h, b_lnN + t + 1)
                    if t >= 2:
                        te.wait_ge(s_evT.h, b_evT + (t - 1))
                    g0 = (t % 2) * tb
                    sl0 = (t % 2) * D
                    ins = None
                    for c in range(DC):
                        ins = te.matmul(
                            out=bank(g0 + (c * 128) // 512, 128, (c * 128) % 512),
                            lhsT=xln[:, sl0 + c * 128:sl0 + (c + 1) * 128],
                            rhs=ident[:],
                            start=((c * 128) % 512 == 0),
                            stop=((c * 128) % 512 == 512 - 128 or c == DC - 1))
                    ins.then_inc(*s_mmT.inc())


        def emit_gating(tag, xlnT, span, ntiles, tile0, alloc=None, balloc=None,
                        waits=()):
            alloc = alloc or sbuf
            balloc = balloc or alloc
            pg = balloc(tag + "_pg", [128, 4 * E], F32)
            p_all = balloc(tag + "_pa", [128, ntiles * E], F32)
            gz = balloc(tag + "_gz", [128, 16], F32)
            t8 = balloc(tag + "_t8", [128, 8], F32)
            mk = balloc(tag + "_mk", [128, 4 * E], F32)
            mT = balloc(tag + "_mT", [32, ntiles * 128], BF16)
            cum = balloc(tag + "_cum", [32, ntiles * 128], F32)
            zer = balloc(tag + "_zer", [32, ntiles * 128], BF16)
            dT = balloc(tag + "_dT", [128, 32], F32)
            scr = balloc(tag + "_scr", [128, E + 8], F32)
            b_mmG, b_scG, b_evG, b_rt = s_mmG.n, s_scG.n, s_evG.n, s_rt.n
            GE = 4
            ngrp = (ntiles + GE - 1) // GE
            DUMMY = float(E * CAP)
            W3_ = ntiles * 128

            @blk.sync
            def _(sy):
                sy.wait_ge(s_xg.h, s_xg.n)
                sy.dma_start(out=tok_dram.rearrange("(q s) o -> q (s o)", q=128),
                             in_=tok_pad_in).then_inc(*s_tok.inc(16))

            @blk.tensor
            def _(te):
                te.wait_ge(s_w.h, 48)
                te.wait_ge(s_evT.h, s_evT.n)
                for g in range(ngrp):
                    tl = list(range(g * GE, min((g + 1) * GE, ntiles)))
                    if g >= 2:
                        te.wait_ge(s_scG.h, b_scG + (g - 1))
                    bk6 = 6 + (g % 2)
                    ins = None
                    for t in tl:
                        for c in range(DC):
                            ins = te.matmul(
                                out=bank(bk6, E, (t % GE) * E),
                                lhsT=xlnT[:, c * span + t * 128:c * span + (t + 1) * 128],
                                rhs=wg_sb[:, c * E:(c + 1) * E],
                                start=(t == tl[0] and c == 0),
                                stop=(t == tl[-1] and c == DC - 1))
                    ins.then_inc(*s_mmG.inc())

            @blk.vector
            def _(v):
                for sm, val in waits:
                    v.wait_ge(sm.h, val)
                v.memset(zer[:], 0.0)
                for g in range(ngrp):
                    tl = list(range(g * GE, min((g + 1) * GE, ntiles)))
                    bk6 = 6 + (g % 2)
                    v.wait_ge(s_mmG.h, b_mmG + g + 1)
                    for t in tl:
                        j = t % GE
                        v.reduce_max(gz[:, j:j + 1], bank(bk6, E, j * E),
                                     axis=AX, negate=True)
                    v.drain().then_inc(*s_evG.inc())
                    v.wait_ge(s_scG.h, b_scG + g + 1)
                    for t in tl:
                        j = t % GE
                        v.reciprocal(gz[:, 4 + j:5 + j], gz[:, 4 + j:5 + j])
                    v.drain()
                    for t in tl:
                        j = t % GE
                        gt = tile0 + t
                        v.tensor_scalar(p_all[:, t * E:(t + 1) * E],
                                        pg[:, j * E:(j + 1) * E],
                                        gz[:, 4 + j:5 + j], None, OP.mult)
                        v.drain()
                        v.max(out=t8[:], in_=p_all[:, t * E:(t + 1) * E])
                        v.drain()
                        v.tensor_copy(wtk_sb[:, gt * 4:gt * 4 + 4], t8[:, 0:4])
                    v.drain().then_inc(*s_evG.inc())

            @blk.scalar
            def _(sc):
                for g in range(ngrp):
                    tl = list(range(g * GE, min((g + 1) * GE, ntiles)))
                    bk6 = 6 + (g % 2)
                    sc.wait_ge(s_evG.h, b_evG + 2 * g + 1)
                    for t in tl:
                        j = t % GE
                        sc.activation(out=pg[:, j * E:(j + 1) * E],
                                      in_=bank(bk6, E, j * E), func=AF.Exp,
                                      bias=gz[:, j:j + 1], scale=1.0,
                                      accum_out=gz[:, 4 + j:5 + j])
                    sc.drain().then_inc(*s_scG.inc())

            vA_end = s_evG.n
            b1_ev = s_evG.n
            b1_mm = s_mmG.n

            @blk.vector
            def _(v):
                nonlocal_marker = None
                for g in range(ngrp):
                    tl = list(range(g * GE, min((g + 1) * GE, ntiles)))
                    for t in tl:
                        j = t % GE
                        gt = tile0 + t
                        if g >= 1:
                            v.wait_ge(s_mmG.h, b1_mm + g)
                        v.tensor_scalar(mk[:, j * E:(j + 1) * E],
                                        p_all[:, t * E:(t + 1) * E],
                                        wtk_sb[:, gt * 4 + 3:gt * 4 + 4],
                                        None, OP.is_ge)
                    v.drain().then_inc(*s_evG.inc())
                    v.wait_ge(s_mmG.h, b1_mm + g + 1)
                    v.tensor_copy(out=mT[0:E, g * GE * 128:(g * GE + len(tl)) * 128],
                                  in_=PS[0:E, 0:len(tl) * 128]).then_inc(*s_evG.inc())

            @blk.tensor
            def _(te):
                for g in range(ngrp):
                    tl = list(range(g * GE, min((g + 1) * GE, ntiles)))
                    te.wait_ge(s_evG.h, b1_ev + 2 * g + 1)
                    if g >= 1:
                        te.wait_ge(s_evG.h, b1_ev + 2 * g)
                    ins = None
                    for t in tl:
                        j = t % GE
                        ins = te.matmul(out=PS[0:E, j * 128:(j + 1) * 128],
                                        lhsT=mk[:, j * E:(j + 1) * E],
                                        rhs=identf[:],
                                        start=(j == 0), stop=(t == tl[-1]))
                    ins.then_inc(*s_mmG.inc())

            b2_ev = s_evG.n

            @blk.vector
            def _(v):
                v.wait_ge(s_evG.h, b2_ev)
                v.tensor_tensor_scan(out=cum[0:E, :], data0=mT[0:E, :],
                                     data1=zer[0:E, :], initial=0.0,
                                     op0=OP.add, op1=OP.add)
                v.drain()
                v.tensor_scalar(mT[0:E, :], cum[0:E, :], float(CAP), None, OP.is_le)
                v.tensor_scalar(cum[0:E, :], cum[0:E, :], eoff_sb[0:E, :],
                                None, OP.add)
                v.tensor_scalar(cum[0:E, :], cum[0:E, :], -DUMMY, None, OP.add)
                v.tensor_mul(cum[0:E, :], cum[0:E, :], mT[0:E, :])
                v.tensor_scalar(cum[0:E, :], cum[0:E, :], DUMMY, None, OP.add)
                v.drain().then_inc(*s_evG.inc())

            b3_ev = s_evG.n
            b3_mm = s_mmG.n

            @blk.tensor
            def _(te):
                te.wait_ge(s_evG.h, b3_ev)
                for t in range(ntiles):
                    if t >= 1:
                        te.wait_ge(s_evG.h, b3_ev + t)
                    te.matmul(out=PS[:, 512:512 + 32],
                              lhsT=cum[0:E, t * 128:(t + 1) * 128],
                              rhs=identf[0:E, 0:32],
                              start=True, stop=True).then_inc(*s_mmG.inc())

            @blk.vector
            def _(v):
                for t in range(ntiles):
                    gt = tile0 + t
                    v.wait_ge(s_mmG.h, b3_mm + t + 1)
                    v.tensor_copy(out=dT[:], in_=PS[:, 512:512 + 32])
                    v.drain()
                    ins = None
                    for k in range(4):
                        v.tensor_scalar(scr[:, 0:E], p_all[:, t * E:(t + 1) * E],
                                        wtk_sb[:, gt * 4 + k:gt * 4 + k + 1],
                                        None, OP.is_equal)
                        v.drain()
                        v.scalar_tensor_tensor(
                            out=scr[:, 0:E], in0=scr[:, 0:E], scalar=1.0,
                            in1=dT[:, 0:E], op0=OP.mult, op1=OP.mult,
                            accum_out=scr[:, E + k:E + k + 1])
                        v.drain()
                        v.tensor_copy(dtk_sb[:, gt * 4 + k:gt * 4 + k + 1],
                                      scr[:, E + k:E + k + 1])
                        v.tensor_copy(scri[:, 0:1], scr[:, E + k:E + k + 1])
                        v.drain()
                        v.tensor_scalar(scri[:, 1:2], scri[:, 0:1], 7, None,
                                        OP.logical_shift_right)
                        v.tensor_scalar(scri[:, 2:3], scri[:, 0:1], 127, None,
                                        OP.bitwise_and)
                        v.drain()
                        v.tensor_scalar(scri[:, 2:3], scri[:, 2:3], WT, None,
                                        OP.mult)
                        v.drain()
                        v.tensor_add(scri[:, 3:4], scri[:, 2:3], scri[:, 1:2])
                        v.drain()
                        v.tensor_copy(dtk2_sb[:, gt * 4 + k:gt * 4 + k + 1],
                                      scri[:, 3:4])
                    v.drain().then_inc(*s_evG.inc())

            @blk.gpsimd
            def _(g):
                g.wait_ge(s_tok.h, s_tok.n)
                for t in range(ntiles):
                    gt = tile0 + t
                    g.wait_ge(s_evG.h, b3_ev + t + 1)
                    for k in range(4):
                        g.indirect_dma_start(
                            out=tok_dram, out_offset=IndirectOffsetOnAxis(
                                dtk2_sb[:, gt * 4 + k:gt * 4 + k + 1], 0),
                            in_=iota_tok[:, gt:gt + 1], in_offset=None
                            ).then_inc(*s_rt.inc(16))

        def emit_swiglu(tag, xlnT, span, w1sb, w3sb, w2sb, hxc, y_sb, hg, hgl,
                        wait_w=None, extra_waits=()):
            b_mmH, b_scH, b_evH = s_mmH.n, s_scH.n, s_evH.n
            b_mmO, b_evO = s_mmO.n, s_evO.n
            HW = hxc * 128

            @blk.tensor
            def _(te):
                if wait_w is not None:
                    te.wait_ge(s_w.h, wait_w)
                te.wait_ge(s_evT.h, s_evT.n)
                for sm, val in extra_waits:
                    te.wait_ge(sm.h, val)
                for b in range(NTB):
                    col = b * TB
                    for hc in range(hxc):
                        idx = b * hxc + hc
                        if idx >= 2:
                            te.wait_ge(s_evH.h, b_evH + idx - 1)
                        bh1, bh3 = idx % 2, 2 + idx % 2
                        for c in range(DC):
                            te.matmul(out=bank(bh1, TB),
                                      lhsT=w1sb[:, c * HW + hc * 128:c * HW + (hc + 1) * 128],
                                      rhs=xlnT[:, c * span + col:c * span + col + TB],
                                      start=(c == 0), stop=(c == DC - 1))
                        ins = None
                        for c in range(DC):
                            ins = te.matmul(out=bank(bh3, TB),
                                            lhsT=w3sb[:, c * HW + hc * 128:c * HW + (hc + 1) * 128],
                                            rhs=xlnT[:, c * span + col:c * span + col + TB],
                                            start=(c == 0), stop=(c == DC - 1))
                        ins.then_inc(*s_mmH.inc())
                    te.wait_ge(s_evH.h, b_evH + (b + 1) * hxc)
                    for tc in range(TBC):
                        for n in range(NB2):
                            oidx = b * TBC * NB2 + tc * NB2 + n
                            bo = 4 + oidx % 2
                            if oidx >= 2:
                                te.wait_ge(s_evO.h, b_evO + oidx - 1)
                            ins = None
                            for k in range(hxc):
                                ins = te.matmul(
                                    out=bank(bo, OB),
                                    lhsT=hgl[:, k * TB + tc * 128:k * TB + (tc + 1) * 128],
                                    rhs=w2sb[:, k * D + n * OB:k * D + (n + 1) * OB],
                                    start=(k == 0), stop=(k == hxc - 1))
                            ins.then_inc(*s_mmO.inc())

            @blk.scalar
            def _(sc):
                for b in range(NTB):
                    for hc in range(hxc):
                        idx = b * hxc + hc
                        sc.wait_ge(s_mmH.h, b_mmH + idx + 1)
                        sc.activation(out=hg[:, (idx % 2) * TB:(idx % 2) * TB + TB],
                                      in_=bank(idx % 2, TB), func=AF.Silu
                                      ).then_inc(*s_scH.inc())

            @blk.vector
            def _(v):
                for b in range(NTB):
                    for hc in range(hxc):
                        idx = b * hxc + hc
                        v.wait_ge(s_scH.h, b_scH + idx + 1)
                        v.tensor_mul(hgl[:, hc * TB:hc * TB + TB],
                                     bank(2 + idx % 2, TB),
                                     hg[:, (idx % 2) * TB:(idx % 2) * TB + TB]
                                     ).then_inc(*s_evH.inc())
                    for tc in range(TBC):
                        for n in range(NB2):
                            oidx = b * TBC * NB2 + tc * NB2 + n
                            bo = 4 + oidx % 2
                            v.wait_ge(s_mmO.h, b_mmO + oidx + 1)
                            gt = b * TBC + tc
                            ysl = y_sb[:, gt * D + n * OB:gt * D + (n + 1) * OB]
                            v.tensor_scalar(ysl, bank(bo, OB), 0.5, None, OP.mult
                                            ).then_inc(*s_evO.inc())

        def emit_expert_routed(e, w1sb, w3sb, w2sb, tok_sb, xg, xeT, po_sb,
                               hg, hgl, wait_w, free_sem, b_tok, sweep,
                               prefetch=None):
            b_xg, b_mmT, b_evT = s_xg.n, s_mmT.n, s_evT.n
            b_mmH, b_scH, b_evH = s_mmH.n, s_scH.n, s_evH.n
            b_mmO, b_evO, b_po = s_mmO.n, s_evO.n, s_po.n
            PC = CAP // 128
            NHX = NPB * HC

            @blk.gpsimd
            def _(g):
                g.wait_ge(s_tok.h, b_tok)
                for gc in range(GCH):
                    gci = e * GCH + gc
                    if gci >= 2:
                        g.wait_ge(s_mmT.h, sweep["mmT"] + gci - 1)
                    g.indirect_dma_start(
                        out=xg[:, (gc % 2) * D:(gc % 2 + 1) * D].bitcast(F32),
                        out_offset=None,
                        in_=xln_dram.bitcast(F32),
                        in_offset=IndirectOffsetOnAxis(
                            tok_sb[:, e * GCH + gc:e * GCH + gc + 1], 0)
                        ).then_inc(*s_xg.inc(16))
                if prefetch is not None:
                    prefetch(g)

            @blk.tensor
            def _(te):
                te.wait_ge(s_w.h, wait_w)
                for gc in range(GCH):
                    gci = e * GCH + gc
                    te.wait_ge(s_xg.h, b_xg + 16 * (gc + 1))
                    if gci >= 1:
                        te.wait_ge(s_evT.h, sweep["evT"] + gci)
                    ins = None
                    for c in range(DC):
                        ins = te.matmul(
                            out=bank(6 + c // 4, 128, (c % 4) * 128),
                            lhsT=xg[:, (gc % 2) * D + c * 128:
                                    (gc % 2) * D + (c + 1) * 128],
                            rhs=ident[:],
                            start=(c % 4 == 0), stop=(c % 4 == 3 or c == DC - 1))
                    ins.then_inc(*s_mmT.inc())
                for pb in range(NPB):
                    for hc in range(HC):
                        idx = pb * HC + hc
                        ghx = e * NHX + idx
                        te.wait_ge(s_evH.h, sweep["evH"] + max(ghx - 1, 0))
                        te.wait_ge(s_evT.h, b_evT + GCH)
                        bh1, bh3 = idx % 2, 2 + idx % 2
                        for c in range(DC):
                            te.matmul(out=bank(bh1, PB),
                                      lhsT=w1sb[:, c * H + hc * 128:c * H + (hc + 1) * 128],
                                      rhs=xeT[:, c * CAP + pb * PB:c * CAP + (pb + 1) * PB],
                                      start=(c == 0), stop=(c == DC - 1))
                        ins = None
                        for c in range(DC):
                            ins = te.matmul(out=bank(bh3, PB),
                                            lhsT=w3sb[:, c * H + hc * 128:c * H + (hc + 1) * 128],
                                            rhs=xeT[:, c * CAP + pb * PB:c * CAP + (pb + 1) * PB],
                                            start=(c == 0), stop=(c == DC - 1))
                        ins.then_inc(*s_mmH.inc())
                te.wait_ge(s_evH.h, sweep["evH"] + (e + 1) * NHX)
                for pc in range(PC):
                    for n in range(NB2):
                        oidx = pc * NB2 + n
                        gox = e * PC * NB2 + oidx
                        gpc = e * PC + pc
                        bo = 4 + oidx % 2
                        te.wait_ge(s_evO.h, sweep["evO"] + max(gox - 1, 0))
                        if gpc >= 2:
                            te.wait_ge(s_po.h, sweep["po"] + 16 * (gpc - 1))
                        ins = None
                        for k in range(HC):
                            ins = te.matmul(
                                out=bank(bo, OB),
                                lhsT=hgl[:, k * CAP + pc * 128:k * CAP + (pc + 1) * 128],
                                rhs=w2sb[:, k * D + n * OB:k * D + (n + 1) * OB],
                                start=(k == 0), stop=(k == HC - 1))
                        ins.then_inc(*s_mmO.inc())

            @blk.vector
            def _(v):
                for gc in range(GCH):
                    v.wait_ge(s_mmT.h, b_mmT + gc + 1)
                    out_ap = AP(tensor=xeT, offset=gc * 128,
                                ap=[[xeT.shape[1], 128], [CAP, DC], [1, 128]])
                    in_ap = AP(tensor=PS, offset=6 * 512,
                               ap=[[4096, 128], [128, DC], [1, 128]])
                    v.tensor_copy(out=out_ap, in_=in_ap).then_inc(*s_evT.inc())
                for pb in range(NPB):
                    for hc in range(HC):
                        idx = pb * HC + hc
                        v.wait_ge(s_scH.h, b_scH + idx + 1)
                        v.tensor_mul(hgl[:, hc * CAP + pb * PB:hc * CAP + pb * PB + PB],
                                     bank(2 + idx % 2, PB),
                                     hg[:, (idx % 2) * PB:(idx % 2) * PB + PB]
                                     ).then_inc(*s_evH.inc())
                ins = None
                for pc in range(PC):
                    for n in range(NB2):
                        oidx = pc * NB2 + n
                        gpc = e * PC + pc
                        bo = 4 + oidx % 2
                        v.wait_ge(s_mmO.h, b_mmO + oidx + 1)
                        if gpc >= 2 and n == 0:
                            v.wait_ge(s_po.h, sweep["po"] + 16 * (gpc - 1))
                        ins = v.tensor_scalar(
                            po_sb[:, (pc % 2) * D + n * OB:(pc % 2) * D + (n + 1) * OB],
                            bank(bo, OB), 0.5, None, OP.mult)
                        ins.then_inc(*s_evO.inc())
                if free_sem is not None:
                    v.drain().then_inc(*free_sem.inc())

            @blk.scalar
            def _(sc):
                for pb in range(NPB):
                    for hc in range(HC):
                        idx = pb * HC + hc
                        sc.wait_ge(s_mmH.h, b_mmH + idx + 1)
                        sc.activation(out=hg[:, (idx % 2) * PB:(idx % 2) * PB + PB],
                                      in_=bank(idx % 2, PB), func=AF.Silu
                                      ).then_inc(*s_scH.inc())

            @blk.sync
            def _(sy):
                for pc in range(PC):
                    sy.wait_ge(s_evO.h, b_evO + pc * NB2 + NB2)
                    sy.dma_start(
                        out=po_dram[e * CAP + pc * 128:e * CAP + (pc + 1) * 128, :],
                        in_=po_sb[:, (pc % 2) * D:(pc % 2 + 1) * D]
                        ).then_inc(*s_po.inc(16))

        def emit_ffn(tag, src, fold_dram, writeback, combine):
            with ExitStack() as ph:
                psb = lambda n, s2, d: ph.enter_context(nc.sbuf_tensor(tag + n, s2, d))
                y_sb = psb("_y", [128, NT * D], BF16)
                hg = psb("_hg", [128, 2 * max(TB, PB)], BF16)
                hgl = psb("_hgl", [128, max(HsC * TB, HC * CAP)], BF16)
                with ExitStack() as st_a:
                    asb = lambda n, s2, d: st_a.enter_context(
                        nc.sbuf_tensor(tag + n, s2, d))
                    xlnT = asb("_xlnT", [128, DC * T], BF16)
                    st_l = ExitStack()
                    lsb = lambda n, s2, d: st_l.enter_context(
                        nc.sbuf_tensor(tag + n, s2, d))
                    emit_ln_pass(tag + "_ln", src, 0, NT, 0, fold_dram, writeback,
                                 xlnT, T, alloc=lsb, stage_xln=True,
                                 combine=combine)
                    ln_marks = [(s_xo, s_xo.n), (s_xst, s_xst.n),
                                (s_scA, s_scA.n), (s_fold, s_fold.n),
                                (s_lnN, s_lnN.n)]
                    st_l.close()
                    st_g = ExitStack()
                    gsb = lambda n, s2, d: st_g.enter_context(
                        nc.sbuf_tensor(tag + n, s2, d))
                    emit_gating(tag + "_g", xlnT, T, NT, 0, alloc=asb,
                                balloc=gsb, waits=ln_marks)
                    rt_marks = [(s_evG, s_evG.n), (s_scG, s_scG.n),
                                (s_mmG, s_mmG.n)]
                    st_g.close()
                    if DBG and tag == "f1":
                        @blk.sync
                        def _(sy):
                            sy.wait_ge(s_evT.h, s_evT.n)
                            sy.dma_start(out=dbg_xln, in_=xlnT[:]
                                         ).then_inc(*s_xo.inc(16))
                    ws1 = asb("_ws1", [128, DC * Hs], BF16)
                    ws3 = asb("_ws3", [128, DC * Hs], BF16)
                    ws2 = asb("_ws2", [128, HsC * D], BF16)
                    b_w = s_w.n

                    @blk.gpsimd
                    def _(g):
                        for sm, val in rt_marks:
                            g.wait_ge(sm.h, val)
                        g.dma_start(out=ws1[:].rearrange("p (c h) -> p c h", c=DC),
                                    in_=Ws1_in.rearrange("(c p) h -> p c h", p=128)
                                    ).then_inc(*s_w.inc(16))
                        g.dma_start(out=ws3[:].rearrange("p (c h) -> p c h", c=DC),
                                    in_=Ws3_in.rearrange("(c p) h -> p c h", p=128)
                                    ).then_inc(*s_w.inc(16))
                        g.dma_start(out=ws2[:].rearrange("p (c h) -> p c h", c=HsC),
                                    in_=Ws2_in.rearrange("(c p) h -> p c h", p=128)
                                    ).then_inc(*s_w.inc(16))

                    emit_swiglu(tag + "_sh", xlnT, T, ws1, ws3, ws2, HsC, y_sb,
                                hg, hgl, wait_w=b_w + 48, extra_waits=rt_marks)
                shared_mmO = s_mmO.n
                sweep_marks = [(s_mmO, s_mmO.n), (s_scG, s_scG.n)]
                with ExitStack() as st_b:
                    bsb = lambda n, s2, d: st_b.enter_context(
                        nc.sbuf_tensor(tag + n, s2, d))
                    tok_sb = bsb("_tok", [128, E * GCH], U32)
                    xg = bsb("_xg", [128, 2 * D], BF16)
                    xeT = bsb("_xeT", [128, DC * CAP], BF16)
                    po_sb = bsb("_po2", [128, 2 * D], BF16)
                    w1e = bsb("_w1e", [128, 2 * DC * H], BF16)
                    w3e = bsb("_w3e", [128, 2 * DC * H], BF16)
                    w2e = bsb("_w2e", [128, 2 * HC * D], BF16)
                    EH = DC * H
                    b_w2 = s_w.n
                    b_wf = s_wf.n
                    b_rt_all = s_rt.n

                    @blk.sync
                    def _(sy):
                        sy.wait_ge(s_rt.h, b_rt_all)
                        sy.wait_ge(s_mmO.h, shared_mmO)
                        sy.wait_ge(s_evO.h, s_evO.n)
                        sy.dma_start(out=tok_sb[:],
                                     in_=tok_dram.rearrange("(q s) o -> q (s o)", q=128)
                                     [:, 0:E * GCH]).then_inc(*s_tok.inc(16))
                    b_tok = s_tok.n

                    load_tgt = {}

                    def load_expert(g, e):
                        sl1 = slice((e % 2) * EH, (e % 2 + 1) * EH)
                        sl2 = slice((e % 2) * HC * D, (e % 2 + 1) * HC * D)
                        g.dma_start(out=w1e[:, sl1].rearrange("p (c h) -> p c h", c=DC),
                                    in_=W1_in[e].rearrange("(c p) h -> p c h", p=128)
                                    ).then_inc(*s_w.inc(16))
                        g.dma_start(out=w3e[:, sl1].rearrange("p (c h) -> p c h", c=DC),
                                    in_=W3_in[e].rearrange("(c p) h -> p c h", p=128)
                                    ).then_inc(*s_w.inc(16))
                        g.dma_start(out=w2e[:, sl2].rearrange("p (c h) -> p c h", c=HC),
                                    in_=W2_in[e].rearrange("(c p) h -> p c h", p=128)
                                    ).then_inc(*s_w.inc(16))
                        load_tgt[e] = s_w.n

                    @blk.gpsimd
                    def _(g):
                        g.wait_ge(s_mmO.h, shared_mmO)
                        load_expert(g, 0)
                        if E > 1:
                            load_expert(g, 1)

                    first = True
                    sweep = dict(mmT=s_mmT.n, evT=s_evT.n, evH=s_evH.n,
                                 evO=s_evO.n, po=s_po.n)
                    for e in range(E):
                        if first:
                            @blk.tensor
                            def _(te):
                                for sm, val in sweep_marks:
                                    te.wait_ge(sm.h, val)
                            first = False
                        if 1 <= e < E - 1:
                            def prefetch(g, e=e):
                                g.wait_ge(s_wf.h, b_wf + e)
                                load_expert(g, e + 1)
                        else:
                            prefetch = None
                        sl1 = slice((e % 2) * EH, (e % 2 + 1) * EH)
                        sl2 = slice((e % 2) * HC * D, (e % 2 + 1) * HC * D)
                        emit_expert_routed(e, w1e[:, sl1], w3e[:, sl1],
                                           w2e[:, sl2], tok_sb, xg, xeT, po_sb,
                                           hg, hgl, load_tgt[e], s_wf, b_tok,
                                           sweep, prefetch=prefetch)
                    if DBG and tag == "f1":
                        @blk.sync
                        def _(sy):
                            sy.wait_ge(s_wf.h, b_wf + E)
                            sy.dma_start(out=dbg_toksb, in_=tok_sb[:]
                                         ).then_inc(*s_xo.inc(16))
                            sy.dma_start(out=dbg_xg, in_=xg[:]
                                         ).then_inc(*s_xo.inc(16))
                            sy.dma_start(out=dbg_xe, in_=xeT[:]
                                         ).then_inc(*s_xo.inc(16))
                @blk.sync
                def _(sy):
                    sy.wait_ge(s_evO.h, s_evO.n)
                    sy.dma_start(out=y_dram.rearrange("(t p) d -> p t d", p=128),
                                 in_=y_sb[:].rearrange("p (t d) -> p t d", d=D)
                                 ).then_inc(*s_xo.inc(16))
                barrier()

        def emit_attn():
            for sq_ in range(B_loc):
                with ExitStack() as ph:
                    psb = lambda n, sh, d: ph.enter_context(
                        nc.sbuf_tensor(f"at{sq_}{n}", sh, d))
                    bv_sb = psb("_bv", [128, D], F32)

                    @blk.sync
                    def _(sy):
                        sy.dma_start(out=bv_sb[:], in_=bv_in).then_inc(*s_boot.inc(16))
                    qT = psb("_qT", [128, DC * S], BF16)
                    kT = psb("_kT", [128, DC * S], BF16)
                    v_sb = psb("_v", [128, ST * D], BF16)
                    st1 = ph.enter_context(ExitStack())
                    s1b = lambda n, sh, d: st1.enter_context(
                        nc.sbuf_tensor(f"at{sq_}{n}", sh, d))
                    xlnT = s1b("_xlnT", [128, DC * S], BF16)
                    with ExitStack() as st2:
                        s2b = lambda n, sh, d: st2.enter_context(
                            nc.sbuf_tensor(f"at{sq_}{n}", sh, d))
                        emit_ln_pass(f"at{sq_}_ln", x_in, sq_ * S, ST, sq_ * ST,
                                     y_dram, True, xlnT, S, alloc=s2b,
                                     combine=True)
                        barrier()
                    wpj = s1b("_wpj", [128, DC * D], BF16)
                    b_w = s_w.n
                    b_mmP, b_evP = s_mmP.n, s_evP.n
                    per = DC * SB2

                    @blk.gpsimd
                    def _(g):
                        for j, wt in enumerate((WqT_in, WkT_in, WvT_in)):
                            if j >= 1:
                                g.wait_ge(s_evP.h, b_evP + j * per)
                            g.dma_start(
                                out=wpj[:].rearrange("p (c h) -> p c h", c=DC),
                                in_=wt.rearrange("(c p) h -> p c h", p=128)
                                ).then_inc(*s_w.inc(16))

                    @blk.tensor
                    def _(te):
                        for j in range(3):
                            te.wait_ge(s_w.h, b_w + 16 * (j + 1))
                            wsl = 0
                            if j < 2:
                                for mc in range(DC):
                                    for n in range(SB2):
                                        idx = j * per + mc * SB2 + n
                                        bo = idx % 4
                                        if idx >= 4:
                                            te.wait_ge(s_evP.h, b_evP + idx - 3)
                                        ins = None
                                        for c in range(DC):
                                            ins = te.matmul(
                                                out=bank(bo, SBW),
                                                lhsT=wpj[:, wsl + c * D + mc * 128:
                                                         wsl + c * D + (mc + 1) * 128],
                                                rhs=xlnT[:, c * S + n * SBW:
                                                         c * S + (n + 1) * SBW],
                                                start=(c == 0), stop=(c == DC - 1))
                                        ins.then_inc(*s_mmP.inc())
                            else:
                                for tc in range(ST):
                                    for n in range(NB2):
                                        idx = 2 * per + tc * NB2 + n
                                        bo = idx % 4
                                        if idx >= 4:
                                            te.wait_ge(s_evP.h, b_evP + idx - 3)
                                        ins = None
                                        for c in range(DC):
                                            ins = te.matmul(
                                                out=bank(bo, OB),
                                                lhsT=xlnT[:, c * S + tc * 128:
                                                          c * S + (tc + 1) * 128],
                                                rhs=wpj[:, wsl + c * D + n * OB:
                                                         wsl + c * D + (n + 1) * OB],
                                                start=(c == 0), stop=(c == DC - 1))
                                        ins.then_inc(*s_mmP.inc())

                    @blk.vector
                    def _(v):
                        for j in range(3):
                            if j < 2:
                                bias = bq_sb if j == 0 else bk_sb
                                for mc in range(DC):
                                    for n in range(SB2):
                                        idx = j * per + mc * SB2 + n
                                        bo = idx % 4
                                        v.wait_ge(s_mmP.h, b_mmP + idx + 1)
                                        dst = (qT if j == 0 else kT)[
                                            :, mc * S + n * SBW:mc * S + (n + 1) * SBW]
                                        if j == 0:
                                            ins = v.tensor_scalar(
                                                dst, bank(bo, SBW), bias[:, mc:mc + 1],
                                                0.125, OP.add, OP.mult)
                                        else:
                                            ins = v.tensor_scalar(
                                                dst, bank(bo, SBW), bias[:, mc:mc + 1],
                                                None, OP.add)
                                        ins.then_inc(*s_evP.inc())
                            else:
                                for tc in range(ST):
                                    for n in range(NB2):
                                        idx = 2 * per + tc * NB2 + n
                                        bo = idx % 4
                                        v.wait_ge(s_mmP.h, b_mmP + idx + 1)
                                        v.tensor_add(
                                            v_sb[:, tc * D + n * OB:tc * D + (n + 1) * OB],
                                            bank(bo, OB), bv_sb[:, n * OB:(n + 1) * OB]
                                            ).then_inc(*s_evP.inc())
                    barrier()
                    st1.close()

                    ctx_seq = psb("_ctx", [128, ST * D], BF16)
                    relkT = psb("_relkT", [128, JT * IQ], BF16)
                    kd_sb = psb("_kd", [128, 2 * WK], BF16)
                    qdb = psb("_qdb", [128, 4 * WQ], BF16)
                    rq_sb = psb("_rq", [128, 8 * JB], BF16)
                    Pb = psb("_Pb", [128, S], BF16)
                    PT = psb("_PT", [128, 512], BF16)
                    zp = psb("_zp", [128, 2 * NJB + 4], F32)

                    for h in range(NH):
                        hb = 64 * (h % 2)
                        hcol = (h // 2) * S
                        for iq in range(NIQ):
                            b_mmK, b_evK, b_shK = s_mmK.n, s_evK.n, s_shK.n

                            @blk.tensor
                            def _(te):
                                te.wait_ge(s_w.h, 32)
                                te.wait_ge(s_evS.h, s_evS.n)
                                for jt in range(JT):
                                    if jt >= 2:
                                        te.wait_ge(s_evK.h, b_evK + jt - 1)
                                    s0 = C0 + iq * IQ - jt * 128 - 127
                                    g0 = (jt % 2) * 2
                                    w1k = min(512, WK)
                                    ins = te.matmul(out=bank(g0, w1k),
                                              lhsT=kT[hb:hb + 64,
                                                      hcol + jt * 128:hcol + (jt + 1) * 128],
                                              rhs=deT[hb:hb + 64, s0:s0 + w1k],
                                              start=True, stop=True)
                                    if WK > 512:
                                        ins = te.matmul(out=bank(g0 + 1, WK - 512),
                                              lhsT=kT[hb:hb + 64,
                                                      hcol + jt * 128:hcol + (jt + 1) * 128],
                                              rhs=deT[hb:hb + 64, s0 + 512:s0 + WK],
                                              start=True, stop=True)
                                    ins.then_inc(*s_mmK.inc())

                            @blk.vector
                            def _(v):
                                for jt in range(JT):
                                    v.wait_ge(s_mmK.h, b_mmK + jt + 1)
                                    if jt >= 2:
                                        v.wait_ge(s_shK.h, b_shK + 16 * (jt - 1))
                                    g0 = (jt % 2) * 2
                                    v.tensor_scalar(
                                        kd_sb[:, (jt % 2) * WK:(jt % 2) * WK + WK],
                                        PS[:, g0 * 512:g0 * 512 + WK],
                                        0.125, None, OP.mult).then_inc(*s_evK.inc())

                            @blk.scalar
                            def _(sc):
                                for jt in range(JT):
                                    sc.wait_ge(s_evK.h, b_evK + jt + 1)
                                    diag = AP(tensor=kd_sb, offset=(jt % 2) * WK + 127,
                                              ap=[[2 * WK - 1, 128], [1, IQ]])
                                    sc.dma_start(out=relkT[:, jt * IQ:(jt + 1) * IQ],
                                                 in_=diag).then_inc(*s_shK.inc(16))

                            G4 = 4 if JT % 4 == 0 else JT
                            shK_end = s_shK.n

                            def emit_qd(it, itl):
                                g0 = s_mmS.n
                                evK_end = s_evK.n

                                @blk.tensor
                                def _(te):
                                    for jb in range(NJB):
                                        g = g0 + jb
                                        if itl == 0 and jb < 2:
                                            te.wait_ge(s_evK.h, evK_end)
                                        if g >= 2:
                                            te.wait_ge(s_evS.h, g - 1)
                                        gb = (jb % 2) * 2
                                        s0q = C0 + jb * JB - it * 128 - 127
                                        w1q = min(512, WQ)
                                        ins = te.matmul(out=bank(gb, w1q),
                                                  lhsT=qT[hb:hb + 64,
                                                          hcol + it * 128:hcol + (it + 1) * 128],
                                                  rhs=derT[hb:hb + 64, s0q:s0q + w1q],
                                                  start=True, stop=True)
                                        if WQ > 512:
                                            ins = te.matmul(out=bank(gb + 1, WQ - 512),
                                                  lhsT=qT[hb:hb + 64,
                                                          hcol + it * 128:hcol + (it + 1) * 128],
                                                  rhs=derT[hb:hb + 64, s0q + 512:s0q + WQ],
                                                  start=True, stop=True)
                                        ins.then_inc(*s_mmS.inc())

                                @blk.vector
                                def _(v):
                                    for jb in range(NJB):
                                        g = g0 + jb
                                        v.wait_ge(s_mmS.h, g + 1)
                                        if g >= 2:
                                            v.wait_ge(s_shS.h, 16 * (g - 1))
                                        gb = (jb % 2) * 2
                                        q4 = (it % 2) * 2 + (jb % 2)
                                        v.tensor_copy(
                                            out=qdb[:, q4 * WQ:q4 * WQ + WQ],
                                            in_=PS[:, gb * 512:gb * 512 + WQ]
                                            ).then_inc(*s_evS.inc())

                                @blk.scalar
                                def _(sc):
                                    for jb in range(NJB):
                                        g = g0 + jb
                                        sc.wait_ge(s_evS.h, g + 1)
                                        if g >= 2 * NJB:
                                            sc.wait_ge(s_mmU.h, g - 2 * NJB + 1)
                                        q4 = (it % 2) * 2 + (jb % 2)
                                        r8 = (it % 2) * 4 + jb
                                        diag = AP(tensor=qdb, offset=q4 * WQ + 127,
                                                  ap=[[4 * WQ - 1, 128], [1, JB]])
                                        sc.dma_start(
                                            out=rq_sb[:, r8 * JB:(r8 + 1) * JB],
                                            in_=diag).then_inc(*s_shS.inc(16))

                            def emit_su(it, itl):
                                gs0 = s_mmU.n
                                git = s_evZ.n
                                zb = (git % 2) * NJB
                                b_mmX = s_mmX.n
                                b_ptc = s_ptc.n
                                b_evX = s_evX.n

                                @blk.tensor
                                def _(te):
                                    if itl == 0:
                                        te.wait_ge(s_shK.h, shK_end)
                                    for jb in range(NJB):
                                        gs = gs0 + jb
                                        bs = 4 + jb % 2
                                        te.wait_ge(s_shS.h, 16 * (gs + 1))
                                        if gs >= 2:
                                            te.wait_ge(s_scS.h, gs - 1)
                                        r8 = (it % 2) * 4 + jb
                                        te.matmul(out=bank(bs, JB),
                                                  lhsT=qT[hb:hb + 64,
                                                          hcol + it * 128:hcol + (it + 1) * 128],
                                                  rhs=kT[hb:hb + 64,
                                                         hcol + jb * JB:hcol + (jb + 1) * JB],
                                                  start=True, stop=False)
                                        for j2 in range(JB // 128):
                                            jt = jb * (JB // 128) + j2
                                            te.matmul(out=bank(bs, 128, j2 * 128),
                                                      lhsT=relkT[:, jt * IQ + itl * 128:
                                                                 jt * IQ + (itl + 1) * 128],
                                                      rhs=ident[:],
                                                      start=False, stop=False)
                                        te.matmul(out=bank(bs, JB), lhsT=ident[:],
                                                  rhs=rq_sb[:, r8 * JB:(r8 + 1) * JB],
                                                  start=False, stop=True
                                                  ).then_inc(*s_mmU.inc())

                                @blk.scalar
                                def _(sc):
                                    if git >= 2:
                                        sc.wait_ge(s_evZ.h, git - 1)
                                    if git >= 1:
                                        sc.wait_ge(s_mmX.h, b_mmX)
                                    for jb in range(NJB):
                                        gs = gs0 + jb
                                        sc.wait_ge(s_mmU.h, gs + 1)
                                        bs = 4 + jb % 2
                                        sc.activation(out=Pb[:, jb * JB:(jb + 1) * JB],
                                                      in_=bank(bs, JB), func=AF.Exp,
                                                      bias=float(-EXPB), scale=1.0,
                                                      accum_out=zp[:, zb + jb:zb + jb + 1]
                                                      ).then_inc(*s_scS.inc())
                                    sc.drain().then_inc(*s_zpD.inc())

                                @blk.vector
                                def _(v):
                                    v.wait_ge(s_zpD.h, git + 1)
                                    v.reduce_sum(zp[:, 2 * NJB + (git % 2):2 * NJB + (git % 2) + 1],
                                                 zp[:, zb:zb + NJB], axis=AX)
                                    v.drain()
                                    v.reciprocal(zp[:, 2 * NJB + 2 + (git % 2):2 * NJB + 3 + (git % 2)],
                                                 zp[:, 2 * NJB + (git % 2):2 * NJB + (git % 2) + 1])
                                    v.drain().then_inc(*s_evZ.inc())

                                @blk.tensor
                                def _(te):
                                    te.wait_ge(s_scS.h, gs0 + NJB)
                                    for grp in range(JT // G4):
                                        if grp >= 1:
                                            te.wait_ge(s_ptc.h, b_ptc + grp)
                                        ins = None
                                        for j2 in range(G4):
                                            jt = grp * G4 + j2
                                            ins = te.matmul(out=bank(6, 128, j2 * 128),
                                                            lhsT=Pb[:, jt * 128:(jt + 1) * 128],
                                                            rhs=ident[:],
                                                            start=(j2 == 0), stop=(j2 == G4 - 1))
                                        ins.then_inc(*s_mmX.inc())
                                        te.wait_ge(s_ptc.h, b_ptc + grp + 1)
                                        if grp == 0:
                                            te.wait_ge(s_evX.h, b_evX)
                                        ins = None
                                        for j2 in range(G4):
                                            jt = grp * G4 + j2
                                            ins = te.matmul(
                                                out=bank(7, 64),
                                                lhsT=PT[:, j2 * 128:(j2 + 1) * 128],
                                                rhs=v_sb[:, jt * D + h * 64:jt * D + (h + 1) * 64],
                                                start=(jt == 0), stop=(jt == JT - 1))
                                        ins.then_inc(*s_mmX.inc())

                                @blk.vector
                                def _(v):
                                    for grp in range(JT // G4):
                                        v.wait_ge(s_mmX.h, b_mmX + 2 * grp + 1)
                                        v.tensor_copy(out=PT[:, 0:G4 * 128],
                                                      in_=bank(6, G4 * 128)
                                                      ).then_inc(*s_ptc.inc())

                                @blk.vector
                                def _(v):
                                    v.wait_ge(s_mmX.h, b_mmX + 2 * (JT // G4))
                                    v.wait_ge(s_evZ.h, git + 1)
                                    v.tensor_scalar(
                                        ctx_seq[:, it * D + h * 64:it * D + (h + 1) * 64],
                                        bank(7, 64),
                                        zp[:, 2 * NJB + 2 + (git % 2):2 * NJB + 3 + (git % 2)],
                                        None, OP.mult).then_inc(*s_evX.inc())

                            for itl in range(IQC):
                                emit_qd(iq * IQC + itl, itl)
                                emit_su(iq * IQC + itl, itl)

                    @blk.sync
                    def _(sy):
                        sy.wait_ge(s_evX.h, s_evX.n)
                        sy.dma_start(
                            out=ctx_dram[sq_ * S:(sq_ + 1) * S]
                            .rearrange("(t p) d -> p t d", p=128),
                            in_=ctx_seq[:].rearrange("p (t d) -> p t d", d=D)
                            ).then_inc(*s_xo.inc(16))
                    barrier()

        LVL = p.get("LVL", 3)
        emit_ffn("f1", x_in, None, False, False)
        if DBG:
            @blk.sync
            def _(sy):
                sy.dma_start(out=dbg_y.rearrange("p (t d) -> p t d", d=D),
                             in_=y_dram.rearrange("(t p) d -> p t d", p=128)
                             ).then_inc(*s_xo.inc(16))
                sy.dma_start(out=dbg_wtk, in_=wtk_sb[:]).then_inc(*s_xo.inc(16))
                sy.dma_start(out=dbg_dtk, in_=dtk_sb[:]).then_inc(*s_xo.inc(16))
                sy.dma_start(out=dbg_dtk2, in_=dtk2_sb[:]).then_inc(*s_xo.inc(16))
                sy.dma_start(out=dbg_tok,
                             in_=tok_dram.rearrange("(q s) o -> q (s o)", q=128)
                             ).then_inc(*s_xo.inc(16))
                sy.dma_start(out=dbg_po, in_=po_dram).then_inc(*s_xo.inc(16))

            barrier()
        if LVL >= 2:
            emit_attn()
        if DBG and LVL >= 2:
            @blk.sync
            def _(sy):
                sy.dma_start(out=dbg_ctx.rearrange("p (t d) -> p t d", d=D),
                             in_=ctx_dram.rearrange("(t p) d -> p t d", p=128)
                             ).then_inc(*s_xo.inc(16))
            barrier()
        if LVL >= 3:
            emit_ffn("f2", x_out, ctx_dram, True, False)
            barrier()
            emit_ln_pass("fin", x_out, 0, NT, 0, y_dram, True, None, 0,
                         ln=False, combine=True)

        @blk.sync
        def _(sy):
            sy.wait_ge(s_xo.h, s_xo.n)

    es.close()
    return nc


def _prep_inputs(inp, p, core):
    p = dict(p)
    p.setdefault("CAP", p["B_loc"] * p["S"])
    S, D = p["S"], p["D"]
    DC = D // 128
    B_loc = p["B_loc"]
    T = B_loc * S
    x = np.ascontiguousarray(
        inp["L0_attn_hidden_states"][core * B_loc:(core + 1) * B_loc]
    ).reshape(T, D)
    de = np.asarray(inp["L0_attn_dist_emb"])
    deT = np.ascontiguousarray(de.T)
    derT = np.ascontiguousarray(de[::-1].T)
    f32 = np.float32
    return {
        "x_in": np.asarray(x, f32),
        "WqT": np.ascontiguousarray(np.asarray(inp["L0_attn_Wq"], f32).T),
        "WkT": np.ascontiguousarray(np.asarray(inp["L0_attn_Wk"], f32).T),
        "WvT": np.ascontiguousarray(np.asarray(inp["L0_attn_Wv"], f32).T),
        "bqT": np.ascontiguousarray(np.asarray(inp["L0_attn_bq"], f32).reshape(DC, 128).T),
        "bkT": np.ascontiguousarray(np.asarray(inp["L0_attn_bk"], f32).reshape(DC, 128).T),
        "bv_rep": np.ascontiguousarray(
            np.broadcast_to(np.asarray(inp["L0_attn_bv"], f32), (128, D))),
        "deT_d": np.ascontiguousarray(np.concatenate([deT, deT], axis=0), f32),
        "derT_d": np.ascontiguousarray(np.concatenate([derT, derT], axis=0), f32),
        "Wg_in": np.asarray(inp["L0_ffn_Wg"], f32),
        "W1_in": np.asarray(inp["L0_ffn_W1"], f32),
        "W2_in": np.asarray(inp["L0_ffn_W2"], f32),
        "W3_in": np.asarray(inp["L0_ffn_W3"], f32),
        "Ws1_in": np.asarray(inp["L0_ffn_Ws1"], f32),
        "Ws2_in": np.asarray(inp["L0_ffn_Ws2"], f32),
        "Ws3_in": np.asarray(inp["L0_ffn_Ws3"], f32),
        "ident_in": np.eye(128, dtype=ml_dtypes.bfloat16),
        "identf": np.eye(128, dtype=np.float32),
        "iota_tok": np.ascontiguousarray(
            (np.arange(T // 128)[None, :] * 128 + np.arange(128)[:, None])
            .astype(np.uint32)),
        "tok_pad": np.full((128, (p["E"] * p["CAP"]) // 128 + 1), T,
                           dtype=np.uint32),
        "eoff": np.ascontiguousarray(
            (np.arange(32, dtype=np.float32) * p["CAP"] - 1.0).reshape(32, 1)),
    }


def kernel(**inputs):
    p = FULL
    nc = build(p)
    in_maps = [_prep_inputs(inputs, p, c) for c in range(N_CORES)]
    res = run_bass_kernel_spmd(nc, in_maps, core_ids=list(range(N_CORES)))
    out = np.concatenate(
        [r["x_out"].reshape(p["B_loc"], p["S"], p["D"]) for r in res.results], axis=0)
    return np.ascontiguousarray(out, dtype=np.float32)


# version 04
# speedup vs baseline: 1.5623x; 1.0052x over previous
from contextlib import ExitStack

import numpy as np
import ml_dtypes

import concourse.bass as bass
import concourse.mybir as mybir
from concourse.bass_types import AP
from concourse.bass_core_rust import IndirectOffsetOnAxis
from concourse.bass_utils import run_bass_kernel_spmd

F32 = mybir.dt.float32
BF16 = mybir.dt.bfloat16
U32 = mybir.dt.uint32
I32 = mybir.dt.int32
AF = mybir.ActivationFunctionType
OP = mybir.AluOpType
AX = mybir.AxisListType.X

N_CORES = 8

FULL = dict(S=2048, D=1024, NH=16, E=32, H=512, Hs=1024, B_loc=2, MAXP=2048,
            JB=512, IQ=512, EXPB=4.0, CAP=768)


def build(p):
    S, D, E, H, Hs = p["S"], p["D"], p["E"], p["H"], p["Hs"]
    NH, B_loc, MAXP = p["NH"], p["B_loc"], p["MAXP"]
    JB, IQ, EXPB = p["JB"], p["IQ"], p["EXPB"]
    T = B_loc * S
    CAP = p.get("CAP", T)
    GCH = CAP // 128
    PB = min(512, CAP)
    while CAP % PB:
        PB -= 128
    NPB = CAP // PB
    DC, HC, HsC = D // 128, H // 128, Hs // 128
    NT, ST = T // 128, S // 128
    R, C0 = 2 * MAXP - 1, MAXP - 1
    NJB, JT, NIQ = S // JB, S // 128, S // IQ
    IQC = IQ // 128
    TB = 512 if T >= 512 else T
    NTB = T // TB
    TBC = TB // 128
    OB = min(512, D)
    SBW = min(512, S)
    NB2 = D // OB
    SB2 = S // SBW
    WQ = JB + 127
    WK = IQ + 127

    nc = bass.Bass("TRN2", target_bir_lowering=False, detect_race_conditions=False)
    es = ExitStack()

    def sbuf(name, shape, dt):
        return es.enter_context(nc.sbuf_tensor(name, shape, dt))

    all_sems = []

    class Sem:
        __slots__ = ("h", "n")

        def __init__(self, name):
            self.h = es.enter_context(nc.semaphore(name))
            self.n = 0
            all_sems.append(self)

        def inc(self, k=1):
            self.n += k
            return (self.h, k)

    dt_ = nc.dram_tensor
    x_in = dt_("x_in", [T, D], F32, kind="ExternalInput").ap()
    x_out = dt_("x_out", [T, D], F32, kind="ExternalOutput").ap()
    WqT_in = dt_("WqT", [D, D], F32, kind="ExternalInput").ap()
    WkT_in = dt_("WkT", [D, D], F32, kind="ExternalInput").ap()
    WvT_in = dt_("WvT", [D, D], F32, kind="ExternalInput").ap()
    bqT_in = dt_("bqT", [128, DC], F32, kind="ExternalInput").ap()
    bkT_in = dt_("bkT", [128, DC], F32, kind="ExternalInput").ap()
    bv_in = dt_("bv_rep", [128, D], F32, kind="ExternalInput").ap()
    deT_in = dt_("deT_d", [128, R], F32, kind="ExternalInput").ap()
    derT_in = dt_("derT_d", [128, R], F32, kind="ExternalInput").ap()
    Wg_in = dt_("Wg_in", [D, E], F32, kind="ExternalInput").ap()
    W1_in = dt_("W1_in", [E, D, H], F32, kind="ExternalInput").ap()
    W2_in = dt_("W2_in", [E, H, D], F32, kind="ExternalInput").ap()
    W3_in = dt_("W3_in", [E, D, H], F32, kind="ExternalInput").ap()
    Ws1_in = dt_("Ws1_in", [D, Hs], F32, kind="ExternalInput").ap()
    Ws2_in = dt_("Ws2_in", [Hs, D], F32, kind="ExternalInput").ap()
    Ws3_in = dt_("Ws3_in", [D, Hs], F32, kind="ExternalInput").ap()
    ident_in = dt_("ident_in", [128, 128], BF16, kind="ExternalInput").ap()
    y_dram = dt_("y_dram", [T, D], BF16, kind="Internal").ap()
    ctx_dram = dt_("ctx_dram", [T, D], BF16, kind="Internal").ap()
    xln_dram = dt_("xln_dram", [T + 128, D], BF16, kind="Internal").ap()
    po_dram = dt_("po_dram", [E * CAP + 128, D], BF16, kind="Internal").ap()
    WT = E * GCH + 1
    tok_dram = dt_("tok_dram", [128 * WT, 1], U32, kind="Internal").ap()
    iota_tok_in = dt_("iota_tok", [128, NT], U32, kind="ExternalInput").ap()
    tok_pad_in = dt_("tok_pad", [128, (E * CAP + 128) // 128], U32,
                     kind="ExternalInput").ap()
    eoff_in = dt_("eoff", [32, 1], F32, kind="ExternalInput").ap()
    identf_in = dt_("identf", [128, 128], F32, kind="ExternalInput").ap()
    DBG = p.get("DBG", False)
    if DBG:
        dbg_y = dt_("dbg_y", [128, NT * D], BF16, kind="ExternalOutput").ap()
        dbg_mw = dt_("dbg_mw", [128, NT * E], F32, kind="ExternalOutput").ap()
        dbg_ctx = dt_("dbg_ctx", [128, NT * D], BF16, kind="ExternalOutput").ap()
        dbg_xln = dt_("dbg_xln", [128, (D // 128) * T], BF16, kind="ExternalOutput").ap()
        dbg_wtk = dt_("dbg_wtk", [128, NT * 4], F32, kind="ExternalOutput").ap()
        dbg_dtk = dt_("dbg_dtk", [128, NT * 4], U32, kind="ExternalOutput").ap()
        dbg_dtk2 = dt_("dbg_dtk2", [128, NT * 4], U32, kind="ExternalOutput").ap()
        dbg_tok = dt_("dbg_tok", [128, WT], U32, kind="ExternalOutput").ap()
        dbg_po = dt_("dbg_po", [E * CAP + 128, D], BF16, kind="ExternalOutput").ap()
        dbg_toksb = dt_("dbg_toksb", [128, E * GCH], U32, kind="ExternalOutput").ap()
        dbg_xg = dt_("dbg_xg", [128, 2 * D], BF16, kind="ExternalOutput").ap()
        dbg_xe = dt_("dbg_xe", [128, (D // 128) * CAP], BF16, kind="ExternalOutput").ap()

    consts = sbuf("consts", [128, 8], F32)
    nc.const_aps.register(F32, 1.0, consts[:, 0:1])
    nc.const_aps.register(F32, 0.5, consts[:, 1:2])
    nc.const_aps.register(F32, 1e-5, consts[:, 2:3])
    nc.const_aps.register(F32, float(-EXPB), consts[:, 3:4])
    nc.const_aps.register(F32, 0.0, consts[:, 4:5])
    ident = sbuf("ident", [128, 128], BF16)
    deT = sbuf("deT", [128, R], BF16)
    derT = sbuf("derT", [128, R], BF16)
    wg_sb = sbuf("wg_sb", [128, DC * E], BF16)
    bq_sb = sbuf("bq_sb", [128, DC], F32)
    bk_sb = sbuf("bk_sb", [128, DC], F32)
    wtk_sb = sbuf("wtk_sb", [128, NT * 4], F32)
    dtk_sb = sbuf("dtk_sb", [128, NT * 4], U32)
    dtk2_sb = sbuf("dtk2_sb", [128, NT * 4], U32)
    scri = sbuf("scri", [128, 8], I32)
    iota_tok = sbuf("iota_tok_sb", [128, NT], U32)
    eoff_sb = sbuf("eoff_sb", [32, 1], F32)
    identf = sbuf("identf_sb", [128, 128], F32)
    zrow = sbuf("zrow", [128, D], BF16)

    PS = es.enter_context(nc.psum_tensor("PS", [128, 4096], F32))

    def bank(i, w=512, off=0):
        return PS[:, i * 512 + off:i * 512 + off + w]

    s_boot, s_cboot = Sem("s_boot"), Sem("s_cboot")
    s_x, s_w = Sem("s_x"), Sem("s_w")
    s_fold, s_lnA, s_scA, s_lnN = Sem("s_fold"), Sem("s_lnA"), Sem("s_scA"), Sem("s_lnN")
    s_mmT, s_evT, s_xo = Sem("s_mmT"), Sem("s_evT"), Sem("s_xo")
    s_mmG, s_scG, s_evG = Sem("s_mmG"), Sem("s_scG"), Sem("s_evG")
    s_mmH, s_scH, s_evH = Sem("s_mmH"), Sem("s_scH"), Sem("s_evH")
    s_mmO, s_evO, s_wf = Sem("s_mmO"), Sem("s_evO"), Sem("s_wf")
    s_mmP, s_evP = Sem("s_mmP"), Sem("s_evP")
    s_mmK, s_evK, s_shK = Sem("s_mmK"), Sem("s_evK"), Sem("s_shK")
    s_mmS, s_evS, s_shS, s_scS = Sem("s_mmS"), Sem("s_evS"), Sem("s_shS"), Sem("s_scS")
    s_evZ, s_mmX, s_evX = Sem("s_evZ"), Sem("s_mmX"), Sem("s_evX")
    s_mmU, s_zpD, s_ptc, s_mmC = Sem("s_mmU"), Sem("s_zpD"), Sem("s_ptc"), Sem("s_mmC")
    s_xst, s_rt, s_tok = Sem("s_xst"), Sem("s_rt"), Sem("s_tok")
    s_pog, s_po, s_xg = Sem("s_pog"), Sem("s_po"), Sem("s_xg")

    with nc.Block() as blk:

        def barrier():
            snap = [(s.h, s.n) for s in all_sems if s.n > 0]

            def w(eng):
                for h, n in snap:
                    eng.wait_ge(h, n)
            blk.sync(w)
            blk.vector(w)
            blk.scalar(w)
            blk.tensor(w)
            blk.gpsimd(w)

        @blk.vector
        def _(v):
            v.memset(consts[:, 0:1], 1.0)
            v.memset(consts[:, 1:2], 0.5)
            v.memset(consts[:, 2:3], 1e-5)
            v.memset(consts[:, 3:4], float(-EXPB))
            v.memset(consts[:, 4:5], 0.0).then_inc(*s_cboot.inc())

        @blk.sync
        def _(sy):
            sy.dma_start(out=ident[:], in_=ident_in).then_inc(*s_boot.inc(16))
            sy.dma_start(out=bq_sb[:], in_=bqT_in).then_inc(*s_boot.inc(16))
            sy.dma_start(out=bk_sb[:], in_=bkT_in).then_inc(*s_boot.inc(16))
            sy.dma_start(out=iota_tok[:], in_=iota_tok_in).then_inc(*s_boot.inc(16))
            sy.dma_start(out=eoff_sb[:], in_=eoff_in).then_inc(*s_boot.inc(16))
            sy.dma_start(out=identf[:], in_=identf_in).then_inc(*s_boot.inc(16))

        @blk.vector
        def _(v):
            v.memset(zrow[:], 0.0).then_inc(*s_cboot.inc())

        @blk.sync
        def _(sy):
            sy.wait_ge(s_cboot.h, s_cboot.n)
            sy.dma_start(out=xln_dram[T:T + 128, :], in_=zrow[:]
                         ).then_inc(*s_boot.inc(16))
            sy.dma_start(out=po_dram[E * CAP:E * CAP + 128, :], in_=zrow[:]
                         ).then_inc(*s_boot.inc(16))

        @blk.gpsimd
        def _(g):
            g.dma_start(out=deT[:], in_=deT_in).then_inc(*s_w.inc(16))
            g.dma_start(out=derT[:], in_=derT_in).then_inc(*s_w.inc(16))
            g.dma_start(out=wg_sb[:].rearrange("p (c e) -> p c e", c=DC),
                        in_=Wg_in.rearrange("(c p) e -> p c e", p=128)
                        ).then_inc(*s_w.inc(16))

        def emit_ln_pass(tag, src, row0, ntiles, tile0, fold_dram, writeback,
                         xlnT, span, ln=True, alloc=None, stage_xln=False,
                         combine=False):
            alloc = alloc or sbuf
            xt = alloc(tag + "_xt", [128, 2 * D], F32)
            xw = alloc(tag + "_xw", [128, 2 * D], F32)
            stat = alloc(tag + "_stat", [128, 8], F32)
            sq = alloc(tag + "_sq", [128, D], BF16)
            xln = alloc(tag + "_xln", [128, 2 * D], BF16) if ln else None
            have_fold = fold_dram is not None
            fbuf = alloc(tag + "_fb", [128, 2 * D], BF16) if have_fold else None
            pob = alloc(tag + "_po", [128, 4 * D], BF16) if combine else None
            b_x, b_fold, b_lnA, b_scA = s_x.n, s_fold.n, s_lnA.n, s_scA.n
            b_lnN, b_mmT, b_evT, b_xo = s_lnN.n, s_mmT.n, s_evT.n, s_xo.n
            b_xst, b_pog = s_xst.n, s_pog.n
            tb = max(1, (DC * 128) // 512)
            per_t = 32 if have_fold else 16
            assert ln or (have_fold and writeback)

            def emit_evac(v, u):
                v.wait_ge(s_mmT.h, b_mmT + u + 1)
                g0 = (u % 2) * tb
                out_ap = AP(tensor=xlnT, offset=u * 128,
                            ap=[[xlnT.shape[1], 128], [span, DC], [1, 128]])
                in_ap = AP(tensor=PS, offset=g0 * 512,
                           ap=[[4096, 128], [128, DC], [1, 128]])
                v.tensor_copy(out=out_ap, in_=in_ap).then_inc(*s_evT.inc())

            @blk.sync
            def _(sy):
                for t in range(ntiles):
                    if t >= 2:
                        if ln:
                            sy.wait_ge(s_lnN.h, b_lnN + (t - 1))
                        else:
                            sy.wait_ge(s_fold.h, b_fold + (t - 1))
                    sy.dma_start(out=xt[:, (t % 2) * D:(t % 2 + 1) * D],
                                 in_=src[row0 + t * 128:row0 + (t + 1) * 128, :]
                                 ).then_inc(*s_x.inc(16))
                    if have_fold:
                        sy.dma_start(out=fbuf[:, (t % 2) * D:(t % 2 + 1) * D],
                                     in_=fold_dram[row0 + t * 128:row0 + (t + 1) * 128, :]
                                     ).then_inc(*s_x.inc(16))
                    if writeback:
                        sy.wait_ge(s_fold.h, b_fold + t + 1)
                        sy.dma_start(out=x_out[row0 + t * 128:row0 + (t + 1) * 128, :],
                                     in_=xw[:, (t % 2) * D:(t % 2 + 1) * D]
                                     ).then_inc(*s_xo.inc(16))
                    if stage_xln:
                        sy.wait_ge(s_lnN.h, b_lnN + t + 1)
                        sy.dma_start(out=xln_dram[row0 + t * 128:row0 + (t + 1) * 128, :],
                                     in_=xln[:, (t % 2) * D:(t % 2 + 1) * D]
                                     ).then_inc(*s_xst.inc(16))

            if combine:
                @blk.gpsimd
                def _(g):
                    for t in range(ntiles):
                        if t >= 1:
                            g.wait_ge(s_fold.h, b_fold + t)
                        for k in range(4):
                            col = (tile0 + t) * 4 + k
                            dst = pob[:, k * D:(k + 1) * D]
                            g.indirect_dma_start(
                                out=dst.bitcast(F32), out_offset=None,
                                in_=po_dram.bitcast(F32),
                                in_offset=IndirectOffsetOnAxis(
                                    dtk_sb[:, col:col + 1], 0)
                                ).then_inc(*s_pog.inc(16))

            @blk.vector
            def _(v):
                v.wait_ge(s_cboot.h, s_cboot.n)
                for t in range(ntiles):
                    sl = slice((t % 2) * D, (t % 2 + 1) * D)
                    v.wait_ge(s_x.h, b_x + per_t * (t + 1))
                    cur = xt[:, sl]
                    if have_fold:
                        if t >= 2:
                            if ln:
                                v.wait_ge(s_scA.h, b_scA + 2 * (t - 2) + 1)
                            if writeback:
                                v.wait_ge(s_xo.h, b_xo + 16 * (t - 1))
                        if combine:
                            v.tensor_add(xw[:, sl], cur, fbuf[:, sl])
                            v.wait_ge(s_pog.h, b_pog + 64 * (t + 1))
                            ins = None
                            for k in range(4):
                                col = (tile0 + t) * 4 + k
                                pk = pob[:, k * D:(k + 1) * D]
                                ins = v.scalar_tensor_tensor(
                                    out=xw[:, sl], in0=pk,
                                    scalar=wtk_sb[:, col:col + 1],
                                    in1=xw[:, sl], op0=OP.mult, op1=OP.add)
                            ins.then_inc(*s_fold.inc())
                        else:
                            v.tensor_add(xw[:, sl], cur, fbuf[:, sl]
                                         ).then_inc(*s_fold.inc())
                        cur = xw[:, sl]
                    if not ln:
                        continue
                    v.reduce_sum(stat[:, 0:1], cur, axis=AX)
                    v.drain()
                    v.tensor_scalar(stat[:, 1:2], stat[:, 0:1], -1.0 / D, None, OP.mult)
                    v.wait_ge(s_scA.h, b_scA + 2 * t + 1)
                    v.tensor_scalar(stat[:, 2:3], stat[:, 4:5], 1.0 / D, None, OP.mult)
                    v.drain()
                    v.tensor_mul(stat[:, 3:4], stat[:, 1:2], stat[:, 1:2])
                    v.drain()
                    v.tensor_sub(stat[:, 2:3], stat[:, 2:3], stat[:, 3:4])
                    v.drain()
                    v.tensor_scalar(stat[:, 2:3], stat[:, 2:3], 1e-5, None, OP.add)
                    v.drain().then_inc(*s_lnA.inc())
                    v.wait_ge(s_scA.h, b_scA + 2 * t + 2)
                    v.reciprocal(stat[:, 6:7], stat[:, 5:6])
                    v.drain()
                    if t >= 2:
                        v.wait_ge(s_mmT.h, b_mmT + (t - 1))
                        if stage_xln:
                            v.wait_ge(s_xst.h, b_xst + 16 * (t - 1))
                    v.tensor_scalar(xln[:, sl], cur, stat[:, 1:2], stat[:, 6:7],
                                    OP.add, OP.mult).then_inc(*s_lnN.inc())
                    if t >= 1:
                        emit_evac(v, t - 1)
                if ln:
                    emit_evac(v, ntiles - 1)

            if not ln:
                return

            @blk.scalar
            def _(sc):
                sc.wait_ge(s_cboot.h, s_cboot.n)
                for t in range(ntiles):
                    sl = slice((t % 2) * D, (t % 2 + 1) * D)
                    src_t = xw if have_fold else xt
                    if have_fold:
                        sc.wait_ge(s_fold.h, b_fold + t + 1)
                    else:
                        sc.wait_ge(s_x.h, b_x + per_t * (t + 1))
                    sc.activation(out=sq[:], in_=src_t[:, sl], func=AF.Square,
                                  accum_out=stat[:, 4:5])
                    sc.drain().then_inc(*s_scA.inc())
                    sc.wait_ge(s_lnA.h, b_lnA + t + 1)
                    sc.activation(out=stat[:, 5:6], in_=stat[:, 2:3], func=AF.Sqrt)
                    sc.drain().then_inc(*s_scA.inc())

            @blk.tensor
            def _(te):
                te.wait_ge(s_boot.h, 16)
                for t in range(ntiles):
                    te.wait_ge(s_lnN.h, b_lnN + t + 1)
                    if t >= 2:
                        te.wait_ge(s_evT.h, b_evT + (t - 1))
                    g0 = (t % 2) * tb
                    sl0 = (t % 2) * D
                    ins = None
                    for c in range(DC):
                        ins = te.matmul(
                            out=bank(g0 + (c * 128) // 512, 128, (c * 128) % 512),
                            lhsT=xln[:, sl0 + c * 128:sl0 + (c + 1) * 128],
                            rhs=ident[:],
                            start=((c * 128) % 512 == 0),
                            stop=((c * 128) % 512 == 512 - 128 or c == DC - 1))
                    ins.then_inc(*s_mmT.inc())


        def emit_gating(tag, xlnT, span, ntiles, tile0, alloc=None, balloc=None,
                        waits=()):
            alloc = alloc or sbuf
            balloc = balloc or alloc
            pg = balloc(tag + "_pg", [128, 4 * E], F32)
            p_all = balloc(tag + "_pa", [128, ntiles * E], F32)
            gz = balloc(tag + "_gz", [128, 16], F32)
            t8 = balloc(tag + "_t8", [128, 8], F32)
            mk = balloc(tag + "_mk", [128, 4 * E], F32)
            mT = balloc(tag + "_mT", [32, ntiles * 128], BF16)
            cum = balloc(tag + "_cum", [32, ntiles * 128], F32)
            zer = balloc(tag + "_zer", [32, ntiles * 128], BF16)
            dT = balloc(tag + "_dT", [128, 32], F32)
            scr = balloc(tag + "_scr", [128, E + 8], F32)
            b_mmG, b_scG, b_evG, b_rt = s_mmG.n, s_scG.n, s_evG.n, s_rt.n
            GE = 4
            ngrp = (ntiles + GE - 1) // GE
            DUMMY = float(E * CAP)
            W3_ = ntiles * 128

            @blk.sync
            def _(sy):
                sy.wait_ge(s_xg.h, s_xg.n)
                sy.dma_start(out=tok_dram.rearrange("(q s) o -> q (s o)", q=128),
                             in_=tok_pad_in).then_inc(*s_tok.inc(16))

            @blk.tensor
            def _(te):
                te.wait_ge(s_w.h, 48)
                te.wait_ge(s_evT.h, s_evT.n)
                for g in range(ngrp):
                    tl = list(range(g * GE, min((g + 1) * GE, ntiles)))
                    if g >= 2:
                        te.wait_ge(s_scG.h, b_scG + (g - 1))
                    bk6 = 6 + (g % 2)
                    ins = None
                    for t in tl:
                        for c in range(DC):
                            ins = te.matmul(
                                out=bank(bk6, E, (t % GE) * E),
                                lhsT=xlnT[:, c * span + t * 128:c * span + (t + 1) * 128],
                                rhs=wg_sb[:, c * E:(c + 1) * E],
                                start=(t == tl[0] and c == 0),
                                stop=(t == tl[-1] and c == DC - 1))
                    ins.then_inc(*s_mmG.inc())

            @blk.vector
            def _(v):
                for sm, val in waits:
                    v.wait_ge(sm.h, val)
                v.memset(zer[:], 0.0)
                for g in range(ngrp):
                    tl = list(range(g * GE, min((g + 1) * GE, ntiles)))
                    bk6 = 6 + (g % 2)
                    v.wait_ge(s_mmG.h, b_mmG + g + 1)
                    for t in tl:
                        j = t % GE
                        v.reduce_max(gz[:, j:j + 1], bank(bk6, E, j * E),
                                     axis=AX, negate=True)
                    v.drain().then_inc(*s_evG.inc())
                    v.wait_ge(s_scG.h, b_scG + g + 1)
                    for t in tl:
                        j = t % GE
                        v.reciprocal(gz[:, 4 + j:5 + j], gz[:, 4 + j:5 + j])
                    v.drain()
                    for t in tl:
                        j = t % GE
                        gt = tile0 + t
                        v.tensor_scalar(p_all[:, t * E:(t + 1) * E],
                                        pg[:, j * E:(j + 1) * E],
                                        gz[:, 4 + j:5 + j], None, OP.mult)
                        v.drain()
                        v.max(out=t8[:], in_=p_all[:, t * E:(t + 1) * E])
                        v.drain()
                        v.tensor_copy(wtk_sb[:, gt * 4:gt * 4 + 4], t8[:, 0:4])
                    v.drain().then_inc(*s_evG.inc())

            @blk.scalar
            def _(sc):
                for g in range(ngrp):
                    tl = list(range(g * GE, min((g + 1) * GE, ntiles)))
                    bk6 = 6 + (g % 2)
                    sc.wait_ge(s_evG.h, b_evG + 2 * g + 1)
                    for t in tl:
                        j = t % GE
                        sc.activation(out=pg[:, j * E:(j + 1) * E],
                                      in_=bank(bk6, E, j * E), func=AF.Exp,
                                      bias=gz[:, j:j + 1], scale=1.0,
                                      accum_out=gz[:, 4 + j:5 + j])
                    sc.drain().then_inc(*s_scG.inc())

            vA_end = s_evG.n
            b1_ev = s_evG.n
            b1_mm = s_mmG.n

            @blk.vector
            def _(v):
                nonlocal_marker = None
                for g in range(ngrp):
                    tl = list(range(g * GE, min((g + 1) * GE, ntiles)))
                    for t in tl:
                        j = t % GE
                        gt = tile0 + t
                        if g >= 1:
                            v.wait_ge(s_mmG.h, b1_mm + g)
                        v.tensor_scalar(mk[:, j * E:(j + 1) * E],
                                        p_all[:, t * E:(t + 1) * E],
                                        wtk_sb[:, gt * 4 + 3:gt * 4 + 4],
                                        None, OP.is_ge)
                    v.drain().then_inc(*s_evG.inc())
                    v.wait_ge(s_mmG.h, b1_mm + g + 1)
                    v.tensor_copy(out=mT[0:E, g * GE * 128:(g * GE + len(tl)) * 128],
                                  in_=PS[0:E, 0:len(tl) * 128]).then_inc(*s_evG.inc())

            @blk.tensor
            def _(te):
                for g in range(ngrp):
                    tl = list(range(g * GE, min((g + 1) * GE, ntiles)))
                    te.wait_ge(s_evG.h, b1_ev + 2 * g + 1)
                    if g >= 1:
                        te.wait_ge(s_evG.h, b1_ev + 2 * g)
                    ins = None
                    for t in tl:
                        j = t % GE
                        ins = te.matmul(out=PS[0:E, j * 128:(j + 1) * 128],
                                        lhsT=mk[:, j * E:(j + 1) * E],
                                        rhs=identf[:],
                                        start=(j == 0), stop=(t == tl[-1]))
                    ins.then_inc(*s_mmG.inc())

            b2_ev = s_evG.n

            @blk.vector
            def _(v):
                v.wait_ge(s_evG.h, b2_ev)
                v.tensor_tensor_scan(out=cum[0:E, :], data0=mT[0:E, :],
                                     data1=zer[0:E, :], initial=0.0,
                                     op0=OP.add, op1=OP.add)
                v.drain()
                v.tensor_scalar(mT[0:E, :], cum[0:E, :], float(CAP), None, OP.is_le)
                v.tensor_scalar(cum[0:E, :], cum[0:E, :], eoff_sb[0:E, :],
                                None, OP.add)
                v.tensor_scalar(cum[0:E, :], cum[0:E, :], -DUMMY, None, OP.add)
                v.tensor_mul(cum[0:E, :], cum[0:E, :], mT[0:E, :])
                v.tensor_scalar(cum[0:E, :], cum[0:E, :], DUMMY, None, OP.add)
                v.drain().then_inc(*s_evG.inc())

            b3_ev = s_evG.n
            b3_mm = s_mmG.n

            @blk.tensor
            def _(te):
                te.wait_ge(s_evG.h, b3_ev)
                for t in range(ntiles):
                    if t >= 1:
                        te.wait_ge(s_evG.h, b3_ev + t)
                    te.matmul(out=PS[:, 512:512 + 32],
                              lhsT=cum[0:E, t * 128:(t + 1) * 128],
                              rhs=identf[0:E, 0:32],
                              start=True, stop=True).then_inc(*s_mmG.inc())

            @blk.vector
            def _(v):
                for t in range(ntiles):
                    gt = tile0 + t
                    v.wait_ge(s_mmG.h, b3_mm + t + 1)
                    v.tensor_copy(out=dT[:], in_=PS[:, 512:512 + 32])
                    v.drain()
                    ins = None
                    for k in range(4):
                        v.tensor_scalar(scr[:, 0:E], p_all[:, t * E:(t + 1) * E],
                                        wtk_sb[:, gt * 4 + k:gt * 4 + k + 1],
                                        None, OP.is_equal)
                        v.drain()
                        v.scalar_tensor_tensor(
                            out=scr[:, 0:E], in0=scr[:, 0:E], scalar=1.0,
                            in1=dT[:, 0:E], op0=OP.mult, op1=OP.mult,
                            accum_out=scr[:, E + k:E + k + 1])
                        v.drain()
                        v.tensor_copy(dtk_sb[:, gt * 4 + k:gt * 4 + k + 1],
                                      scr[:, E + k:E + k + 1])
                        v.tensor_copy(scri[:, 0:1], scr[:, E + k:E + k + 1])
                        v.drain()
                        v.tensor_scalar(scri[:, 1:2], scri[:, 0:1], 7, None,
                                        OP.logical_shift_right)
                        v.tensor_scalar(scri[:, 2:3], scri[:, 0:1], 127, None,
                                        OP.bitwise_and)
                        v.drain()
                        v.tensor_scalar(scri[:, 2:3], scri[:, 2:3], WT, None,
                                        OP.mult)
                        v.drain()
                        v.tensor_add(scri[:, 3:4], scri[:, 2:3], scri[:, 1:2])
                        v.drain()
                        v.tensor_copy(dtk2_sb[:, gt * 4 + k:gt * 4 + k + 1],
                                      scri[:, 3:4])
                    v.drain().then_inc(*s_evG.inc())

            @blk.gpsimd
            def _(g):
                g.wait_ge(s_tok.h, s_tok.n)
                for t in range(ntiles):
                    gt = tile0 + t
                    g.wait_ge(s_evG.h, b3_ev + t + 1)
                    for k in range(4):
                        g.indirect_dma_start(
                            out=tok_dram, out_offset=IndirectOffsetOnAxis(
                                dtk2_sb[:, gt * 4 + k:gt * 4 + k + 1], 0),
                            in_=iota_tok[:, gt:gt + 1], in_offset=None
                            ).then_inc(*s_rt.inc(16))

        def emit_swiglu(tag, xlnT, span, w1sb, w3sb, w2sb, hxc, y_sb, hg, hgl,
                        wait_w=None, extra_waits=()):
            b_mmH, b_scH, b_evH = s_mmH.n, s_scH.n, s_evH.n
            b_mmO, b_evO = s_mmO.n, s_evO.n
            HW = hxc * 128

            @blk.tensor
            def _(te):
                if wait_w is not None:
                    te.wait_ge(s_w.h, wait_w)
                te.wait_ge(s_evT.h, s_evT.n)
                for sm, val in extra_waits:
                    te.wait_ge(sm.h, val)
                for b in range(NTB):
                    col = b * TB
                    for hc in range(hxc):
                        idx = b * hxc + hc
                        if idx >= 2:
                            te.wait_ge(s_evH.h, b_evH + idx - 1)
                        bh1, bh3 = idx % 2, 2 + idx % 2
                        for c in range(DC):
                            te.matmul(out=bank(bh1, TB),
                                      lhsT=w1sb[:, c * HW + hc * 128:c * HW + (hc + 1) * 128],
                                      rhs=xlnT[:, c * span + col:c * span + col + TB],
                                      start=(c == 0), stop=(c == DC - 1))
                        ins = None
                        for c in range(DC):
                            ins = te.matmul(out=bank(bh3, TB),
                                            lhsT=w3sb[:, c * HW + hc * 128:c * HW + (hc + 1) * 128],
                                            rhs=xlnT[:, c * span + col:c * span + col + TB],
                                            start=(c == 0), stop=(c == DC - 1))
                        ins.then_inc(*s_mmH.inc())
                    te.wait_ge(s_evH.h, b_evH + (b + 1) * hxc)
                    for tc in range(TBC):
                        for n in range(NB2):
                            oidx = b * TBC * NB2 + tc * NB2 + n
                            bo = 4 + oidx % 2
                            if oidx >= 2:
                                te.wait_ge(s_evO.h, b_evO + oidx - 1)
                            ins = None
                            for k in range(hxc):
                                ins = te.matmul(
                                    out=bank(bo, OB),
                                    lhsT=hgl[:, k * TB + tc * 128:k * TB + (tc + 1) * 128],
                                    rhs=w2sb[:, k * D + n * OB:k * D + (n + 1) * OB],
                                    start=(k == 0), stop=(k == hxc - 1))
                            ins.then_inc(*s_mmO.inc())

            @blk.scalar
            def _(sc):
                for b in range(NTB):
                    for hc in range(hxc):
                        idx = b * hxc + hc
                        sc.wait_ge(s_mmH.h, b_mmH + idx + 1)
                        sc.activation(out=hg[:, (idx % 2) * TB:(idx % 2) * TB + TB],
                                      in_=bank(idx % 2, TB), func=AF.Silu
                                      ).then_inc(*s_scH.inc())

            @blk.vector
            def _(v):
                for b in range(NTB):
                    for hc in range(hxc):
                        idx = b * hxc + hc
                        v.wait_ge(s_scH.h, b_scH + idx + 1)
                        v.tensor_mul(hgl[:, hc * TB:hc * TB + TB],
                                     bank(2 + idx % 2, TB),
                                     hg[:, (idx % 2) * TB:(idx % 2) * TB + TB]
                                     ).then_inc(*s_evH.inc())
                    for tc in range(TBC):
                        for n in range(NB2):
                            oidx = b * TBC * NB2 + tc * NB2 + n
                            bo = 4 + oidx % 2
                            v.wait_ge(s_mmO.h, b_mmO + oidx + 1)
                            gt = b * TBC + tc
                            ysl = y_sb[:, gt * D + n * OB:gt * D + (n + 1) * OB]
                            v.tensor_scalar(ysl, bank(bo, OB), 0.5, None, OP.mult
                                            ).then_inc(*s_evO.inc())

        def emit_expert_routed(e, w1sb, w3sb, w2sb, tok_sb, xg, xeT, po_sb,
                               hg, hgl, wait_w, free_sem, b_tok, sweep,
                               prefetch=None):
            b_xg, b_mmT, b_evT = s_xg.n, s_mmT.n, s_evT.n
            b_mmH, b_scH, b_evH = s_mmH.n, s_scH.n, s_evH.n
            b_mmO, b_evO, b_po = s_mmO.n, s_evO.n, s_po.n
            PC = CAP // 128
            NHX = NPB * HC

            @blk.gpsimd
            def _(g):
                g.wait_ge(s_tok.h, b_tok)
                for gc in range(GCH):
                    gci = e * GCH + gc
                    if gci >= 2:
                        g.wait_ge(s_mmT.h, sweep["mmT"] + gci - 1)
                    g.indirect_dma_start(
                        out=xg[:, (gc % 2) * D:(gc % 2 + 1) * D].bitcast(F32),
                        out_offset=None,
                        in_=xln_dram.bitcast(F32),
                        in_offset=IndirectOffsetOnAxis(
                            tok_sb[:, e * GCH + gc:e * GCH + gc + 1], 0)
                        ).then_inc(*s_xg.inc(16))
                if prefetch is not None:
                    prefetch(g)

            @blk.tensor
            def _(te):
                te.wait_ge(s_w.h, wait_w)
                for gc in range(GCH):
                    gci = e * GCH + gc
                    te.wait_ge(s_xg.h, b_xg + 16 * (gc + 1))
                    if gci >= 1:
                        te.wait_ge(s_evT.h, sweep["evT"] + gci)
                    ins = None
                    for c in range(DC):
                        ins = te.matmul(
                            out=bank(6 + c // 4, 128, (c % 4) * 128),
                            lhsT=xg[:, (gc % 2) * D + c * 128:
                                    (gc % 2) * D + (c + 1) * 128],
                            rhs=ident[:],
                            start=(c % 4 == 0), stop=(c % 4 == 3 or c == DC - 1))
                    ins.then_inc(*s_mmT.inc())
                for pb in range(NPB):
                    for hc in range(HC):
                        idx = pb * HC + hc
                        ghx = e * NHX + idx
                        te.wait_ge(s_evH.h, sweep["evH"] + max(ghx - 1, 0))
                        te.wait_ge(s_evT.h, b_evT + GCH)
                        bh1, bh3 = idx % 2, 2 + idx % 2
                        for c in range(DC):
                            te.matmul(out=bank(bh1, PB),
                                      lhsT=w1sb[:, c * H + hc * 128:c * H + (hc + 1) * 128],
                                      rhs=xeT[:, c * CAP + pb * PB:c * CAP + (pb + 1) * PB],
                                      start=(c == 0), stop=(c == DC - 1))
                        ins = None
                        for c in range(DC):
                            ins = te.matmul(out=bank(bh3, PB),
                                            lhsT=w3sb[:, c * H + hc * 128:c * H + (hc + 1) * 128],
                                            rhs=xeT[:, c * CAP + pb * PB:c * CAP + (pb + 1) * PB],
                                            start=(c == 0), stop=(c == DC - 1))
                        ins.then_inc(*s_mmH.inc())
                te.wait_ge(s_evH.h, sweep["evH"] + (e + 1) * NHX)
                for pc in range(PC):
                    for n in range(NB2):
                        oidx = pc * NB2 + n
                        gox = e * PC * NB2 + oidx
                        gpc = e * PC + pc
                        bo = 4 + oidx % 2
                        te.wait_ge(s_evO.h, sweep["evO"] + max(gox - 1, 0))
                        if gpc >= 2:
                            te.wait_ge(s_po.h, sweep["po"] + 16 * (gpc - 1))
                        ins = None
                        for k in range(HC):
                            ins = te.matmul(
                                out=bank(bo, OB),
                                lhsT=hgl[:, k * CAP + pc * 128:k * CAP + (pc + 1) * 128],
                                rhs=w2sb[:, k * D + n * OB:k * D + (n + 1) * OB],
                                start=(k == 0), stop=(k == HC - 1))
                        ins.then_inc(*s_mmO.inc())

            @blk.vector
            def _(v):
                for gc in range(GCH):
                    v.wait_ge(s_mmT.h, b_mmT + gc + 1)
                    out_ap = AP(tensor=xeT, offset=gc * 128,
                                ap=[[xeT.shape[1], 128], [CAP, DC], [1, 128]])
                    in_ap = AP(tensor=PS, offset=6 * 512,
                               ap=[[4096, 128], [128, DC], [1, 128]])
                    v.tensor_copy(out=out_ap, in_=in_ap).then_inc(*s_evT.inc())
                for pb in range(NPB):
                    for hc in range(HC):
                        idx = pb * HC + hc
                        v.wait_ge(s_scH.h, b_scH + idx + 1)
                        v.tensor_mul(hgl[:, hc * CAP + pb * PB:hc * CAP + pb * PB + PB],
                                     bank(2 + idx % 2, PB),
                                     hg[:, (idx % 2) * PB:(idx % 2) * PB + PB]
                                     ).then_inc(*s_evH.inc())
                ins = None
                for pc in range(PC):
                    for n in range(NB2):
                        oidx = pc * NB2 + n
                        gpc = e * PC + pc
                        bo = 4 + oidx % 2
                        v.wait_ge(s_mmO.h, b_mmO + oidx + 1)
                        if gpc >= 2 and n == 0:
                            v.wait_ge(s_po.h, sweep["po"] + 16 * (gpc - 1))
                        ins = v.tensor_scalar(
                            po_sb[:, (pc % 2) * D + n * OB:(pc % 2) * D + (n + 1) * OB],
                            bank(bo, OB), 0.5, None, OP.mult)
                        ins.then_inc(*s_evO.inc())
                if free_sem is not None:
                    v.drain().then_inc(*free_sem.inc())

            @blk.scalar
            def _(sc):
                for pb in range(NPB):
                    for hc in range(HC):
                        idx = pb * HC + hc
                        sc.wait_ge(s_mmH.h, b_mmH + idx + 1)
                        sc.activation(out=hg[:, (idx % 2) * PB:(idx % 2) * PB + PB],
                                      in_=bank(idx % 2, PB), func=AF.Silu
                                      ).then_inc(*s_scH.inc())

            @blk.sync
            def _(sy):
                for pc in range(PC):
                    sy.wait_ge(s_evO.h, b_evO + pc * NB2 + NB2)
                    sy.dma_start(
                        out=po_dram[e * CAP + pc * 128:e * CAP + (pc + 1) * 128, :],
                        in_=po_sb[:, (pc % 2) * D:(pc % 2 + 1) * D]
                        ).then_inc(*s_po.inc(16))

        def emit_ffn(tag, src, fold_dram, writeback, combine):
            with ExitStack() as ph:
                psb = lambda n, s2, d: ph.enter_context(nc.sbuf_tensor(tag + n, s2, d))
                y_sb = psb("_y", [128, NT * D], BF16)
                hg = psb("_hg", [128, 2 * max(TB, PB)], BF16)
                hgl = psb("_hgl", [128, max(HsC * TB, HC * CAP)], BF16)
                with ExitStack() as st_a:
                    asb = lambda n, s2, d: st_a.enter_context(
                        nc.sbuf_tensor(tag + n, s2, d))
                    xlnT = asb("_xlnT", [128, DC * T], BF16)
                    st_l = ExitStack()
                    lsb = lambda n, s2, d: st_l.enter_context(
                        nc.sbuf_tensor(tag + n, s2, d))
                    emit_ln_pass(tag + "_ln", src, 0, NT, 0, fold_dram, writeback,
                                 xlnT, T, alloc=lsb, stage_xln=True,
                                 combine=combine)
                    ln_marks = [(s_xo, s_xo.n), (s_xst, s_xst.n),
                                (s_scA, s_scA.n), (s_fold, s_fold.n),
                                (s_lnN, s_lnN.n)]
                    st_l.close()
                    st_g = ExitStack()
                    gsb = lambda n, s2, d: st_g.enter_context(
                        nc.sbuf_tensor(tag + n, s2, d))
                    emit_gating(tag + "_g", xlnT, T, NT, 0, alloc=asb,
                                balloc=gsb, waits=ln_marks)
                    rt_marks = [(s_evG, s_evG.n), (s_scG, s_scG.n),
                                (s_mmG, s_mmG.n)]
                    st_g.close()
                    if DBG and tag == "f1":
                        @blk.sync
                        def _(sy):
                            sy.wait_ge(s_evT.h, s_evT.n)
                            sy.dma_start(out=dbg_xln, in_=xlnT[:]
                                         ).then_inc(*s_xo.inc(16))
                    ws1 = asb("_ws1", [128, DC * Hs], BF16)
                    ws3 = asb("_ws3", [128, DC * Hs], BF16)
                    ws2 = asb("_ws2", [128, HsC * D], BF16)
                    b_w = s_w.n

                    @blk.gpsimd
                    def _(g):
                        for sm, val in rt_marks:
                            g.wait_ge(sm.h, val)
                        g.dma_start(out=ws1[:].rearrange("p (c h) -> p c h", c=DC),
                                    in_=Ws1_in.rearrange("(c p) h -> p c h", p=128)
                                    ).then_inc(*s_w.inc(16))
                        g.dma_start(out=ws3[:].rearrange("p (c h) -> p c h", c=DC),
                                    in_=Ws3_in.rearrange("(c p) h -> p c h", p=128)
                                    ).then_inc(*s_w.inc(16))
                        g.dma_start(out=ws2[:].rearrange("p (c h) -> p c h", c=HsC),
                                    in_=Ws2_in.rearrange("(c p) h -> p c h", p=128)
                                    ).then_inc(*s_w.inc(16))

                    emit_swiglu(tag + "_sh", xlnT, T, ws1, ws3, ws2, HsC, y_sb,
                                hg, hgl, wait_w=b_w + 48, extra_waits=rt_marks)
                shared_mmO = s_mmO.n
                sweep_marks = [(s_mmO, s_mmO.n), (s_scG, s_scG.n)]
                with ExitStack() as st_b:
                    bsb = lambda n, s2, d: st_b.enter_context(
                        nc.sbuf_tensor(tag + n, s2, d))
                    tok_sb = bsb("_tok", [128, E * GCH], U32)
                    xg = bsb("_xg", [128, 2 * D], BF16)
                    xeT = bsb("_xeT", [128, DC * CAP], BF16)
                    po_sb = bsb("_po2", [128, 2 * D], BF16)
                    w1e = bsb("_w1e", [128, 2 * DC * H], BF16)
                    w3e = bsb("_w3e", [128, 2 * DC * H], BF16)
                    w2e = bsb("_w2e", [128, 2 * HC * D], BF16)
                    EH = DC * H
                    b_w2 = s_w.n
                    b_wf = s_wf.n
                    b_rt_all = s_rt.n

                    @blk.sync
                    def _(sy):
                        sy.wait_ge(s_rt.h, b_rt_all)
                        sy.wait_ge(s_mmO.h, shared_mmO)
                        sy.wait_ge(s_evO.h, s_evO.n)
                        sy.dma_start(out=tok_sb[:],
                                     in_=tok_dram.rearrange("(q s) o -> q (s o)", q=128)
                                     [:, 0:E * GCH]).then_inc(*s_tok.inc(16))
                    b_tok = s_tok.n

                    load_tgt = {}

                    def load_expert(g, e):
                        sl1 = slice((e % 2) * EH, (e % 2 + 1) * EH)
                        sl2 = slice((e % 2) * HC * D, (e % 2 + 1) * HC * D)
                        g.dma_start(out=w1e[:, sl1].rearrange("p (c h) -> p c h", c=DC),
                                    in_=W1_in[e].rearrange("(c p) h -> p c h", p=128)
                                    ).then_inc(*s_w.inc(16))
                        g.dma_start(out=w3e[:, sl1].rearrange("p (c h) -> p c h", c=DC),
                                    in_=W3_in[e].rearrange("(c p) h -> p c h", p=128)
                                    ).then_inc(*s_w.inc(16))
                        g.dma_start(out=w2e[:, sl2].rearrange("p (c h) -> p c h", c=HC),
                                    in_=W2_in[e].rearrange("(c p) h -> p c h", p=128)
                                    ).then_inc(*s_w.inc(16))
                        load_tgt[e] = s_w.n

                    @blk.gpsimd
                    def _(g):
                        g.wait_ge(s_mmO.h, shared_mmO)
                        load_expert(g, 0)
                        if E > 1:
                            load_expert(g, 1)

                    first = True
                    sweep = dict(mmT=s_mmT.n, evT=s_evT.n, evH=s_evH.n,
                                 evO=s_evO.n, po=s_po.n)
                    for e in range(E):
                        if first:
                            @blk.tensor
                            def _(te):
                                for sm, val in sweep_marks:
                                    te.wait_ge(sm.h, val)
                            first = False
                        if 1 <= e < E - 1:
                            def prefetch(g, e=e):
                                g.wait_ge(s_wf.h, b_wf + e)
                                load_expert(g, e + 1)
                        else:
                            prefetch = None
                        sl1 = slice((e % 2) * EH, (e % 2 + 1) * EH)
                        sl2 = slice((e % 2) * HC * D, (e % 2 + 1) * HC * D)
                        emit_expert_routed(e, w1e[:, sl1], w3e[:, sl1],
                                           w2e[:, sl2], tok_sb, xg, xeT, po_sb,
                                           hg, hgl, load_tgt[e], s_wf, b_tok,
                                           sweep, prefetch=prefetch)
                    if DBG and tag == "f1":
                        @blk.sync
                        def _(sy):
                            sy.wait_ge(s_wf.h, b_wf + E)
                            sy.dma_start(out=dbg_toksb, in_=tok_sb[:]
                                         ).then_inc(*s_xo.inc(16))
                            sy.dma_start(out=dbg_xg, in_=xg[:]
                                         ).then_inc(*s_xo.inc(16))
                            sy.dma_start(out=dbg_xe, in_=xeT[:]
                                         ).then_inc(*s_xo.inc(16))
                @blk.sync
                def _(sy):
                    sy.wait_ge(s_evO.h, s_evO.n)
                    sy.dma_start(out=y_dram.rearrange("(t p) d -> p t d", p=128),
                                 in_=y_sb[:].rearrange("p (t d) -> p t d", d=D)
                                 ).then_inc(*s_xo.inc(16))
                barrier()

        def emit_attn():
            for sq_ in range(B_loc):
                with ExitStack() as ph:
                    psb = lambda n, sh, d: ph.enter_context(
                        nc.sbuf_tensor(f"at{sq_}{n}", sh, d))
                    bv_sb = psb("_bv", [128, D], F32)

                    @blk.sync
                    def _(sy):
                        sy.dma_start(out=bv_sb[:], in_=bv_in).then_inc(*s_boot.inc(16))
                    qT = psb("_qT", [128, DC * S], BF16)
                    kT = psb("_kT", [128, DC * S], BF16)
                    v_sb = psb("_v", [128, ST * D], BF16)
                    st1 = ph.enter_context(ExitStack())
                    s1b = lambda n, sh, d: st1.enter_context(
                        nc.sbuf_tensor(f"at{sq_}{n}", sh, d))
                    xlnT = s1b("_xlnT", [128, DC * S], BF16)
                    with ExitStack() as st2:
                        s2b = lambda n, sh, d: st2.enter_context(
                            nc.sbuf_tensor(f"at{sq_}{n}", sh, d))
                        emit_ln_pass(f"at{sq_}_ln", x_in, sq_ * S, ST, sq_ * ST,
                                     y_dram, True, xlnT, S, alloc=s2b,
                                     combine=True)
                        barrier()
                    wpj = s1b("_wpj", [128, DC * D], BF16)
                    b_w = s_w.n
                    b_mmP, b_evP = s_mmP.n, s_evP.n
                    per = DC * SB2

                    @blk.gpsimd
                    def _(g):
                        for j, wt in enumerate((WqT_in, WkT_in, WvT_in)):
                            if j >= 1:
                                g.wait_ge(s_evP.h, b_evP + j * per)
                            g.dma_start(
                                out=wpj[:].rearrange("p (c h) -> p c h", c=DC),
                                in_=wt.rearrange("(c p) h -> p c h", p=128)
                                ).then_inc(*s_w.inc(16))

                    @blk.tensor
                    def _(te):
                        for j in range(3):
                            te.wait_ge(s_w.h, b_w + 16 * (j + 1))
                            wsl = 0
                            if j < 2:
                                for mc in range(DC):
                                    for n in range(SB2):
                                        idx = j * per + mc * SB2 + n
                                        bo = idx % 4
                                        if idx >= 4:
                                            te.wait_ge(s_evP.h, b_evP + idx - 3)
                                        ins = None
                                        for c in range(DC):
                                            ins = te.matmul(
                                                out=bank(bo, SBW),
                                                lhsT=wpj[:, wsl + c * D + mc * 128:
                                                         wsl + c * D + (mc + 1) * 128],
                                                rhs=xlnT[:, c * S + n * SBW:
                                                         c * S + (n + 1) * SBW],
                                                start=(c == 0), stop=(c == DC - 1))
                                        ins.then_inc(*s_mmP.inc())
                            else:
                                for tc in range(ST):
                                    for n in range(NB2):
                                        idx = 2 * per + tc * NB2 + n
                                        bo = idx % 4
                                        if idx >= 4:
                                            te.wait_ge(s_evP.h, b_evP + idx - 3)
                                        ins = None
                                        for c in range(DC):
                                            ins = te.matmul(
                                                out=bank(bo, OB),
                                                lhsT=xlnT[:, c * S + tc * 128:
                                                          c * S + (tc + 1) * 128],
                                                rhs=wpj[:, wsl + c * D + n * OB:
                                                         wsl + c * D + (n + 1) * OB],
                                                start=(c == 0), stop=(c == DC - 1))
                                        ins.then_inc(*s_mmP.inc())

                    @blk.vector
                    def _(v):
                        for j in range(3):
                            if j < 2:
                                bias = bq_sb if j == 0 else bk_sb
                                for mc in range(DC):
                                    for n in range(SB2):
                                        idx = j * per + mc * SB2 + n
                                        bo = idx % 4
                                        v.wait_ge(s_mmP.h, b_mmP + idx + 1)
                                        dst = (qT if j == 0 else kT)[
                                            :, mc * S + n * SBW:mc * S + (n + 1) * SBW]
                                        if j == 0:
                                            ins = v.tensor_scalar(
                                                dst, bank(bo, SBW), bias[:, mc:mc + 1],
                                                0.125, OP.add, OP.mult)
                                        else:
                                            ins = v.tensor_scalar(
                                                dst, bank(bo, SBW), bias[:, mc:mc + 1],
                                                None, OP.add)
                                        ins.then_inc(*s_evP.inc())
                            else:
                                for tc in range(ST):
                                    for n in range(NB2):
                                        idx = 2 * per + tc * NB2 + n
                                        bo = idx % 4
                                        v.wait_ge(s_mmP.h, b_mmP + idx + 1)
                                        v.tensor_add(
                                            v_sb[:, tc * D + n * OB:tc * D + (n + 1) * OB],
                                            bank(bo, OB), bv_sb[:, n * OB:(n + 1) * OB]
                                            ).then_inc(*s_evP.inc())
                    barrier()
                    st1.close()

                    ctx_seq = psb("_ctx", [128, ST * D], BF16)
                    relkT = psb("_relkT", [128, JT * IQ], BF16)
                    kd_sb = psb("_kd", [128, 2 * WK], BF16)
                    qdb = psb("_qdb", [128, 4 * WQ], BF16)
                    rq_sb = psb("_rq", [128, 8 * JB], BF16)
                    Pb = psb("_Pb", [128, S], BF16)
                    PT = psb("_PT", [128, 1024], BF16)
                    zp = psb("_zp", [128, 2 * NJB + 4], F32)

                    for h in range(NH):
                        hb = 64 * (h % 2)
                        hcol = (h // 2) * S
                        for iq in range(NIQ):
                            b_mmK, b_evK, b_shK = s_mmK.n, s_evK.n, s_shK.n

                            @blk.tensor
                            def _(te):
                                te.wait_ge(s_w.h, 32)
                                te.wait_ge(s_evS.h, s_evS.n)
                                for jt in range(JT):
                                    if jt >= 2:
                                        te.wait_ge(s_evK.h, b_evK + jt - 1)
                                    s0 = C0 + iq * IQ - jt * 128 - 127
                                    g0 = (jt % 2) * 2
                                    w1k = min(512, WK)
                                    ins = te.matmul(out=bank(g0, w1k),
                                              lhsT=kT[hb:hb + 64,
                                                      hcol + jt * 128:hcol + (jt + 1) * 128],
                                              rhs=deT[hb:hb + 64, s0:s0 + w1k],
                                              start=True, stop=True)
                                    if WK > 512:
                                        ins = te.matmul(out=bank(g0 + 1, WK - 512),
                                              lhsT=kT[hb:hb + 64,
                                                      hcol + jt * 128:hcol + (jt + 1) * 128],
                                              rhs=deT[hb:hb + 64, s0 + 512:s0 + WK],
                                              start=True, stop=True)
                                    ins.then_inc(*s_mmK.inc())

                            @blk.vector
                            def _(v):
                                for jt in range(JT):
                                    v.wait_ge(s_mmK.h, b_mmK + jt + 1)
                                    if jt >= 2:
                                        v.wait_ge(s_shK.h, b_shK + 16 * (jt - 1))
                                    g0 = (jt % 2) * 2
                                    v.tensor_scalar(
                                        kd_sb[:, (jt % 2) * WK:(jt % 2) * WK + WK],
                                        PS[:, g0 * 512:g0 * 512 + WK],
                                        0.125, None, OP.mult).then_inc(*s_evK.inc())

                            @blk.scalar
                            def _(sc):
                                for jt in range(JT):
                                    sc.wait_ge(s_evK.h, b_evK + jt + 1)
                                    diag = AP(tensor=kd_sb, offset=(jt % 2) * WK + 127,
                                              ap=[[2 * WK - 1, 128], [1, IQ]])
                                    sc.dma_start(out=relkT[:, jt * IQ:(jt + 1) * IQ],
                                                 in_=diag).then_inc(*s_shK.inc(16))

                            G4 = 4 if JT % 4 == 0 else JT
                            shK_end = s_shK.n

                            def emit_qd(it, itl):
                                g0 = s_mmS.n
                                evK_end = s_evK.n

                                @blk.tensor
                                def _(te):
                                    for jb in range(NJB):
                                        g = g0 + jb
                                        if itl == 0 and jb < 2:
                                            te.wait_ge(s_evK.h, evK_end)
                                        if g >= 2:
                                            te.wait_ge(s_evS.h, g - 1)
                                        gb = (jb % 2) * 2
                                        s0q = C0 + jb * JB - it * 128 - 127
                                        w1q = min(512, WQ)
                                        ins = te.matmul(out=bank(gb, w1q),
                                                  lhsT=qT[hb:hb + 64,
                                                          hcol + it * 128:hcol + (it + 1) * 128],
                                                  rhs=derT[hb:hb + 64, s0q:s0q + w1q],
                                                  start=True, stop=True)
                                        if WQ > 512:
                                            ins = te.matmul(out=bank(gb + 1, WQ - 512),
                                                  lhsT=qT[hb:hb + 64,
                                                          hcol + it * 128:hcol + (it + 1) * 128],
                                                  rhs=derT[hb:hb + 64, s0q + 512:s0q + WQ],
                                                  start=True, stop=True)
                                        ins.then_inc(*s_mmS.inc())

                                @blk.vector
                                def _(v):
                                    for jb in range(NJB):
                                        g = g0 + jb
                                        v.wait_ge(s_mmS.h, g + 1)
                                        if g >= 2:
                                            v.wait_ge(s_shS.h, 16 * (g - 1))
                                        gb = (jb % 2) * 2
                                        q4 = (it % 2) * 2 + (jb % 2)
                                        v.tensor_copy(
                                            out=qdb[:, q4 * WQ:q4 * WQ + WQ],
                                            in_=PS[:, gb * 512:gb * 512 + WQ]
                                            ).then_inc(*s_evS.inc())

                                @blk.gpsimd
                                def _(g_):
                                    for jb in range(NJB):
                                        g = g0 + jb
                                        g_.wait_ge(s_evS.h, g + 1)
                                        if g >= 2 * NJB:
                                            g_.wait_ge(s_mmU.h, g - 2 * NJB + 1)
                                        q4 = (it % 2) * 2 + (jb % 2)
                                        r8 = (it % 2) * 4 + jb
                                        diag = AP(tensor=qdb, offset=q4 * WQ + 127,
                                                  ap=[[4 * WQ - 1, 128], [1, JB]])
                                        g_.dma_start(
                                            out=rq_sb[:, r8 * JB:(r8 + 1) * JB],
                                            in_=diag).then_inc(*s_shS.inc(16))

                            def emit_su(it, itl):
                                gs0 = s_mmU.n
                                git = s_evZ.n
                                zb = (git % 2) * NJB
                                b_mmX = s_mmX.n
                                b_ptc = s_ptc.n
                                b_evX = s_evX.n

                                @blk.tensor
                                def _(te):
                                    if itl == 0:
                                        te.wait_ge(s_shK.h, shK_end)
                                    for jb in range(NJB):
                                        gs = gs0 + jb
                                        bs = 4 + jb % 2
                                        te.wait_ge(s_shS.h, 16 * (gs + 1))
                                        if gs >= 2:
                                            te.wait_ge(s_scS.h, gs - 1)
                                        r8 = (it % 2) * 4 + jb
                                        te.matmul(out=bank(bs, JB),
                                                  lhsT=qT[hb:hb + 64,
                                                          hcol + it * 128:hcol + (it + 1) * 128],
                                                  rhs=kT[hb:hb + 64,
                                                         hcol + jb * JB:hcol + (jb + 1) * JB],
                                                  start=True, stop=False)
                                        for j2 in range(JB // 128):
                                            jt = jb * (JB // 128) + j2
                                            te.matmul(out=bank(bs, 128, j2 * 128),
                                                      lhsT=relkT[:, jt * IQ + itl * 128:
                                                                 jt * IQ + (itl + 1) * 128],
                                                      rhs=ident[:],
                                                      start=False, stop=False)
                                        te.matmul(out=bank(bs, JB), lhsT=ident[:],
                                                  rhs=rq_sb[:, r8 * JB:(r8 + 1) * JB],
                                                  start=False, stop=True
                                                  ).then_inc(*s_mmU.inc())

                                @blk.scalar
                                def _(sc):
                                    if git >= 2:
                                        sc.wait_ge(s_evZ.h, git - 1)
                                    if git >= 1:
                                        sc.wait_ge(s_mmX.h, b_mmX)
                                    for jb in range(NJB):
                                        gs = gs0 + jb
                                        sc.wait_ge(s_mmU.h, gs + 1)
                                        bs = 4 + jb % 2
                                        sc.activation(out=Pb[:, jb * JB:(jb + 1) * JB],
                                                      in_=bank(bs, JB), func=AF.Exp,
                                                      bias=float(-EXPB), scale=1.0,
                                                      accum_out=zp[:, zb + jb:zb + jb + 1]
                                                      ).then_inc(*s_scS.inc())
                                    sc.drain().then_inc(*s_zpD.inc())

                                @blk.vector
                                def _(v):
                                    v.wait_ge(s_zpD.h, git + 1)
                                    v.reduce_sum(zp[:, 2 * NJB + (git % 2):2 * NJB + (git % 2) + 1],
                                                 zp[:, zb:zb + NJB], axis=AX)
                                    v.drain()
                                    v.reciprocal(zp[:, 2 * NJB + 2 + (git % 2):2 * NJB + 3 + (git % 2)],
                                                 zp[:, 2 * NJB + (git % 2):2 * NJB + (git % 2) + 1])
                                    v.drain().then_inc(*s_evZ.inc())

                                NG = JT // G4
                                b_mmC = s_mmC.n

                                def ctx_mm(te, grp):
                                    if grp == 0:
                                        te.wait_ge(s_evX.h, b_evX)
                                    ins = None
                                    for j2 in range(G4):
                                        jt = grp * G4 + j2
                                        ins = te.matmul(
                                            out=bank(7, 64),
                                            lhsT=PT[:, (grp % 2) * 512 + j2 * 128:
                                                    (grp % 2) * 512 + (j2 + 1) * 128],
                                            rhs=v_sb[:, jt * D + h * 64:jt * D + (h + 1) * 64],
                                            start=(jt == 0), stop=(jt == JT - 1))
                                    ins.then_inc(*s_mmC.inc())

                                @blk.tensor
                                def _(te):
                                    te.wait_ge(s_scS.h, gs0 + NJB)
                                    for grp in range(NG):
                                        if grp >= 1:
                                            te.wait_ge(s_ptc.h, b_ptc + grp)
                                        ins = None
                                        for j2 in range(G4):
                                            jt = grp * G4 + j2
                                            ins = te.matmul(out=bank(6, 128, j2 * 128),
                                                            lhsT=Pb[:, jt * 128:(jt + 1) * 128],
                                                            rhs=ident[:],
                                                            start=(j2 == 0), stop=(j2 == G4 - 1))
                                        ins.then_inc(*s_mmX.inc())
                                        if grp >= 1:
                                            ctx_mm(te, grp - 1)
                                    te.wait_ge(s_ptc.h, b_ptc + NG)
                                    ctx_mm(te, NG - 1)

                                @blk.scalar
                                def _(sc):
                                    for grp in range(NG):
                                        sc.wait_ge(s_mmX.h, b_mmX + grp + 1)
                                        if grp >= 2:
                                            sc.wait_ge(s_mmC.h, b_mmC + grp - 1)
                                        sc.activation(
                                            out=PT[:, (grp % 2) * 512:
                                                   (grp % 2) * 512 + G4 * 128],
                                            in_=bank(6, G4 * 128),
                                            func=AF.Identity, bias=0.0,
                                            scale=1.0).then_inc(*s_ptc.inc())

                                @blk.vector
                                def _(v):
                                    v.wait_ge(s_mmC.h, b_mmC + NG)
                                    v.wait_ge(s_evZ.h, git + 1)
                                    v.tensor_scalar(
                                        ctx_seq[:, it * D + h * 64:it * D + (h + 1) * 64],
                                        bank(7, 64),
                                        zp[:, 2 * NJB + 2 + (git % 2):2 * NJB + 3 + (git % 2)],
                                        None, OP.mult).then_inc(*s_evX.inc())

                            for itl in range(IQC):
                                emit_qd(iq * IQC + itl, itl)
                                emit_su(iq * IQC + itl, itl)

                    @blk.sync
                    def _(sy):
                        sy.wait_ge(s_evX.h, s_evX.n)
                        sy.dma_start(
                            out=ctx_dram[sq_ * S:(sq_ + 1) * S]
                            .rearrange("(t p) d -> p t d", p=128),
                            in_=ctx_seq[:].rearrange("p (t d) -> p t d", d=D)
                            ).then_inc(*s_xo.inc(16))
                    barrier()

        LVL = p.get("LVL", 3)
        emit_ffn("f1", x_in, None, False, False)
        if DBG:
            @blk.sync
            def _(sy):
                sy.dma_start(out=dbg_y.rearrange("p (t d) -> p t d", d=D),
                             in_=y_dram.rearrange("(t p) d -> p t d", p=128)
                             ).then_inc(*s_xo.inc(16))
                sy.dma_start(out=dbg_wtk, in_=wtk_sb[:]).then_inc(*s_xo.inc(16))
                sy.dma_start(out=dbg_dtk, in_=dtk_sb[:]).then_inc(*s_xo.inc(16))
                sy.dma_start(out=dbg_dtk2, in_=dtk2_sb[:]).then_inc(*s_xo.inc(16))
                sy.dma_start(out=dbg_tok,
                             in_=tok_dram.rearrange("(q s) o -> q (s o)", q=128)
                             ).then_inc(*s_xo.inc(16))
                sy.dma_start(out=dbg_po, in_=po_dram).then_inc(*s_xo.inc(16))

            barrier()
        if LVL >= 2:
            emit_attn()
        if DBG and LVL >= 2:
            @blk.sync
            def _(sy):
                sy.dma_start(out=dbg_ctx.rearrange("p (t d) -> p t d", d=D),
                             in_=ctx_dram.rearrange("(t p) d -> p t d", p=128)
                             ).then_inc(*s_xo.inc(16))
            barrier()
        if LVL >= 3:
            emit_ffn("f2", x_out, ctx_dram, True, False)
            barrier()
            emit_ln_pass("fin", x_out, 0, NT, 0, y_dram, True, None, 0,
                         ln=False, combine=True)

        @blk.sync
        def _(sy):
            sy.wait_ge(s_xo.h, s_xo.n)

    es.close()
    return nc


def _prep_inputs(inp, p, core):
    p = dict(p)
    p.setdefault("CAP", p["B_loc"] * p["S"])
    S, D = p["S"], p["D"]
    DC = D // 128
    B_loc = p["B_loc"]
    T = B_loc * S
    x = np.ascontiguousarray(
        inp["L0_attn_hidden_states"][core * B_loc:(core + 1) * B_loc]
    ).reshape(T, D)
    de = np.asarray(inp["L0_attn_dist_emb"])
    deT = np.ascontiguousarray(de.T)
    derT = np.ascontiguousarray(de[::-1].T)
    f32 = np.float32
    return {
        "x_in": np.asarray(x, f32),
        "WqT": np.ascontiguousarray(np.asarray(inp["L0_attn_Wq"], f32).T),
        "WkT": np.ascontiguousarray(np.asarray(inp["L0_attn_Wk"], f32).T),
        "WvT": np.ascontiguousarray(np.asarray(inp["L0_attn_Wv"], f32).T),
        "bqT": np.ascontiguousarray(np.asarray(inp["L0_attn_bq"], f32).reshape(DC, 128).T),
        "bkT": np.ascontiguousarray(np.asarray(inp["L0_attn_bk"], f32).reshape(DC, 128).T),
        "bv_rep": np.ascontiguousarray(
            np.broadcast_to(np.asarray(inp["L0_attn_bv"], f32), (128, D))),
        "deT_d": np.ascontiguousarray(np.concatenate([deT, deT], axis=0), f32),
        "derT_d": np.ascontiguousarray(np.concatenate([derT, derT], axis=0), f32),
        "Wg_in": np.asarray(inp["L0_ffn_Wg"], f32),
        "W1_in": np.asarray(inp["L0_ffn_W1"], f32),
        "W2_in": np.asarray(inp["L0_ffn_W2"], f32),
        "W3_in": np.asarray(inp["L0_ffn_W3"], f32),
        "Ws1_in": np.asarray(inp["L0_ffn_Ws1"], f32),
        "Ws2_in": np.asarray(inp["L0_ffn_Ws2"], f32),
        "Ws3_in": np.asarray(inp["L0_ffn_Ws3"], f32),
        "ident_in": np.eye(128, dtype=ml_dtypes.bfloat16),
        "identf": np.eye(128, dtype=np.float32),
        "iota_tok": np.ascontiguousarray(
            (np.arange(T // 128)[None, :] * 128 + np.arange(128)[:, None])
            .astype(np.uint32)),
        "tok_pad": np.full((128, (p["E"] * p["CAP"]) // 128 + 1), T,
                           dtype=np.uint32),
        "eoff": np.ascontiguousarray(
            (np.arange(32, dtype=np.float32) * p["CAP"] - 1.0).reshape(32, 1)),
    }


def kernel(**inputs):
    p = FULL
    nc = build(p)
    in_maps = [_prep_inputs(inputs, p, c) for c in range(N_CORES)]
    res = run_bass_kernel_spmd(nc, in_maps, core_ids=list(range(N_CORES)))
    out = np.concatenate(
        [r["x_out"].reshape(p["B_loc"], p["S"], p["D"]) for r in res.results], axis=0)
    return np.ascontiguousarray(out, dtype=np.float32)


# version 05 (final)
# speedup vs baseline: 1.6018x; 1.0253x over previous
from contextlib import ExitStack

import numpy as np
import ml_dtypes

import concourse.bass as bass
import concourse.mybir as mybir
from concourse.bass_types import AP
from concourse.bass_core_rust import IndirectOffsetOnAxis
from concourse.bass_utils import run_bass_kernel_spmd

F32 = mybir.dt.float32
BF16 = mybir.dt.bfloat16
U32 = mybir.dt.uint32
I32 = mybir.dt.int32
AF = mybir.ActivationFunctionType
OP = mybir.AluOpType
AX = mybir.AxisListType.X

N_CORES = 8

FULL = dict(S=2048, D=1024, NH=16, E=32, H=512, Hs=1024, B_loc=2, MAXP=2048,
            JB=512, IQ=512, EXPB=4.0, CAP=768)


def build(p):
    S, D, E, H, Hs = p["S"], p["D"], p["E"], p["H"], p["Hs"]
    NH, B_loc, MAXP = p["NH"], p["B_loc"], p["MAXP"]
    JB, IQ, EXPB = p["JB"], p["IQ"], p["EXPB"]
    T = B_loc * S
    CAP = p.get("CAP", T)
    GCH = CAP // 128
    PB = min(512, CAP)
    while CAP % PB:
        PB -= 128
    NPB = CAP // PB
    DC, HC, HsC = D // 128, H // 128, Hs // 128
    NT, ST = T // 128, S // 128
    R, C0 = 2 * MAXP - 1, MAXP - 1
    NJB, JT, NIQ = S // JB, S // 128, S // IQ
    IQC = IQ // 128
    TB = 512 if T >= 512 else T
    NTB = T // TB
    TBC = TB // 128
    OB = min(512, D)
    SBW = min(512, S)
    NB2 = D // OB
    SB2 = S // SBW
    WQ = JB + 127
    WK = IQ + 127

    nc = bass.Bass("TRN2", target_bir_lowering=False, detect_race_conditions=False)
    es = ExitStack()

    def sbuf(name, shape, dt):
        return es.enter_context(nc.sbuf_tensor(name, shape, dt))

    all_sems = []

    class Sem:
        __slots__ = ("h", "n")

        def __init__(self, name):
            self.h = es.enter_context(nc.semaphore(name))
            self.n = 0
            all_sems.append(self)

        def inc(self, k=1):
            self.n += k
            return (self.h, k)

    dt_ = nc.dram_tensor
    x_in = dt_("x_in", [T, D], F32, kind="ExternalInput").ap()
    x_out = dt_("x_out", [T, D], F32, kind="ExternalOutput").ap()
    WqT_in = dt_("WqT", [D, D], F32, kind="ExternalInput").ap()
    WkT_in = dt_("WkT", [D, D], F32, kind="ExternalInput").ap()
    WvT_in = dt_("WvT", [D, D], F32, kind="ExternalInput").ap()
    bqT_in = dt_("bqT", [128, DC], F32, kind="ExternalInput").ap()
    bkT_in = dt_("bkT", [128, DC], F32, kind="ExternalInput").ap()
    bv_in = dt_("bv_rep", [128, D], F32, kind="ExternalInput").ap()
    deT_in = dt_("deT_d", [128, R], F32, kind="ExternalInput").ap()
    derT_in = dt_("derT_d", [128, R], F32, kind="ExternalInput").ap()
    Wg_in = dt_("Wg_in", [D, E], F32, kind="ExternalInput").ap()
    W1_in = dt_("W1_in", [E, D, H], F32, kind="ExternalInput").ap()
    W2_in = dt_("W2_in", [E, H, D], F32, kind="ExternalInput").ap()
    W3_in = dt_("W3_in", [E, D, H], F32, kind="ExternalInput").ap()
    Ws1_in = dt_("Ws1_in", [D, Hs], F32, kind="ExternalInput").ap()
    Ws2_in = dt_("Ws2_in", [Hs, D], F32, kind="ExternalInput").ap()
    Ws3_in = dt_("Ws3_in", [D, Hs], F32, kind="ExternalInput").ap()
    ident_in = dt_("ident_in", [128, 128], BF16, kind="ExternalInput").ap()
    y_dram = dt_("y_dram", [T, D], BF16, kind="Internal").ap()
    ctx_dram = dt_("ctx_dram", [T, D], BF16, kind="Internal").ap()
    xln_dram = dt_("xln_dram", [T + 128, D], BF16, kind="Internal").ap()
    po_dram = dt_("po_dram", [E * CAP + 128, D], BF16, kind="Internal").ap()
    WT = E * GCH + 1
    tok_dram = dt_("tok_dram", [128 * WT, 1], U32, kind="Internal").ap()
    iota_tok_in = dt_("iota_tok", [128, NT], U32, kind="ExternalInput").ap()
    tok_pad_in = dt_("tok_pad", [128, (E * CAP + 128) // 128], U32,
                     kind="ExternalInput").ap()
    eoff_in = dt_("eoff", [32, 1], F32, kind="ExternalInput").ap()
    identf_in = dt_("identf", [128, 128], F32, kind="ExternalInput").ap()
    DBG = p.get("DBG", False)
    if DBG:
        dbg_y = dt_("dbg_y", [128, NT * D], BF16, kind="ExternalOutput").ap()
        dbg_mw = dt_("dbg_mw", [128, NT * E], F32, kind="ExternalOutput").ap()
        dbg_ctx = dt_("dbg_ctx", [128, NT * D], BF16, kind="ExternalOutput").ap()
        dbg_xln = dt_("dbg_xln", [128, (D // 128) * T], BF16, kind="ExternalOutput").ap()
        dbg_wtk = dt_("dbg_wtk", [128, NT * 4], F32, kind="ExternalOutput").ap()
        dbg_dtk = dt_("dbg_dtk", [128, NT * 4], U32, kind="ExternalOutput").ap()
        dbg_dtk2 = dt_("dbg_dtk2", [128, NT * 4], U32, kind="ExternalOutput").ap()
        dbg_tok = dt_("dbg_tok", [128, WT], U32, kind="ExternalOutput").ap()
        dbg_po = dt_("dbg_po", [E * CAP + 128, D], BF16, kind="ExternalOutput").ap()
        dbg_toksb = dt_("dbg_toksb", [128, E * GCH], U32, kind="ExternalOutput").ap()
        dbg_xg = dt_("dbg_xg", [128, 2 * D], BF16, kind="ExternalOutput").ap()
        dbg_xe = dt_("dbg_xe", [128, (D // 128) * CAP], BF16, kind="ExternalOutput").ap()

    consts = sbuf("consts", [128, 8], F32)
    nc.const_aps.register(F32, 1.0, consts[:, 0:1])
    nc.const_aps.register(F32, 0.5, consts[:, 1:2])
    nc.const_aps.register(F32, 1e-5, consts[:, 2:3])
    nc.const_aps.register(F32, float(-EXPB), consts[:, 3:4])
    nc.const_aps.register(F32, 0.0, consts[:, 4:5])
    ident = sbuf("ident", [128, 128], BF16)
    deT = sbuf("deT", [128, R], BF16)
    derT = sbuf("derT", [128, R], BF16)
    wg_sb = sbuf("wg_sb", [128, DC * E], BF16)
    bq_sb = sbuf("bq_sb", [128, DC], F32)
    bk_sb = sbuf("bk_sb", [128, DC], F32)
    wtk_sb = sbuf("wtk_sb", [128, NT * 4], F32)
    dtk_sb = sbuf("dtk_sb", [128, NT * 4], U32)
    dtk2_sb = sbuf("dtk2_sb", [128, NT * 4], U32)
    scri = sbuf("scri", [128, 8], I32)
    iota_tok = sbuf("iota_tok_sb", [128, NT], U32)
    eoff_sb = sbuf("eoff_sb", [32, 1], F32)
    identf = sbuf("identf_sb", [128, 128], F32)
    zrow = sbuf("zrow", [128, D], BF16)

    PS = es.enter_context(nc.psum_tensor("PS", [128, 4096], F32))

    def bank(i, w=512, off=0):
        return PS[:, i * 512 + off:i * 512 + off + w]

    s_boot, s_cboot = Sem("s_boot"), Sem("s_cboot")
    s_x, s_w = Sem("s_x"), Sem("s_w")
    s_fold, s_lnA, s_scA, s_lnN = Sem("s_fold"), Sem("s_lnA"), Sem("s_scA"), Sem("s_lnN")
    s_mmT, s_evT, s_xo = Sem("s_mmT"), Sem("s_evT"), Sem("s_xo")
    s_mmG, s_scG, s_evG = Sem("s_mmG"), Sem("s_scG"), Sem("s_evG")
    s_mmH, s_scH, s_evH = Sem("s_mmH"), Sem("s_scH"), Sem("s_evH")
    s_mmO, s_evO, s_wf = Sem("s_mmO"), Sem("s_evO"), Sem("s_wf")
    s_mmP, s_evP = Sem("s_mmP"), Sem("s_evP")
    s_mmK, s_evK, s_shK = Sem("s_mmK"), Sem("s_evK"), Sem("s_shK")
    s_mmS, s_evS, s_shS, s_scS = Sem("s_mmS"), Sem("s_evS"), Sem("s_shS"), Sem("s_scS")
    s_evZ, s_mmX, s_evX = Sem("s_evZ"), Sem("s_mmX"), Sem("s_evX")
    s_mmU, s_zpD, s_ptc, s_mmC = Sem("s_mmU"), Sem("s_zpD"), Sem("s_ptc"), Sem("s_mmC")
    s_xst, s_rt, s_tok = Sem("s_xst"), Sem("s_rt"), Sem("s_tok")
    s_pog, s_po, s_xg = Sem("s_pog"), Sem("s_po"), Sem("s_xg")

    with nc.Block() as blk:

        def barrier():
            snap = [(s.h, s.n) for s in all_sems if s.n > 0]

            def w(eng):
                for h, n in snap:
                    eng.wait_ge(h, n)
            blk.sync(w)
            blk.vector(w)
            blk.scalar(w)
            blk.tensor(w)
            blk.gpsimd(w)

        @blk.vector
        def _(v):
            v.memset(consts[:, 0:1], 1.0)
            v.memset(consts[:, 1:2], 0.5)
            v.memset(consts[:, 2:3], 1e-5)
            v.memset(consts[:, 3:4], float(-EXPB))
            v.memset(consts[:, 4:5], 0.0).then_inc(*s_cboot.inc())

        @blk.sync
        def _(sy):
            sy.dma_start(out=ident[:], in_=ident_in).then_inc(*s_boot.inc(16))
            sy.dma_start(out=bq_sb[:], in_=bqT_in).then_inc(*s_boot.inc(16))
            sy.dma_start(out=bk_sb[:], in_=bkT_in).then_inc(*s_boot.inc(16))
            sy.dma_start(out=iota_tok[:], in_=iota_tok_in).then_inc(*s_boot.inc(16))
            sy.dma_start(out=eoff_sb[:], in_=eoff_in).then_inc(*s_boot.inc(16))
            sy.dma_start(out=identf[:], in_=identf_in).then_inc(*s_boot.inc(16))

        @blk.vector
        def _(v):
            v.memset(zrow[:], 0.0).then_inc(*s_cboot.inc())

        @blk.sync
        def _(sy):
            sy.wait_ge(s_cboot.h, s_cboot.n)
            sy.dma_start(out=xln_dram[T:T + 128, :], in_=zrow[:]
                         ).then_inc(*s_boot.inc(16))
            sy.dma_start(out=po_dram[E * CAP:E * CAP + 128, :], in_=zrow[:]
                         ).then_inc(*s_boot.inc(16))

        @blk.gpsimd
        def _(g):
            g.dma_start(out=deT[:], in_=deT_in).then_inc(*s_w.inc(16))
            g.dma_start(out=derT[:], in_=derT_in).then_inc(*s_w.inc(16))
            g.dma_start(out=wg_sb[:].rearrange("p (c e) -> p c e", c=DC),
                        in_=Wg_in.rearrange("(c p) e -> p c e", p=128)
                        ).then_inc(*s_w.inc(16))

        def emit_ln_pass(tag, src, row0, ntiles, tile0, fold_dram, writeback,
                         xlnT, span, ln=True, alloc=None, stage_xln=False,
                         combine=False):
            alloc = alloc or sbuf
            xt = alloc(tag + "_xt", [128, 2 * D], F32)
            xw = alloc(tag + "_xw", [128, 2 * D], F32)
            stat = alloc(tag + "_stat", [128, 8], F32)
            sq = alloc(tag + "_sq", [128, D], BF16)
            xln = alloc(tag + "_xln", [128, 2 * D], BF16) if ln else None
            have_fold = fold_dram is not None
            fbuf = alloc(tag + "_fb", [128, 2 * D], BF16) if have_fold else None
            pob = alloc(tag + "_po", [128, 4 * D], BF16) if combine else None
            b_x, b_fold, b_lnA, b_scA = s_x.n, s_fold.n, s_lnA.n, s_scA.n
            b_lnN, b_mmT, b_evT, b_xo = s_lnN.n, s_mmT.n, s_evT.n, s_xo.n
            b_xst, b_pog = s_xst.n, s_pog.n
            tb = max(1, (DC * 128) // 512)
            per_t = 32 if have_fold else 16
            assert ln or (have_fold and writeback)

            def emit_evac(v, u):
                v.wait_ge(s_mmT.h, b_mmT + u + 1)
                g0 = (u % 2) * tb
                out_ap = AP(tensor=xlnT, offset=u * 128,
                            ap=[[xlnT.shape[1], 128], [span, DC], [1, 128]])
                in_ap = AP(tensor=PS, offset=g0 * 512,
                           ap=[[4096, 128], [128, DC], [1, 128]])
                v.tensor_copy(out=out_ap, in_=in_ap).then_inc(*s_evT.inc())

            @blk.sync
            def _(sy):
                for t in range(ntiles):
                    if t >= 2:
                        if ln:
                            sy.wait_ge(s_lnN.h, b_lnN + (t - 1))
                        else:
                            sy.wait_ge(s_fold.h, b_fold + (t - 1))
                    sy.dma_start(out=xt[:, (t % 2) * D:(t % 2 + 1) * D],
                                 in_=src[row0 + t * 128:row0 + (t + 1) * 128, :]
                                 ).then_inc(*s_x.inc(16))
                    if have_fold:
                        sy.dma_start(out=fbuf[:, (t % 2) * D:(t % 2 + 1) * D],
                                     in_=fold_dram[row0 + t * 128:row0 + (t + 1) * 128, :]
                                     ).then_inc(*s_x.inc(16))
                    if writeback:
                        sy.wait_ge(s_fold.h, b_fold + t + 1)
                        sy.dma_start(out=x_out[row0 + t * 128:row0 + (t + 1) * 128, :],
                                     in_=xw[:, (t % 2) * D:(t % 2 + 1) * D]
                                     ).then_inc(*s_xo.inc(16))
                    if stage_xln:
                        sy.wait_ge(s_lnN.h, b_lnN + t + 1)
                        sy.dma_start(out=xln_dram[row0 + t * 128:row0 + (t + 1) * 128, :],
                                     in_=xln[:, (t % 2) * D:(t % 2 + 1) * D]
                                     ).then_inc(*s_xst.inc(16))

            if combine:
                @blk.gpsimd
                def _(g):
                    for t in range(ntiles):
                        if t >= 1:
                            g.wait_ge(s_fold.h, b_fold + t)
                        for k in range(4):
                            col = (tile0 + t) * 4 + k
                            dst = pob[:, k * D:(k + 1) * D]
                            g.indirect_dma_start(
                                out=dst.bitcast(F32), out_offset=None,
                                in_=po_dram.bitcast(F32),
                                in_offset=IndirectOffsetOnAxis(
                                    dtk_sb[:, col:col + 1], 0)
                                ).then_inc(*s_pog.inc(16))

            @blk.vector
            def _(v):
                v.wait_ge(s_cboot.h, s_cboot.n)
                for t in range(ntiles):
                    sl = slice((t % 2) * D, (t % 2 + 1) * D)
                    v.wait_ge(s_x.h, b_x + per_t * (t + 1))
                    cur = xt[:, sl]
                    if have_fold:
                        if t >= 2:
                            if ln:
                                v.wait_ge(s_scA.h, b_scA + 2 * (t - 2) + 1)
                            if writeback:
                                v.wait_ge(s_xo.h, b_xo + 16 * (t - 1))
                        if combine:
                            v.tensor_add(xw[:, sl], cur, fbuf[:, sl])
                            v.wait_ge(s_pog.h, b_pog + 64 * (t + 1))
                            ins = None
                            for k in range(4):
                                col = (tile0 + t) * 4 + k
                                pk = pob[:, k * D:(k + 1) * D]
                                ins = v.scalar_tensor_tensor(
                                    out=xw[:, sl], in0=pk,
                                    scalar=wtk_sb[:, col:col + 1],
                                    in1=xw[:, sl], op0=OP.mult, op1=OP.add)
                            ins.then_inc(*s_fold.inc())
                        else:
                            v.tensor_add(xw[:, sl], cur, fbuf[:, sl]
                                         ).then_inc(*s_fold.inc())
                        cur = xw[:, sl]
                    if not ln:
                        continue
                    v.reduce_sum(stat[:, 0:1], cur, axis=AX)
                    v.drain()
                    v.tensor_scalar(stat[:, 1:2], stat[:, 0:1], -1.0 / D, None, OP.mult)
                    v.wait_ge(s_scA.h, b_scA + 2 * t + 1)
                    v.tensor_scalar(stat[:, 2:3], stat[:, 4:5], 1.0 / D, None, OP.mult)
                    v.drain()
                    v.tensor_mul(stat[:, 3:4], stat[:, 1:2], stat[:, 1:2])
                    v.drain()
                    v.tensor_sub(stat[:, 2:3], stat[:, 2:3], stat[:, 3:4])
                    v.drain()
                    v.tensor_scalar(stat[:, 2:3], stat[:, 2:3], 1e-5, None, OP.add)
                    v.drain().then_inc(*s_lnA.inc())
                    v.wait_ge(s_scA.h, b_scA + 2 * t + 2)
                    v.reciprocal(stat[:, 6:7], stat[:, 5:6])
                    v.drain()
                    if t >= 2:
                        v.wait_ge(s_mmT.h, b_mmT + (t - 1))
                        if stage_xln:
                            v.wait_ge(s_xst.h, b_xst + 16 * (t - 1))
                    v.tensor_scalar(xln[:, sl], cur, stat[:, 1:2], stat[:, 6:7],
                                    OP.add, OP.mult).then_inc(*s_lnN.inc())
                    if t >= 1:
                        emit_evac(v, t - 1)
                if ln:
                    emit_evac(v, ntiles - 1)

            if not ln:
                return

            @blk.scalar
            def _(sc):
                sc.wait_ge(s_cboot.h, s_cboot.n)
                for t in range(ntiles):
                    sl = slice((t % 2) * D, (t % 2 + 1) * D)
                    src_t = xw if have_fold else xt
                    if have_fold:
                        sc.wait_ge(s_fold.h, b_fold + t + 1)
                    else:
                        sc.wait_ge(s_x.h, b_x + per_t * (t + 1))
                    sc.activation(out=sq[:], in_=src_t[:, sl], func=AF.Square,
                                  accum_out=stat[:, 4:5])
                    sc.drain().then_inc(*s_scA.inc())
                    sc.wait_ge(s_lnA.h, b_lnA + t + 1)
                    sc.activation(out=stat[:, 5:6], in_=stat[:, 2:3], func=AF.Sqrt)
                    sc.drain().then_inc(*s_scA.inc())

            @blk.tensor
            def _(te):
                te.wait_ge(s_boot.h, 16)
                for t in range(ntiles):
                    te.wait_ge(s_lnN.h, b_lnN + t + 1)
                    if t >= 2:
                        te.wait_ge(s_evT.h, b_evT + (t - 1))
                    g0 = (t % 2) * tb
                    sl0 = (t % 2) * D
                    ins = None
                    for c in range(DC):
                        ins = te.matmul(
                            out=bank(g0 + (c * 128) // 512, 128, (c * 128) % 512),
                            lhsT=xln[:, sl0 + c * 128:sl0 + (c + 1) * 128],
                            rhs=ident[:],
                            start=((c * 128) % 512 == 0),
                            stop=((c * 128) % 512 == 512 - 128 or c == DC - 1))
                    ins.then_inc(*s_mmT.inc())


        def emit_gating(tag, xlnT, span, ntiles, tile0, alloc=None, balloc=None,
                        waits=()):
            alloc = alloc or sbuf
            balloc = balloc or alloc
            pg = balloc(tag + "_pg", [128, 4 * E], F32)
            p_all = balloc(tag + "_pa", [128, ntiles * E], F32)
            gz = balloc(tag + "_gz", [128, 16], F32)
            t8 = balloc(tag + "_t8", [128, 8], F32)
            mk = balloc(tag + "_mk", [128, 4 * E], F32)
            mT = balloc(tag + "_mT", [32, ntiles * 128], BF16)
            cum = balloc(tag + "_cum", [32, ntiles * 128], F32)
            zer = balloc(tag + "_zer", [32, ntiles * 128], BF16)
            dT = balloc(tag + "_dT", [128, 32], F32)
            scr = balloc(tag + "_scr", [128, E + 8], F32)
            b_mmG, b_scG, b_evG, b_rt = s_mmG.n, s_scG.n, s_evG.n, s_rt.n
            GE = 4
            ngrp = (ntiles + GE - 1) // GE
            DUMMY = float(E * CAP)
            W3_ = ntiles * 128

            @blk.sync
            def _(sy):
                sy.wait_ge(s_xg.h, s_xg.n)
                sy.dma_start(out=tok_dram.rearrange("(q s) o -> q (s o)", q=128),
                             in_=tok_pad_in).then_inc(*s_tok.inc(16))

            @blk.tensor
            def _(te):
                te.wait_ge(s_w.h, 48)
                te.wait_ge(s_evT.h, s_evT.n)
                for g in range(ngrp):
                    tl = list(range(g * GE, min((g + 1) * GE, ntiles)))
                    if g >= 2:
                        te.wait_ge(s_scG.h, b_scG + (g - 1))
                    bk6 = 6 + (g % 2)
                    ins = None
                    for t in tl:
                        for c in range(DC):
                            ins = te.matmul(
                                out=bank(bk6, E, (t % GE) * E),
                                lhsT=xlnT[:, c * span + t * 128:c * span + (t + 1) * 128],
                                rhs=wg_sb[:, c * E:(c + 1) * E],
                                start=(t == tl[0] and c == 0),
                                stop=(t == tl[-1] and c == DC - 1))
                    ins.then_inc(*s_mmG.inc())

            @blk.vector
            def _(v):
                for sm, val in waits:
                    v.wait_ge(sm.h, val)
                v.memset(zer[:], 0.0)
                for g in range(ngrp):
                    tl = list(range(g * GE, min((g + 1) * GE, ntiles)))
                    bk6 = 6 + (g % 2)
                    v.wait_ge(s_mmG.h, b_mmG + g + 1)
                    for t in tl:
                        j = t % GE
                        v.reduce_max(gz[:, j:j + 1], bank(bk6, E, j * E),
                                     axis=AX, negate=True)
                    v.drain().then_inc(*s_evG.inc())
                    v.wait_ge(s_scG.h, b_scG + g + 1)
                    for t in tl:
                        j = t % GE
                        v.reciprocal(gz[:, 4 + j:5 + j], gz[:, 4 + j:5 + j])
                    v.drain()
                    for t in tl:
                        j = t % GE
                        gt = tile0 + t
                        v.tensor_scalar(p_all[:, t * E:(t + 1) * E],
                                        pg[:, j * E:(j + 1) * E],
                                        gz[:, 4 + j:5 + j], None, OP.mult)
                        v.drain()
                        v.max(out=t8[:], in_=p_all[:, t * E:(t + 1) * E])
                        v.drain()
                        v.tensor_copy(wtk_sb[:, gt * 4:gt * 4 + 4], t8[:, 0:4])
                    v.drain().then_inc(*s_evG.inc())

            @blk.scalar
            def _(sc):
                for g in range(ngrp):
                    tl = list(range(g * GE, min((g + 1) * GE, ntiles)))
                    bk6 = 6 + (g % 2)
                    sc.wait_ge(s_evG.h, b_evG + 2 * g + 1)
                    for t in tl:
                        j = t % GE
                        sc.activation(out=pg[:, j * E:(j + 1) * E],
                                      in_=bank(bk6, E, j * E), func=AF.Exp,
                                      bias=gz[:, j:j + 1], scale=1.0,
                                      accum_out=gz[:, 4 + j:5 + j])
                    sc.drain().then_inc(*s_scG.inc())

            vA_end = s_evG.n
            b1_ev = s_evG.n
            b1_mm = s_mmG.n

            @blk.vector
            def _(v):
                nonlocal_marker = None
                for g in range(ngrp):
                    tl = list(range(g * GE, min((g + 1) * GE, ntiles)))
                    for t in tl:
                        j = t % GE
                        gt = tile0 + t
                        if g >= 1:
                            v.wait_ge(s_mmG.h, b1_mm + g)
                        v.tensor_scalar(mk[:, j * E:(j + 1) * E],
                                        p_all[:, t * E:(t + 1) * E],
                                        wtk_sb[:, gt * 4 + 3:gt * 4 + 4],
                                        None, OP.is_ge)
                    v.drain().then_inc(*s_evG.inc())
                    v.wait_ge(s_mmG.h, b1_mm + g + 1)
                    v.tensor_copy(out=mT[0:E, g * GE * 128:(g * GE + len(tl)) * 128],
                                  in_=PS[0:E, 0:len(tl) * 128]).then_inc(*s_evG.inc())

            @blk.tensor
            def _(te):
                for g in range(ngrp):
                    tl = list(range(g * GE, min((g + 1) * GE, ntiles)))
                    te.wait_ge(s_evG.h, b1_ev + 2 * g + 1)
                    if g >= 1:
                        te.wait_ge(s_evG.h, b1_ev + 2 * g)
                    ins = None
                    for t in tl:
                        j = t % GE
                        ins = te.matmul(out=PS[0:E, j * 128:(j + 1) * 128],
                                        lhsT=mk[:, j * E:(j + 1) * E],
                                        rhs=identf[:],
                                        start=(j == 0), stop=(t == tl[-1]))
                    ins.then_inc(*s_mmG.inc())

            b2_ev = s_evG.n

            @blk.vector
            def _(v):
                v.wait_ge(s_evG.h, b2_ev)
                v.tensor_tensor_scan(out=cum[0:E, :], data0=mT[0:E, :],
                                     data1=zer[0:E, :], initial=0.0,
                                     op0=OP.add, op1=OP.add)
                v.drain()
                v.tensor_scalar(mT[0:E, :], cum[0:E, :], float(CAP), None, OP.is_le)
                v.tensor_scalar(cum[0:E, :], cum[0:E, :], eoff_sb[0:E, :],
                                None, OP.add)
                v.tensor_scalar(cum[0:E, :], cum[0:E, :], -DUMMY, None, OP.add)
                v.tensor_mul(cum[0:E, :], cum[0:E, :], mT[0:E, :])
                v.tensor_scalar(cum[0:E, :], cum[0:E, :], DUMMY, None, OP.add)
                v.drain().then_inc(*s_evG.inc())

            b3_ev = s_evG.n
            b3_mm = s_mmG.n

            @blk.tensor
            def _(te):
                te.wait_ge(s_evG.h, b3_ev)
                for t in range(ntiles):
                    if t >= 1:
                        te.wait_ge(s_evG.h, b3_ev + t)
                    te.matmul(out=PS[:, 512:512 + 32],
                              lhsT=cum[0:E, t * 128:(t + 1) * 128],
                              rhs=identf[0:E, 0:32],
                              start=True, stop=True).then_inc(*s_mmG.inc())

            @blk.vector
            def _(v):
                for t in range(ntiles):
                    gt = tile0 + t
                    v.wait_ge(s_mmG.h, b3_mm + t + 1)
                    v.tensor_copy(out=dT[:], in_=PS[:, 512:512 + 32])
                    v.drain()
                    ins = None
                    for k in range(4):
                        v.tensor_scalar(scr[:, 0:E], p_all[:, t * E:(t + 1) * E],
                                        wtk_sb[:, gt * 4 + k:gt * 4 + k + 1],
                                        None, OP.is_equal)
                        v.drain()
                        v.scalar_tensor_tensor(
                            out=scr[:, 0:E], in0=scr[:, 0:E], scalar=1.0,
                            in1=dT[:, 0:E], op0=OP.mult, op1=OP.mult,
                            accum_out=scr[:, E + k:E + k + 1])
                        v.drain()
                        v.tensor_copy(dtk_sb[:, gt * 4 + k:gt * 4 + k + 1],
                                      scr[:, E + k:E + k + 1])
                        v.tensor_copy(scri[:, 0:1], scr[:, E + k:E + k + 1])
                        v.drain()
                        v.tensor_scalar(scri[:, 1:2], scri[:, 0:1], 7, None,
                                        OP.logical_shift_right)
                        v.tensor_scalar(scri[:, 2:3], scri[:, 0:1], 127, None,
                                        OP.bitwise_and)
                        v.drain()
                        v.tensor_scalar(scri[:, 2:3], scri[:, 2:3], WT, None,
                                        OP.mult)
                        v.drain()
                        v.tensor_add(scri[:, 3:4], scri[:, 2:3], scri[:, 1:2])
                        v.drain()
                        v.tensor_copy(dtk2_sb[:, gt * 4 + k:gt * 4 + k + 1],
                                      scri[:, 3:4])
                    v.drain().then_inc(*s_evG.inc())

            @blk.gpsimd
            def _(g):
                g.wait_ge(s_tok.h, s_tok.n)
                for t in range(ntiles):
                    gt = tile0 + t
                    g.wait_ge(s_evG.h, b3_ev + t + 1)
                    for k in range(4):
                        g.indirect_dma_start(
                            out=tok_dram, out_offset=IndirectOffsetOnAxis(
                                dtk2_sb[:, gt * 4 + k:gt * 4 + k + 1], 0),
                            in_=iota_tok[:, gt:gt + 1], in_offset=None
                            ).then_inc(*s_rt.inc(16))

        def emit_swiglu(tag, xlnT, span, w1sb, w3sb, w2sb, hxc, y_sb, hg, hgl,
                        wait_w=None, extra_waits=()):
            b_mmH, b_scH, b_evH = s_mmH.n, s_scH.n, s_evH.n
            b_mmO, b_evO = s_mmO.n, s_evO.n
            HW = hxc * 128

            @blk.tensor
            def _(te):
                if wait_w is not None:
                    te.wait_ge(s_w.h, wait_w)
                te.wait_ge(s_evT.h, s_evT.n)
                for sm, val in extra_waits:
                    te.wait_ge(sm.h, val)
                for b in range(NTB):
                    col = b * TB
                    for hc in range(hxc):
                        idx = b * hxc + hc
                        if idx >= 2:
                            te.wait_ge(s_evH.h, b_evH + idx - 1)
                        bh1, bh3 = idx % 2, 2 + idx % 2
                        for c in range(DC):
                            te.matmul(out=bank(bh1, TB),
                                      lhsT=w1sb[:, c * HW + hc * 128:c * HW + (hc + 1) * 128],
                                      rhs=xlnT[:, c * span + col:c * span + col + TB],
                                      start=(c == 0), stop=(c == DC - 1))
                        ins = None
                        for c in range(DC):
                            ins = te.matmul(out=bank(bh3, TB),
                                            lhsT=w3sb[:, c * HW + hc * 128:c * HW + (hc + 1) * 128],
                                            rhs=xlnT[:, c * span + col:c * span + col + TB],
                                            start=(c == 0), stop=(c == DC - 1))
                        ins.then_inc(*s_mmH.inc())
                    te.wait_ge(s_evH.h, b_evH + (b + 1) * hxc)
                    for tc in range(TBC):
                        for n in range(NB2):
                            oidx = b * TBC * NB2 + tc * NB2 + n
                            bo = 4 + oidx % 2
                            if oidx >= 2:
                                te.wait_ge(s_evO.h, b_evO + oidx - 1)
                            ins = None
                            for k in range(hxc):
                                ins = te.matmul(
                                    out=bank(bo, OB),
                                    lhsT=hgl[:, k * TB + tc * 128:k * TB + (tc + 1) * 128],
                                    rhs=w2sb[:, k * D + n * OB:k * D + (n + 1) * OB],
                                    start=(k == 0), stop=(k == hxc - 1))
                            ins.then_inc(*s_mmO.inc())

            @blk.scalar
            def _(sc):
                for b in range(NTB):
                    for hc in range(hxc):
                        idx = b * hxc + hc
                        sc.wait_ge(s_mmH.h, b_mmH + idx + 1)
                        sc.activation(out=hg[:, (idx % 2) * TB:(idx % 2) * TB + TB],
                                      in_=bank(idx % 2, TB), func=AF.Silu
                                      ).then_inc(*s_scH.inc())

            @blk.vector
            def _(v):
                for b in range(NTB):
                    for hc in range(hxc):
                        idx = b * hxc + hc
                        v.wait_ge(s_scH.h, b_scH + idx + 1)
                        v.tensor_mul(hgl[:, hc * TB:hc * TB + TB],
                                     bank(2 + idx % 2, TB),
                                     hg[:, (idx % 2) * TB:(idx % 2) * TB + TB]
                                     ).then_inc(*s_evH.inc())
                    for tc in range(TBC):
                        for n in range(NB2):
                            oidx = b * TBC * NB2 + tc * NB2 + n
                            bo = 4 + oidx % 2
                            v.wait_ge(s_mmO.h, b_mmO + oidx + 1)
                            gt = b * TBC + tc
                            ysl = y_sb[:, gt * D + n * OB:gt * D + (n + 1) * OB]
                            v.tensor_scalar(ysl, bank(bo, OB), 0.5, None, OP.mult
                                            ).then_inc(*s_evO.inc())

        def emit_expert_routed(e, w1sb, w3sb, w2sb, tok_sb, xg, xeT, po_sb,
                               hg, hgl, wait_w, free_sem, b_tok, sweep,
                               prefetch=None):
            b_xg, b_mmT, b_evT = s_xg.n, s_mmT.n, s_evT.n
            b_mmH, b_scH, b_evH = s_mmH.n, s_scH.n, s_evH.n
            b_mmO, b_evO, b_po = s_mmO.n, s_evO.n, s_po.n
            PC = CAP // 128
            NHX = NPB * HC

            @blk.gpsimd
            def _(g):
                g.wait_ge(s_tok.h, b_tok)
                for gc in range(GCH):
                    gci = e * GCH + gc
                    if gci >= 2:
                        g.wait_ge(s_mmT.h, sweep["mmT"] + gci - 1)
                    g.indirect_dma_start(
                        out=xg[:, (gc % 2) * D:(gc % 2 + 1) * D].bitcast(F32),
                        out_offset=None,
                        in_=xln_dram.bitcast(F32),
                        in_offset=IndirectOffsetOnAxis(
                            tok_sb[:, e * GCH + gc:e * GCH + gc + 1], 0)
                        ).then_inc(*s_xg.inc(16))
                if prefetch is not None:
                    prefetch(g)

            @blk.tensor
            def _(te):
                te.wait_ge(s_w.h, wait_w)
                for gc in range(GCH):
                    gci = e * GCH + gc
                    te.wait_ge(s_xg.h, b_xg + 16 * (gc + 1))
                    if gci >= 1:
                        te.wait_ge(s_evT.h, sweep["evT"] + gci)
                    ins = None
                    for c in range(DC):
                        ins = te.matmul(
                            out=bank(6 + c // 4, 128, (c % 4) * 128),
                            lhsT=xg[:, (gc % 2) * D + c * 128:
                                    (gc % 2) * D + (c + 1) * 128],
                            rhs=ident[:],
                            start=(c % 4 == 0), stop=(c % 4 == 3 or c == DC - 1))
                    ins.then_inc(*s_mmT.inc())
                for pb in range(NPB):
                    for hc in range(HC):
                        idx = pb * HC + hc
                        ghx = e * NHX + idx
                        te.wait_ge(s_evH.h, sweep["evH"] + max(ghx - 1, 0))
                        te.wait_ge(s_evT.h, b_evT + GCH)
                        bh1, bh3 = idx % 2, 2 + idx % 2
                        for c in range(DC):
                            te.matmul(out=bank(bh1, PB),
                                      lhsT=w1sb[:, c * H + hc * 128:c * H + (hc + 1) * 128],
                                      rhs=xeT[:, c * CAP + pb * PB:c * CAP + (pb + 1) * PB],
                                      start=(c == 0), stop=(c == DC - 1))
                        ins = None
                        for c in range(DC):
                            ins = te.matmul(out=bank(bh3, PB),
                                            lhsT=w3sb[:, c * H + hc * 128:c * H + (hc + 1) * 128],
                                            rhs=xeT[:, c * CAP + pb * PB:c * CAP + (pb + 1) * PB],
                                            start=(c == 0), stop=(c == DC - 1))
                        ins.then_inc(*s_mmH.inc())
                te.wait_ge(s_evH.h, sweep["evH"] + (e + 1) * NHX)
                for pc in range(PC):
                    for n in range(NB2):
                        oidx = pc * NB2 + n
                        gox = e * PC * NB2 + oidx
                        gpc = e * PC + pc
                        bo = 4 + oidx % 2
                        te.wait_ge(s_evO.h, sweep["evO"] + max(gox - 1, 0))
                        if gpc >= 2:
                            te.wait_ge(s_po.h, sweep["po"] + 16 * (gpc - 1))
                        ins = None
                        for k in range(HC):
                            ins = te.matmul(
                                out=bank(bo, OB),
                                lhsT=hgl[:, k * CAP + pc * 128:k * CAP + (pc + 1) * 128],
                                rhs=w2sb[:, k * D + n * OB:k * D + (n + 1) * OB],
                                start=(k == 0), stop=(k == HC - 1))
                        ins.then_inc(*s_mmO.inc())

            @blk.vector
            def _(v):
                for gc in range(GCH):
                    v.wait_ge(s_mmT.h, b_mmT + gc + 1)
                    out_ap = AP(tensor=xeT, offset=gc * 128,
                                ap=[[xeT.shape[1], 128], [CAP, DC], [1, 128]])
                    in_ap = AP(tensor=PS, offset=6 * 512,
                               ap=[[4096, 128], [128, DC], [1, 128]])
                    v.tensor_copy(out=out_ap, in_=in_ap).then_inc(*s_evT.inc())
                for pb in range(NPB):
                    for hc in range(HC):
                        idx = pb * HC + hc
                        v.wait_ge(s_scH.h, b_scH + idx + 1)
                        v.tensor_mul(hgl[:, hc * CAP + pb * PB:hc * CAP + pb * PB + PB],
                                     bank(2 + idx % 2, PB),
                                     hg[:, (idx % 2) * PB:(idx % 2) * PB + PB]
                                     ).then_inc(*s_evH.inc())
                ins = None
                for pc in range(PC):
                    for n in range(NB2):
                        oidx = pc * NB2 + n
                        gpc = e * PC + pc
                        bo = 4 + oidx % 2
                        v.wait_ge(s_mmO.h, b_mmO + oidx + 1)
                        if gpc >= 2 and n == 0:
                            v.wait_ge(s_po.h, sweep["po"] + 16 * (gpc - 1))
                        ins = v.tensor_scalar(
                            po_sb[:, (pc % 2) * D + n * OB:(pc % 2) * D + (n + 1) * OB],
                            bank(bo, OB), 0.5, None, OP.mult)
                        ins.then_inc(*s_evO.inc())
                if free_sem is not None:
                    v.drain().then_inc(*free_sem.inc())

            @blk.scalar
            def _(sc):
                for pb in range(NPB):
                    for hc in range(HC):
                        idx = pb * HC + hc
                        sc.wait_ge(s_mmH.h, b_mmH + idx + 1)
                        sc.activation(out=hg[:, (idx % 2) * PB:(idx % 2) * PB + PB],
                                      in_=bank(idx % 2, PB), func=AF.Silu
                                      ).then_inc(*s_scH.inc())

            @blk.sync
            def _(sy):
                for pc in range(PC):
                    sy.wait_ge(s_evO.h, b_evO + pc * NB2 + NB2)
                    sy.dma_start(
                        out=po_dram[e * CAP + pc * 128:e * CAP + (pc + 1) * 128, :],
                        in_=po_sb[:, (pc % 2) * D:(pc % 2 + 1) * D]
                        ).then_inc(*s_po.inc(16))

        def emit_ffn(tag, src, fold_dram, writeback, combine):
            with ExitStack() as ph:
                psb = lambda n, s2, d: ph.enter_context(nc.sbuf_tensor(tag + n, s2, d))
                y_sb = psb("_y", [128, NT * D], BF16)
                hg = psb("_hg", [128, 2 * max(TB, PB)], BF16)
                hgl = psb("_hgl", [128, max(HsC * TB, HC * CAP)], BF16)
                with ExitStack() as st_a:
                    asb = lambda n, s2, d: st_a.enter_context(
                        nc.sbuf_tensor(tag + n, s2, d))
                    xlnT = asb("_xlnT", [128, DC * T], BF16)
                    st_l = ExitStack()
                    lsb = lambda n, s2, d: st_l.enter_context(
                        nc.sbuf_tensor(tag + n, s2, d))
                    emit_ln_pass(tag + "_ln", src, 0, NT, 0, fold_dram, writeback,
                                 xlnT, T, alloc=lsb, stage_xln=True,
                                 combine=combine)
                    ln_marks = [(s_xo, s_xo.n), (s_xst, s_xst.n),
                                (s_scA, s_scA.n), (s_fold, s_fold.n),
                                (s_lnN, s_lnN.n)]
                    st_l.close()
                    st_g = ExitStack()
                    gsb = lambda n, s2, d: st_g.enter_context(
                        nc.sbuf_tensor(tag + n, s2, d))
                    emit_gating(tag + "_g", xlnT, T, NT, 0, alloc=asb,
                                balloc=gsb, waits=ln_marks)
                    rt_marks = [(s_evG, s_evG.n), (s_scG, s_scG.n),
                                (s_mmG, s_mmG.n)]
                    st_g.close()
                    if DBG and tag == "f1":
                        @blk.sync
                        def _(sy):
                            sy.wait_ge(s_evT.h, s_evT.n)
                            sy.dma_start(out=dbg_xln, in_=xlnT[:]
                                         ).then_inc(*s_xo.inc(16))
                    ws1 = asb("_ws1", [128, DC * Hs], BF16)
                    ws3 = asb("_ws3", [128, DC * Hs], BF16)
                    ws2 = asb("_ws2", [128, HsC * D], BF16)
                    b_w = s_w.n

                    @blk.gpsimd
                    def _(g):
                        for sm, val in rt_marks:
                            g.wait_ge(sm.h, val)
                        g.dma_start(out=ws1[:].rearrange("p (c h) -> p c h", c=DC),
                                    in_=Ws1_in.rearrange("(c p) h -> p c h", p=128)
                                    ).then_inc(*s_w.inc(16))
                        g.dma_start(out=ws3[:].rearrange("p (c h) -> p c h", c=DC),
                                    in_=Ws3_in.rearrange("(c p) h -> p c h", p=128)
                                    ).then_inc(*s_w.inc(16))
                        g.dma_start(out=ws2[:].rearrange("p (c h) -> p c h", c=HsC),
                                    in_=Ws2_in.rearrange("(c p) h -> p c h", p=128)
                                    ).then_inc(*s_w.inc(16))

                    emit_swiglu(tag + "_sh", xlnT, T, ws1, ws3, ws2, HsC, y_sb,
                                hg, hgl, wait_w=b_w + 48, extra_waits=rt_marks)
                shared_mmO = s_mmO.n
                sweep_marks = [(s_mmO, s_mmO.n), (s_scG, s_scG.n)]
                with ExitStack() as st_b:
                    bsb = lambda n, s2, d: st_b.enter_context(
                        nc.sbuf_tensor(tag + n, s2, d))
                    tok_sb = bsb("_tok", [128, E * GCH], U32)
                    xg = bsb("_xg", [128, 2 * D], BF16)
                    xeT = bsb("_xeT", [128, DC * CAP], BF16)
                    po_sb = bsb("_po2", [128, 2 * D], BF16)
                    w1e = bsb("_w1e", [128, 2 * DC * H], BF16)
                    w3e = bsb("_w3e", [128, 2 * DC * H], BF16)
                    w2e = bsb("_w2e", [128, 2 * HC * D], BF16)
                    EH = DC * H
                    b_w2 = s_w.n
                    b_wf = s_wf.n
                    b_rt_all = s_rt.n

                    @blk.sync
                    def _(sy):
                        sy.wait_ge(s_rt.h, b_rt_all)
                        sy.wait_ge(s_mmO.h, shared_mmO)
                        sy.wait_ge(s_evO.h, s_evO.n)
                        sy.dma_start(out=tok_sb[:],
                                     in_=tok_dram.rearrange("(q s) o -> q (s o)", q=128)
                                     [:, 0:E * GCH]).then_inc(*s_tok.inc(16))
                    b_tok = s_tok.n

                    load_tgt = {}

                    def load_expert(g, e):
                        sl1 = slice((e % 2) * EH, (e % 2 + 1) * EH)
                        sl2 = slice((e % 2) * HC * D, (e % 2 + 1) * HC * D)
                        g.dma_start(out=w1e[:, sl1].rearrange("p (c h) -> p c h", c=DC),
                                    in_=W1_in[e].rearrange("(c p) h -> p c h", p=128)
                                    ).then_inc(*s_w.inc(16))
                        g.dma_start(out=w3e[:, sl1].rearrange("p (c h) -> p c h", c=DC),
                                    in_=W3_in[e].rearrange("(c p) h -> p c h", p=128)
                                    ).then_inc(*s_w.inc(16))
                        g.dma_start(out=w2e[:, sl2].rearrange("p (c h) -> p c h", c=HC),
                                    in_=W2_in[e].rearrange("(c p) h -> p c h", p=128)
                                    ).then_inc(*s_w.inc(16))
                        load_tgt[e] = s_w.n

                    @blk.gpsimd
                    def _(g):
                        g.wait_ge(s_mmO.h, shared_mmO)
                        load_expert(g, 0)
                        if E > 1:
                            load_expert(g, 1)

                    first = True
                    sweep = dict(mmT=s_mmT.n, evT=s_evT.n, evH=s_evH.n,
                                 evO=s_evO.n, po=s_po.n)
                    for e in range(E):
                        if first:
                            @blk.tensor
                            def _(te):
                                for sm, val in sweep_marks:
                                    te.wait_ge(sm.h, val)
                            first = False
                        if 1 <= e < E - 1:
                            def prefetch(g, e=e):
                                g.wait_ge(s_wf.h, b_wf + e)
                                load_expert(g, e + 1)
                        else:
                            prefetch = None
                        sl1 = slice((e % 2) * EH, (e % 2 + 1) * EH)
                        sl2 = slice((e % 2) * HC * D, (e % 2 + 1) * HC * D)
                        emit_expert_routed(e, w1e[:, sl1], w3e[:, sl1],
                                           w2e[:, sl2], tok_sb, xg, xeT, po_sb,
                                           hg, hgl, load_tgt[e], s_wf, b_tok,
                                           sweep, prefetch=prefetch)
                    if DBG and tag == "f1":
                        @blk.sync
                        def _(sy):
                            sy.wait_ge(s_wf.h, b_wf + E)
                            sy.dma_start(out=dbg_toksb, in_=tok_sb[:]
                                         ).then_inc(*s_xo.inc(16))
                            sy.dma_start(out=dbg_xg, in_=xg[:]
                                         ).then_inc(*s_xo.inc(16))
                            sy.dma_start(out=dbg_xe, in_=xeT[:]
                                         ).then_inc(*s_xo.inc(16))
                @blk.sync
                def _(sy):
                    sy.wait_ge(s_evO.h, s_evO.n)
                    sy.dma_start(out=y_dram.rearrange("(t p) d -> p t d", p=128),
                                 in_=y_sb[:].rearrange("p (t d) -> p t d", d=D)
                                 ).then_inc(*s_xo.inc(16))
                barrier()

        def emit_attn():
            for sq_ in range(B_loc):
                with ExitStack() as ph:
                    psb = lambda n, sh, d: ph.enter_context(
                        nc.sbuf_tensor(f"at{sq_}{n}", sh, d))
                    bv_sb = psb("_bv", [128, D], F32)

                    @blk.sync
                    def _(sy):
                        sy.dma_start(out=bv_sb[:], in_=bv_in).then_inc(*s_boot.inc(16))
                    qT = psb("_qT", [128, DC * S], BF16)
                    kT = psb("_kT", [128, DC * S], BF16)
                    v_sb = psb("_v", [128, ST * D], BF16)
                    st1 = ph.enter_context(ExitStack())
                    s1b = lambda n, sh, d: st1.enter_context(
                        nc.sbuf_tensor(f"at{sq_}{n}", sh, d))
                    xlnT = s1b("_xlnT", [128, DC * S], BF16)
                    with ExitStack() as st2:
                        s2b = lambda n, sh, d: st2.enter_context(
                            nc.sbuf_tensor(f"at{sq_}{n}", sh, d))
                        emit_ln_pass(f"at{sq_}_ln", x_in, sq_ * S, ST, sq_ * ST,
                                     y_dram, True, xlnT, S, alloc=s2b,
                                     combine=True)
                        barrier()
                    wpj = s1b("_wpj", [128, DC * D], BF16)
                    b_w = s_w.n
                    b_mmP, b_evP = s_mmP.n, s_evP.n
                    per = DC * SB2

                    @blk.gpsimd
                    def _(g):
                        for j, wt in enumerate((WqT_in, WkT_in, WvT_in)):
                            if j >= 1:
                                g.wait_ge(s_evP.h, b_evP + j * per)
                            g.dma_start(
                                out=wpj[:].rearrange("p (c h) -> p c h", c=DC),
                                in_=wt.rearrange("(c p) h -> p c h", p=128)
                                ).then_inc(*s_w.inc(16))

                    @blk.tensor
                    def _(te):
                        for j in range(3):
                            te.wait_ge(s_w.h, b_w + 16 * (j + 1))
                            wsl = 0
                            if j < 2:
                                for mc in range(DC):
                                    for n in range(SB2):
                                        idx = j * per + mc * SB2 + n
                                        bo = idx % 4
                                        if idx >= 4:
                                            te.wait_ge(s_evP.h, b_evP + idx - 3)
                                        ins = None
                                        for c in range(DC):
                                            ins = te.matmul(
                                                out=bank(bo, SBW),
                                                lhsT=wpj[:, wsl + c * D + mc * 128:
                                                         wsl + c * D + (mc + 1) * 128],
                                                rhs=xlnT[:, c * S + n * SBW:
                                                         c * S + (n + 1) * SBW],
                                                start=(c == 0), stop=(c == DC - 1))
                                        ins.then_inc(*s_mmP.inc())
                            else:
                                for tc in range(ST):
                                    for n in range(NB2):
                                        idx = 2 * per + tc * NB2 + n
                                        bo = idx % 4
                                        if idx >= 4:
                                            te.wait_ge(s_evP.h, b_evP + idx - 3)
                                        ins = None
                                        for c in range(DC):
                                            ins = te.matmul(
                                                out=bank(bo, OB),
                                                lhsT=xlnT[:, c * S + tc * 128:
                                                          c * S + (tc + 1) * 128],
                                                rhs=wpj[:, wsl + c * D + n * OB:
                                                         wsl + c * D + (n + 1) * OB],
                                                start=(c == 0), stop=(c == DC - 1))
                                        ins.then_inc(*s_mmP.inc())

                    @blk.vector
                    def _(v):
                        for j in range(3):
                            if j < 2:
                                bias = bq_sb if j == 0 else bk_sb
                                for mc in range(DC):
                                    for n in range(SB2):
                                        idx = j * per + mc * SB2 + n
                                        bo = idx % 4
                                        v.wait_ge(s_mmP.h, b_mmP + idx + 1)
                                        dst = (qT if j == 0 else kT)[
                                            :, mc * S + n * SBW:mc * S + (n + 1) * SBW]
                                        if j == 0:
                                            ins = v.tensor_scalar(
                                                dst, bank(bo, SBW), bias[:, mc:mc + 1],
                                                0.125, OP.add, OP.mult)
                                        else:
                                            ins = v.tensor_scalar(
                                                dst, bank(bo, SBW), bias[:, mc:mc + 1],
                                                None, OP.add)
                                        ins.then_inc(*s_evP.inc())
                            else:
                                for tc in range(ST):
                                    for n in range(NB2):
                                        idx = 2 * per + tc * NB2 + n
                                        bo = idx % 4
                                        v.wait_ge(s_mmP.h, b_mmP + idx + 1)
                                        v.tensor_add(
                                            v_sb[:, tc * D + n * OB:tc * D + (n + 1) * OB],
                                            bank(bo, OB), bv_sb[:, n * OB:(n + 1) * OB]
                                            ).then_inc(*s_evP.inc())
                    barrier()
                    st1.close()

                    ctx_seq = psb("_ctx", [128, ST * D], BF16)
                    relkT = psb("_relkT", [128, JT * IQ], BF16)
                    kd_sb = psb("_kd", [128, 2 * WK], BF16)
                    qdb = psb("_qdb", [128, 4 * WQ], BF16)
                    rq_sb = psb("_rq", [128, 8 * JB], BF16)
                    Pb = psb("_Pb", [128, S], BF16)
                    PT = psb("_PT", [128, 1024], BF16)
                    zp = psb("_zp", [128, 2 * NJB + 4], F32)

                    for h in range(NH):
                        hb = 64 * (h % 2)
                        hcol = (h // 2) * S
                        for iq in range(NIQ):
                            b_mmK, b_evK, b_shK = s_mmK.n, s_evK.n, s_shK.n

                            @blk.tensor
                            def _(te):
                                te.wait_ge(s_w.h, 32)
                                te.wait_ge(s_evS.h, s_evS.n)
                                for jt in range(JT):
                                    if jt >= 2:
                                        te.wait_ge(s_evK.h, b_evK + jt - 1)
                                    s0 = C0 + iq * IQ - jt * 128 - 127
                                    g0 = (jt % 2) * 2
                                    w1k = min(512, WK)
                                    ins = te.matmul(out=bank(g0, w1k),
                                              lhsT=kT[hb:hb + 64,
                                                      hcol + jt * 128:hcol + (jt + 1) * 128],
                                              rhs=deT[hb:hb + 64, s0:s0 + w1k],
                                              start=True, stop=True)
                                    if WK > 512:
                                        ins = te.matmul(out=bank(g0 + 1, WK - 512),
                                              lhsT=kT[hb:hb + 64,
                                                      hcol + jt * 128:hcol + (jt + 1) * 128],
                                              rhs=deT[hb:hb + 64, s0 + 512:s0 + WK],
                                              start=True, stop=True)
                                    ins.then_inc(*s_mmK.inc())

                            @blk.vector
                            def _(v):
                                for jt in range(JT):
                                    v.wait_ge(s_mmK.h, b_mmK + jt + 1)
                                    if jt >= 2:
                                        v.wait_ge(s_shK.h, b_shK + 16 * (jt - 1))
                                    g0 = (jt % 2) * 2
                                    v.tensor_scalar(
                                        kd_sb[:, (jt % 2) * WK:(jt % 2) * WK + WK],
                                        PS[:, g0 * 512:g0 * 512 + WK],
                                        0.125, None, OP.mult).then_inc(*s_evK.inc())

                            @blk.scalar
                            def _(sc):
                                for jt in range(JT):
                                    sc.wait_ge(s_evK.h, b_evK + jt + 1)
                                    diag = AP(tensor=kd_sb, offset=(jt % 2) * WK + 127,
                                              ap=[[2 * WK - 1, 128], [1, IQ]])
                                    sc.dma_start(out=relkT[:, jt * IQ:(jt + 1) * IQ],
                                                 in_=diag).then_inc(*s_shK.inc(16))

                            G4 = 4 if JT % 4 == 0 else JT
                            shK_end = s_shK.n

                            def emit_qd(it, itl):
                                g0 = s_mmS.n
                                evK_end = s_evK.n

                                @blk.tensor
                                def _(te):
                                    for jb in range(NJB):
                                        g = g0 + jb
                                        if itl == 0 and jb < 2:
                                            te.wait_ge(s_evK.h, evK_end)
                                        if g >= 2:
                                            te.wait_ge(s_evS.h, g - 1)
                                        gb = (jb % 2) * 2
                                        s0q = C0 + jb * JB - it * 128 - 127
                                        w1q = min(512, WQ)
                                        ins = te.matmul(out=bank(gb, w1q),
                                                  lhsT=qT[hb:hb + 64,
                                                          hcol + it * 128:hcol + (it + 1) * 128],
                                                  rhs=derT[hb:hb + 64, s0q:s0q + w1q],
                                                  start=True, stop=True)
                                        if WQ > 512:
                                            ins = te.matmul(out=bank(gb + 1, WQ - 512),
                                                  lhsT=qT[hb:hb + 64,
                                                          hcol + it * 128:hcol + (it + 1) * 128],
                                                  rhs=derT[hb:hb + 64, s0q + 512:s0q + WQ],
                                                  start=True, stop=True)
                                        ins.then_inc(*s_mmS.inc())

                                @blk.vector
                                def _(v):
                                    for jb in range(NJB):
                                        g = g0 + jb
                                        v.wait_ge(s_mmS.h, g + 1)
                                        if g >= 2:
                                            v.wait_ge(s_shS.h, 16 * (g - 1))
                                        gb = (jb % 2) * 2
                                        q4 = (it % 2) * 2 + (jb % 2)
                                        v.tensor_copy(
                                            out=qdb[:, q4 * WQ:q4 * WQ + WQ],
                                            in_=PS[:, gb * 512:gb * 512 + WQ]
                                            ).then_inc(*s_evS.inc())

                                @blk.gpsimd
                                def _(g_):
                                    for jb in range(NJB):
                                        g = g0 + jb
                                        g_.wait_ge(s_evS.h, g + 1)
                                        if g >= 2 * NJB:
                                            g_.wait_ge(s_mmU.h, g - 2 * NJB + 1)
                                        q4 = (it % 2) * 2 + (jb % 2)
                                        r8 = (it % 2) * 4 + jb
                                        diag = AP(tensor=qdb, offset=q4 * WQ + 127,
                                                  ap=[[4 * WQ - 1, 128], [1, JB]])
                                        g_.dma_start(
                                            out=rq_sb[:, r8 * JB:(r8 + 1) * JB],
                                            in_=diag).then_inc(*s_shS.inc(16))

                            def emit_su(it, itl):
                                gs0 = s_mmU.n
                                git = s_evZ.n
                                zb = (git % 2) * NJB
                                b_mmX = s_mmX.n
                                b_ptc = s_ptc.n
                                b_evX = s_evX.n

                                @blk.tensor
                                def _(te):
                                    if itl == 0:
                                        te.wait_ge(s_shK.h, shK_end)
                                    for jb in range(NJB):
                                        gs = gs0 + jb
                                        bs = 4 + jb % 2
                                        te.wait_ge(s_shS.h, 16 * (gs + 1))
                                        if gs >= 2:
                                            te.wait_ge(s_scS.h, gs - 1)
                                        r8 = (it % 2) * 4 + jb
                                        te.matmul(out=bank(bs, JB),
                                                  lhsT=qT[hb:hb + 64,
                                                          hcol + it * 128:hcol + (it + 1) * 128],
                                                  rhs=kT[hb:hb + 64,
                                                         hcol + jb * JB:hcol + (jb + 1) * JB],
                                                  start=True, stop=False)
                                        for j2 in range(JB // 128):
                                            jt = jb * (JB // 128) + j2
                                            te.matmul(out=bank(bs, 128, j2 * 128),
                                                      lhsT=relkT[:, jt * IQ + itl * 128:
                                                                 jt * IQ + (itl + 1) * 128],
                                                      rhs=ident[:],
                                                      start=False, stop=False)
                                        te.matmul(out=bank(bs, JB), lhsT=ident[:],
                                                  rhs=rq_sb[:, r8 * JB:(r8 + 1) * JB],
                                                  start=False, stop=True
                                                  ).then_inc(*s_mmU.inc())

                                @blk.scalar
                                def _(sc):
                                    if git >= 2:
                                        sc.wait_ge(s_evZ.h, git - 1)
                                    if git >= 1:
                                        sc.wait_ge(s_mmX.h, b_mmX)
                                    for jb in range(NJB):
                                        gs = gs0 + jb
                                        sc.wait_ge(s_mmU.h, gs + 1)
                                        bs = 4 + jb % 2
                                        sc.activation(out=Pb[:, jb * JB:(jb + 1) * JB],
                                                      in_=bank(bs, JB), func=AF.Exp,
                                                      bias=float(-EXPB), scale=1.0,
                                                      accum_out=zp[:, zb + jb:zb + jb + 1]
                                                      ).then_inc(*s_scS.inc())
                                    sc.drain().then_inc(*s_zpD.inc())

                                @blk.vector
                                def _(v):
                                    v.wait_ge(s_zpD.h, git + 1)
                                    v.reduce_sum(zp[:, 2 * NJB + (git % 2):2 * NJB + (git % 2) + 1],
                                                 zp[:, zb:zb + NJB], axis=AX)
                                    v.drain()
                                    v.reciprocal(zp[:, 2 * NJB + 2 + (git % 2):2 * NJB + 3 + (git % 2)],
                                                 zp[:, 2 * NJB + (git % 2):2 * NJB + (git % 2) + 1])
                                    v.drain().then_inc(*s_evZ.inc())

                                NG = JT // G4
                                b_mmC = s_mmC.n

                                def ctx_mm(te, grp):
                                    if grp == 0:
                                        te.wait_ge(s_evX.h, b_evX)
                                    ins = None
                                    for j2 in range(G4):
                                        jt = grp * G4 + j2
                                        ins = te.matmul(
                                            out=bank(7, 64),
                                            lhsT=PT[:, (grp % 2) * 512 + j2 * 128:
                                                    (grp % 2) * 512 + (j2 + 1) * 128],
                                            rhs=v_sb[:, jt * D + h * 64:jt * D + (h + 1) * 64],
                                            start=(jt == 0), stop=(jt == JT - 1))
                                    ins.then_inc(*s_mmC.inc())

                                @blk.tensor
                                def _(te):
                                    te.wait_ge(s_scS.h, gs0 + NJB)
                                    for grp in range(NG):
                                        if grp >= 1:
                                            te.wait_ge(s_ptc.h, b_ptc + grp)
                                        ins = None
                                        for j2 in range(G4):
                                            jt = grp * G4 + j2
                                            ins = te.matmul(out=bank(6, 128, j2 * 128),
                                                            lhsT=Pb[:, jt * 128:(jt + 1) * 128],
                                                            rhs=ident[:],
                                                            start=(j2 == 0), stop=(j2 == G4 - 1))
                                        ins.then_inc(*s_mmX.inc())
                                        if grp >= 1:
                                            ctx_mm(te, grp - 1)
                                    te.wait_ge(s_ptc.h, b_ptc + NG)
                                    ctx_mm(te, NG - 1)

                                @blk.vector
                                def _(v):
                                    for grp in range(NG):
                                        v.wait_ge(s_mmX.h, b_mmX + grp + 1)
                                        if grp >= 2:
                                            v.wait_ge(s_mmC.h, b_mmC + grp - 1)
                                        v.tensor_copy(
                                            out=PT[:, (grp % 2) * 512:
                                                   (grp % 2) * 512 + G4 * 128],
                                            in_=bank(6, G4 * 128)
                                            ).then_inc(*s_ptc.inc())

                                @blk.vector
                                def _(v):
                                    v.wait_ge(s_mmC.h, b_mmC + NG)
                                    v.wait_ge(s_evZ.h, git + 1)
                                    v.tensor_scalar(
                                        ctx_seq[:, it * D + h * 64:it * D + (h + 1) * 64],
                                        bank(7, 64),
                                        zp[:, 2 * NJB + 2 + (git % 2):2 * NJB + 3 + (git % 2)],
                                        None, OP.mult).then_inc(*s_evX.inc())

                            for itl in range(IQC):
                                emit_qd(iq * IQC + itl, itl)
                                emit_su(iq * IQC + itl, itl)

                    @blk.sync
                    def _(sy):
                        sy.wait_ge(s_evX.h, s_evX.n)
                        sy.dma_start(
                            out=ctx_dram[sq_ * S:(sq_ + 1) * S]
                            .rearrange("(t p) d -> p t d", p=128),
                            in_=ctx_seq[:].rearrange("p (t d) -> p t d", d=D)
                            ).then_inc(*s_xo.inc(16))
                    barrier()

        LVL = p.get("LVL", 3)
        emit_ffn("f1", x_in, None, False, False)
        if DBG:
            @blk.sync
            def _(sy):
                sy.dma_start(out=dbg_y.rearrange("p (t d) -> p t d", d=D),
                             in_=y_dram.rearrange("(t p) d -> p t d", p=128)
                             ).then_inc(*s_xo.inc(16))
                sy.dma_start(out=dbg_wtk, in_=wtk_sb[:]).then_inc(*s_xo.inc(16))
                sy.dma_start(out=dbg_dtk, in_=dtk_sb[:]).then_inc(*s_xo.inc(16))
                sy.dma_start(out=dbg_dtk2, in_=dtk2_sb[:]).then_inc(*s_xo.inc(16))
                sy.dma_start(out=dbg_tok,
                             in_=tok_dram.rearrange("(q s) o -> q (s o)", q=128)
                             ).then_inc(*s_xo.inc(16))
                sy.dma_start(out=dbg_po, in_=po_dram).then_inc(*s_xo.inc(16))

            barrier()
        if LVL >= 2:
            emit_attn()
        if DBG and LVL >= 2:
            @blk.sync
            def _(sy):
                sy.dma_start(out=dbg_ctx.rearrange("p (t d) -> p t d", d=D),
                             in_=ctx_dram.rearrange("(t p) d -> p t d", p=128)
                             ).then_inc(*s_xo.inc(16))
            barrier()
        if LVL >= 3:
            emit_ffn("f2", x_out, ctx_dram, True, False)
            barrier()
            emit_ln_pass("fin", x_out, 0, NT, 0, y_dram, True, None, 0,
                         ln=False, combine=True)

        @blk.sync
        def _(sy):
            sy.wait_ge(s_xo.h, s_xo.n)

    es.close()
    return nc


def _prep_inputs(inp, p, core):
    p = dict(p)
    p.setdefault("CAP", p["B_loc"] * p["S"])
    S, D = p["S"], p["D"]
    DC = D // 128
    B_loc = p["B_loc"]
    T = B_loc * S
    x = np.ascontiguousarray(
        inp["L0_attn_hidden_states"][core * B_loc:(core + 1) * B_loc]
    ).reshape(T, D)
    de = np.asarray(inp["L0_attn_dist_emb"])
    deT = np.ascontiguousarray(de.T)
    derT = np.ascontiguousarray(de[::-1].T)
    f32 = np.float32
    return {
        "x_in": np.asarray(x, f32),
        "WqT": np.ascontiguousarray(np.asarray(inp["L0_attn_Wq"], f32).T),
        "WkT": np.ascontiguousarray(np.asarray(inp["L0_attn_Wk"], f32).T),
        "WvT": np.ascontiguousarray(np.asarray(inp["L0_attn_Wv"], f32).T),
        "bqT": np.ascontiguousarray(np.asarray(inp["L0_attn_bq"], f32).reshape(DC, 128).T),
        "bkT": np.ascontiguousarray(np.asarray(inp["L0_attn_bk"], f32).reshape(DC, 128).T),
        "bv_rep": np.ascontiguousarray(
            np.broadcast_to(np.asarray(inp["L0_attn_bv"], f32), (128, D))),
        "deT_d": np.ascontiguousarray(np.concatenate([deT, deT], axis=0), f32),
        "derT_d": np.ascontiguousarray(np.concatenate([derT, derT], axis=0), f32),
        "Wg_in": np.asarray(inp["L0_ffn_Wg"], f32),
        "W1_in": np.asarray(inp["L0_ffn_W1"], f32),
        "W2_in": np.asarray(inp["L0_ffn_W2"], f32),
        "W3_in": np.asarray(inp["L0_ffn_W3"], f32),
        "Ws1_in": np.asarray(inp["L0_ffn_Ws1"], f32),
        "Ws2_in": np.asarray(inp["L0_ffn_Ws2"], f32),
        "Ws3_in": np.asarray(inp["L0_ffn_Ws3"], f32),
        "ident_in": np.eye(128, dtype=ml_dtypes.bfloat16),
        "identf": np.eye(128, dtype=np.float32),
        "iota_tok": np.ascontiguousarray(
            (np.arange(T // 128)[None, :] * 128 + np.arange(128)[:, None])
            .astype(np.uint32)),
        "tok_pad": np.full((128, (p["E"] * p["CAP"]) // 128 + 1), T,
                           dtype=np.uint32),
        "eoff": np.ascontiguousarray(
            (np.arange(32, dtype=np.float32) * p["CAP"] - 1.0).reshape(32, 1)),
    }


def kernel(**inputs):
    p = FULL
    nc = build(p)
    in_maps = [_prep_inputs(inputs, p, c) for c in range(N_CORES)]
    res = run_bass_kernel_spmd(nc, in_maps, core_ids=list(range(N_CORES)))
    out = np.concatenate(
        [r["x_out"].reshape(p["B_loc"], p["S"], p["D"]) for r in res.results], axis=0)
    return np.ascontiguousarray(out, dtype=np.float32)
